# version 02
# baseline (speedup 1.0000x reference)
import numpy as np
import ml_dtypes
import concourse.bass as bass
import concourse.mybir as mybir
from concourse.bass_utils import run_bass_kernel_spmd

F32 = mybir.dt.float32
BF16 = mybir.dt.bfloat16
I32 = mybir.dt.int32
AF = mybir.ActivationFunctionType
ALU = mybir.AluOpType
AX = mybir.AxisListType

BF = ml_dtypes.bfloat16

B, T, D = 2, 8192, 2048
NCORES = 8
TLOC = (B * T) // NCORES
NTT = TLOC // 128
NDC = D // 128
LN_EPS = 1e-5


def f32(x):
    return np.ascontiguousarray(np.asarray(x), dtype=np.float32)


def to_bf16(x):
    return np.ascontiguousarray(np.asarray(x).astype(BF))


def i32np(x):
    return np.ascontiguousarray(np.asarray(x), dtype=np.int32)



class Buf:
    __slots__ = ("name", "last_w", "readers")

    def __init__(self, name):
        self.name = name
        self.last_w = None
        self.readers = []

    def __repr__(self):
        return f"Buf({self.name})"


class Step:
    __slots__ = ("sid", "eng", "fn", "deps", "dma", "tag", "sig", "sigcount")

    def __init__(self, sid, eng, fn, deps, dma, tag):
        self.sid = sid
        self.eng = eng
        self.fn = fn
        self.deps = deps
        self.dma = dma
        self.tag = tag
        self.sig = False
        self.sigcount = 0


class Prog:
    ENGS = ("tensor", "vector", "scalar", "gpsimd", "sync")

    def __init__(self):
        self.steps = []
        self.bufs = []

    def buf(self, name):
        b = Buf(name)
        self.bufs.append(b)
        return b

    def step(self, eng, fn, reads=(), writes=(), dma=False, tag=""):
        sid = len(self.steps)
        deps = set()
        for r in reads:
            if r.last_w is not None:
                deps.add(r.last_w)
        for w in writes:
            if w.last_w is not None:
                deps.add(w.last_w)
            deps.update(w.readers)
        for r in reads:
            r.readers.append(sid)
        for w in writes:
            w.last_w = sid
            w.readers = []
        deps.discard(sid)
        self.steps.append(Step(sid, eng, fn, deps, dma, tag))
        return sid

    @staticmethod
    def _ekey(s):
        return s.eng + ":dma" if s.dma else s.eng

    DMA_POOL = 6

    def emit(self, nc, block):
        steps = self.steps
        for s in steps:
            for d in s.deps:
                steps[d].sig = True
        for s in steps:
            if s.dma:
                s.sig = True
        counts = {}
        dma_use = {}
        queue_rr = {}
        sigid = {}
        for s in steps:
            ek = self._ekey(s)
            if not s.sig:
                continue
            if s.dma:
                r = queue_rr.get(ek, 0)
                queue_rr[ek] = (r + 1) % self.DMA_POOL
                semkey = (ek, r)
                j = dma_use.get(semkey, 0)
                dma_use[semkey] = j + 1
                sigid[s.sid] = ("d", semkey, 16 * (j + 1), 16 * j)
            else:
                c = counts.get(ek, 0) + 1
                counts[ek] = c
                sigid[s.sid] = ("c", ek, c, None)
        chans = {}
        for ek in counts:
            chans[ek] = nc.alloc_semaphore(name=f"ch_{ek.replace(':', '_')}")
        for semkey in dma_use:
            ek, r = semkey
            chans[semkey] = nc.alloc_semaphore(name=f"dq_{ek.split(':')[0]}_{r}")
        know = {}
        snap = {}
        waits = {}
        for s in steps:
            ek = self._ekey(s)
            k = know.setdefault(ek, {})
            wl = []
            for d in sorted(s.deps):
                if d not in sigid:
                    dk = self._ekey(steps[d])
                    assert dk == ek, f"non-signaling cross-engine dep {d}->{s.sid}"
                    continue
                kind, key, val, _g = sigid[d]
                if k.get(key, -1) >= val:
                    continue
                wl.append((key, val))
                k[key] = val
                for fk, fv in snap[d].items():
                    if k.get(fk, -1) < fv:
                        k[fk] = fv
            waits[s.sid] = wl
            sn = dict(k)
            if s.sid in sigid:
                kind, key, val, _g = sigid[s.sid]
                if kind == "c":
                    sn[key] = val
            snap[s.sid] = sn
        by_eng = {}
        for s in steps:
            by_eng.setdefault(s.eng, []).append(s)
        self._chans = chans
        self._sigid = sigid
        for eng_name in self.ENGS:
            sl = by_eng.get(eng_name)
            if not sl:
                continue

            def make_body(sl):
                def body(eng):
                    for s in sl:
                        for (key, val) in waits[s.sid]:
                            eng.wait_ge(chans[key], val)
                        if s.sid in sigid and sigid[s.sid][0] == "d":
                            _, semkey, val, gate = sigid[s.sid]
                            if gate > 0:
                                eng.wait_ge(chans[semkey], gate)
                        inst = s.fn(eng)
                        if s.sig:
                            assert inst is not None, f"step {s.tag or s.sid} must return its last instruction"
                            kind, key, val, _g = sigid[s.sid]
                            inst.then_inc(chans[key], 16 if s.dma else 1)
                return body
            getattr(block, eng_name)(make_body(sl))
        fc = dict(counts)
        for semkey, j in dma_use.items():
            fc[semkey] = 16 * j
        self.final_counts = fc

    def finish(self, nc, block):
        chans = self._chans
        counts = self.final_counts

        def body(eng):
            for ek, c in counts.items():
                eng.wait_ge(chans[ek], c)
        block.sync(body)



def emit_ln_tile(p, x_ap, x_buf, xln_ap, xln_buf, scr, scr_buf, stats, st_buf):
    Dd = x_ap.shape[-1]
    p.step("scalar", lambda s: s.activation(out=scr, in_=x_ap, func=AF.Square, accum_out=stats[:, 1:2]),
           reads=[x_buf], writes=[scr_buf, st_buf], tag="ln_sq")
    p.step("vector", lambda v: v.reduce_sum(out=stats[:, 0:1], in_=x_ap, axis=AX.X),
           reads=[x_buf], writes=[st_buf], tag="ln_sum")
    p.step("vector", lambda v: v.tensor_scalar_mul(out=stats[:, 0:1], in0=stats[:, 0:1], scalar1=1.0 / Dd),
           reads=[st_buf], writes=[st_buf], tag="ln_m")
    p.step("vector", lambda v: v.scalar_tensor_tensor(out=stats[:, 4:5], in0=stats[:, 0:1], scalar=-1.0, in1=stats[:, 0:1], op0=ALU.mult, op1=ALU.mult),
           reads=[st_buf], writes=[st_buf], tag="ln_msq")
    p.step("vector", lambda v: v.scalar_tensor_tensor(out=stats[:, 2:3], in0=stats[:, 1:2], scalar=1.0 / Dd, in1=stats[:, 4:5], op0=ALU.mult, op1=ALU.add),
           reads=[st_buf], writes=[st_buf], tag="ln_var")
    p.step("vector", lambda v: v.tensor_scalar_add(out=stats[:, 2:3], in0=stats[:, 2:3], scalar1=LN_EPS),
           reads=[st_buf], writes=[st_buf], tag="ln_eps")
    p.step("scalar", lambda s: s.activation(out=stats[:, 3:4], in_=stats[:, 2:3], func=AF.Sqrt),
           reads=[st_buf], writes=[st_buf], tag="ln_sqrt")
    p.step("vector", lambda v: v.reciprocal(out=stats[:, 3:4], in_=stats[:, 3:4]),
           reads=[st_buf], writes=[st_buf], tag="ln_recip")
    p.step("vector", lambda v: v.tensor_scalar(out=xln_ap, in0=x_ap, scalar1=stats[:, 0:1], scalar2=stats[:, 3:4], op0=ALU.subtract, op1=ALU.mult),
           reads=[x_buf, st_buf], writes=[xln_buf], tag="ln_norm")


def emit_ln_phase(p, x_d, x_sb, x_bufs, xln_sb, xln_bufs, xf_sb, xf_buf, scr, scr_bufs, stats, st_bufs,
                  tiles, tb_off=0):
    NB = len(x_bufs)
    for i, gt in enumerate(tiles):
        bi = i % NB
        p.step("sync", lambda sy, gt=gt, bi=bi: sy.dma_start(out=x_sb[:, bi, :], in_=x_d[gt]),
               reads=[], writes=[x_bufs[bi]], dma=True, tag=f"ldx{gt}")
        emit_ln_tile(p, x_sb[:, bi, :], x_bufs[bi], xln_sb[:, bi, :], xln_bufs[bi],
                     scr[:, bi, :], scr_bufs[bi], stats[:, bi, :], st_bufs[bi])
        p.step("sync", lambda sy, i=i, bi=bi: sy.dma_start(
            out=xf_sb[:, :, (tb_off + i) * 128:(tb_off + i + 1) * 128],
            in_=xln_sb[:, bi, :].rearrange("t (c d) -> t c d", c=NDC), transpose=True),
            reads=[xln_bufs[bi]], writes=[xf_buf], dma=True, tag=f"trx{gt}")


def emit_bcast_row(p, ones_s, ones_buf, row_ap, row_buf, out_sb, out_buf, ps, ps_bufs, banks, width, vdtypecopy=True):
    nch = (width + 511) // 512
    def t_mm(t):
        last = None
        for j in range(nch):
            w = min(512, width - j * 512)
            last = t.matmul(out=ps[banks[j]][:, :w], lhsT=ones_s[:], rhs=row_ap[0:1, j * 512:j * 512 + w], start=True, stop=True)
        return last
    p.step("tensor", t_mm, reads=[ones_buf, row_buf], writes=[ps_bufs[b] for b in banks[:nch]], tag="bcast_mm")

    def v_cp(v):
        last = None
        for j in range(nch):
            w = min(512, width - j * 512)
            last = v.tensor_copy(out=out_sb[:, j * 512:j * 512 + w], in_=ps[banks[j]][:, :w])
        return last
    p.step("vector", v_cp, reads=[ps_bufs[b] for b in banks[:nch]], writes=[out_buf], tag="bcast_cp")



HID = 5632
NOF = HID // 128


def build_ffn_quant():
    nc = bass.Bass("TRN2", target_bir_lowering=False)
    x_d = nc.dram_tensor("x", [NTT, 128, D], F32, kind="ExternalInput").ap()
    wg_d = nc.dram_tensor("wg", [NOF, 128, NDC * 128], I32, kind="ExternalInput").ap()
    wu_d = nc.dram_tensor("wu", [NOF, 128, NDC * 128], I32, kind="ExternalInput").ap()
    wd_d = nc.dram_tensor("wd", [NOF, 128, D], I32, kind="ExternalInput").ap()
    sgu_d = nc.dram_tensor("sgu", [NOF, 128, 2], F32, kind="ExternalInput").ap()
    sd_d = nc.dram_tensor("sd", [1, D], F32, kind="ExternalInput").ap()
    ones_d = nc.dram_tensor("ones", [1, 128], F32, kind="ExternalInput").ap()
    y_d = nc.dram_tensor("y", [NTT, 128, D], F32, kind="ExternalOutput").ap()

    TB = 512
    NTB = TLOC // TB
    TTB = TB // 128

    from contextlib import ExitStack
    with ExitStack() as st_:
        e = st_.enter_context
        xt = e(nc.sbuf_tensor("xt", [128, 2, D], F32))
        xln = e(nc.sbuf_tensor("xln", [128, 2, D], BF16))
        xf = e(nc.sbuf_tensor("xf", [128, NDC, TB], BF16))
        scr = e(nc.sbuf_tensor("scr", [128, 2, D], BF16))
        stats = e(nc.sbuf_tensor("stats", [128, 2, 8], F32))
        w32 = e(nc.sbuf_tensor("w32", [128, 2, NDC * 128], F32))
        wb = e(nc.sbuf_tensor("wb", [128, 2, 2, NDC * 128], BF16))
        sgu = e(nc.sbuf_tensor("sgu_s", [128, 2, 2], F32))
        hbuf = e(nc.sbuf_tensor("hbuf", [128, NOF, TB], BF16))
        silu_t = e(nc.sbuf_tensor("silu_t", [128, 512], BF16))
        wd32 = e(nc.sbuf_tensor("wd32", [128, 2, 512], F32))
        wdb = e(nc.sbuf_tensor("wdb", [128, NOF, 512], BF16))
        sdb = e(nc.sbuf_tensor("sdb", [128, D], F32))
        ones_s = e(nc.sbuf_tensor("ones_s", [1, 128], F32))
        sdrow = e(nc.sbuf_tensor("sdrow", [1, D], F32))
        xres = e(nc.sbuf_tensor("xres", [128, 2, 512], F32))
        ytile = e(nc.sbuf_tensor("ytile", [128, 2, 512], F32))
        ps = [e(nc.psum_tensor(f"pb{i}", [128, 512], F32)) for i in range(8)]

        p = Prog()
        x_bufs = [p.buf(f"xt{i}") for i in range(2)]
        xln_bufs = [p.buf(f"xln{i}") for i in range(2)]
        xf_buf = p.buf("xf")
        scr_bufs = [p.buf(f"scr{i}") for i in range(2)]
        st_bufs = [p.buf(f"stats{i}") for i in range(2)]
        w32_bufs = [p.buf(f"w32_{g}") for g in range(2)]
        wb_bufs = [[p.buf(f"wb_{g}_{i}") for i in range(2)] for g in range(2)]
        sgu_bufs = [p.buf(f"sgu{i}") for i in range(2)]
        h_buf = p.buf("hbuf")
        silu_buf = p.buf("silu")
        wd32_bufs = [p.buf(f"wd32_{i}") for i in range(2)]
        wdb_buf = p.buf("wdb")
        sdb_buf = p.buf("sdb")
        ones_buf = p.buf("ones")
        sdrow_buf = p.buf("sdrow")
        xres_bufs = [p.buf(f"xres{i}") for i in range(2)]
        yt_bufs = [p.buf(f"yt{i}") for i in range(2)]
        ps_bufs = [p.buf(f"ps{i}") for i in range(8)]

        p.step("sync", lambda sy: sy.dma_start(out=ones_s[:], in_=ones_d), writes=[ones_buf], dma=True, tag="ld_ones")
        p.step("sync", lambda sy: sy.dma_start(out=sdrow[:], in_=sd_d), writes=[sdrow_buf], dma=True, tag="ld_sd")
        emit_bcast_row(p, ones_s[:], ones_buf, sdrow, sdrow_buf, sdb, sdb_buf, ps, ps_bufs, [0, 1, 2, 3], D)

        for tb in range(NTB):
            tiles = list(range(tb * TTB, (tb + 1) * TTB))
            emit_ln_phase(p, x_d, xt, x_bufs, xln, xln_bufs, xf, xf_buf, scr, scr_bufs, stats, st_bufs, tiles)

            for of in range(NOF):
                bi = of % 2
                for g in range(2):
                    w_d = wg_d if g == 0 else wu_d
                    p.step("gpsimd", lambda gp, w_d=w_d, g=g, of=of: gp.dma_start(
                        out=w32[:, g, :], in_=w_d[of]), reads=[], writes=[w32_bufs[g]], dma=True, tag=f"ldw{g}_{of}")
                    p.step("vector", lambda v, g=g, bi=bi: v.tensor_scalar_add(
                        out=wb[:, g, bi, :], in0=w32[:, g, :], scalar1=-128.0),
                        reads=[w32_bufs[g]], writes=[wb_bufs[g][bi]], tag=f"dq{g}_{of}")
                p.step("sync", lambda sy, of=of, bi=bi: sy.dma_start(out=sgu[:, bi, :], in_=sgu_d[of]),
                       reads=[], writes=[sgu_bufs[bi]], dma=True, tag=f"ldsgu{of}")
                bank_g = (of % 2) * 2
                bank_u = bank_g + 1

                def t_gu(t, bi=bi, bank_g=bank_g, bank_u=bank_u):
                    last = None
                    for c in range(NDC):
                        t.matmul(out=ps[bank_g][:], lhsT=wb[:, 0, bi, c * 128:(c + 1) * 128],
                                 rhs=xf[:, c, :], start=(c == 0), stop=(c == NDC - 1))
                    for c in range(NDC):
                        last = t.matmul(out=ps[bank_u][:], lhsT=wb[:, 1, bi, c * 128:(c + 1) * 128],
                                        rhs=xf[:, c, :], start=(c == 0), stop=(c == NDC - 1))
                    return last
                p.step("tensor", t_gu, reads=[wb_bufs[0][bi], wb_bufs[1][bi], xf_buf],
                       writes=[ps_bufs[bank_g], ps_bufs[bank_u]], tag=f"mm_gu{of}")

                p.step("scalar", lambda s, bi=bi, bank_g=bank_g: s.activation(
                    out=silu_t[:], in_=ps[bank_g][:], func=AF.Silu, scale=sgu[:, bi, 0:1]),
                    reads=[ps_bufs[bank_g], sgu_bufs[bi]], writes=[silu_buf], tag=f"silu{of}")

                p.step("vector", lambda v, of=of, bi=bi, bank_u=bank_u: v.scalar_tensor_tensor(
                    out=hbuf[:, of, :], in0=ps[bank_u][:], scalar=sgu[:, bi, 1:2],
                    in1=silu_t[:], op0=ALU.mult, op1=ALU.mult),
                    reads=[ps_bufs[bank_u], silu_buf, sgu_bufs[bi]], writes=[h_buf], tag=f"hmul{of}")

            for dc in range(4):
                for of in range(NOF):
                    bi = of % 2
                    p.step("gpsimd", lambda gp, of=of, dc=dc, bi=bi: gp.dma_start(
                        out=wd32[:, bi, :], in_=wd_d[of][:, dc * 512:(dc + 1) * 512]),
                        reads=[], writes=[wd32_bufs[bi]], dma=True, tag=f"ldwd{dc}_{of}")
                    p.step("vector", lambda v, of=of, bi=bi: v.tensor_scalar_add(
                        out=wdb[:, of, :], in0=wd32[:, bi, :], scalar1=-128.0),
                        reads=[wd32_bufs[bi]], writes=[wdb_buf], tag=f"dqd{dc}_{of}")
                for tt in range(TTB):
                    gt = tb * TTB + tt
                    bank = 4 + (tt % 4)
                    ri = tt % 2
                    p.step("sync", lambda sy, gt=gt, dc=dc, ri=ri: sy.dma_start(
                        out=xres[:, ri, :], in_=x_d[gt][:, dc * 512:(dc + 1) * 512]),
                        reads=[], writes=[xres_bufs[ri]], dma=True, tag=f"ldres{gt}_{dc}")

                    def t_dn(t, tt=tt, bank=bank):
                        last = None
                        for of in range(NOF):
                            last = t.matmul(out=ps[bank][:], lhsT=hbuf[:, of, tt * 128:(tt + 1) * 128],
                                            rhs=wdb[:, of, :], start=(of == 0), stop=(of == NOF - 1))
                        return last
                    p.step("tensor", t_dn, reads=[h_buf, wdb_buf], writes=[ps_bufs[bank]], tag=f"mm_dn{gt}_{dc}")

                    p.step("vector", lambda v, bank=bank, ri=ri, dc=dc: v.tensor_tensor(
                        out=ytile[:, ri, :], in0=ps[bank][:], in1=sdb[:, dc * 512:(dc + 1) * 512], op=ALU.mult),
                        reads=[ps_bufs[bank], sdb_buf], writes=[yt_bufs[ri]], tag=f"ep1_{gt}_{dc}")
                    p.step("vector", lambda v, ri=ri: v.tensor_tensor(
                        out=ytile[:, ri, :], in0=ytile[:, ri, :], in1=xres[:, ri, :], op=ALU.add),
                        reads=[xres_bufs[ri], yt_bufs[ri]], writes=[yt_bufs[ri]], tag=f"ep2_{gt}_{dc}")

                    p.step("sync", lambda sy, gt=gt, dc=dc, ri=ri: sy.dma_start(
                        out=y_d[gt][:, dc * 512:(dc + 1) * 512], in_=ytile[:, ri, :]),
                        reads=[yt_bufs[ri]], writes=[], dma=True, tag=f"sty{gt}_{dc}")

        with nc.Block() as block:
            p.emit(nc, block)
            p.finish(nc, block)
    return nc


def host_ffn_weights(wq_gate, s_gate, wq_up, s_up, wq_down, s_down):
    def lhsT_tiles(wq):
        A = np.ascontiguousarray(wq).T
        A = A.reshape(NDC, 128, NOF, 128)
        A = A.transpose(2, 1, 0, 3)
        return i32np(A.reshape(NOF, 128, NDC * 128))
    wg = lhsT_tiles(wq_gate)
    wu = lhsT_tiles(wq_up)
    Ad = np.ascontiguousarray(wq_down).T
    wd = i32np(Ad.reshape(NOF, 128, D))
    sgu = f32(np.stack([s_gate.reshape(NOF, 128), s_up.reshape(NOF, 128)], axis=-1))
    sd = f32(s_down.reshape(1, D))
    ones = f32(np.ones((1, 128)))
    return dict(wg=wg, wu=wu, wd=wd, sgu=sgu, sd=sd, ones=ones)


def run_ffn_quant(x_full, weights, nc_cache={}):
    if "ffn" not in nc_cache:
        nc_cache["ffn"] = build_ffn_quant()
    nc = nc_cache["ffn"]
    xr = np.ascontiguousarray(x_full.reshape(NCORES, NTT, 128, D))
    in_maps = []
    for c in range(NCORES):
        m = dict(weights)
        m["x"] = f32(xr[c])
        in_maps.append(m)
    res = run_bass_kernel_spmd(nc, in_maps, core_ids=list(range(NCORES)))
    y = np.stack([r["y"] for r in res.results]).reshape(B, T, D)
    return y


NE = 10
EH = 1408
NOFE = EH // 128


def build_moe():
    nc = bass.Bass("TRN2", target_bir_lowering=False)
    x_d = nc.dram_tensor("x", [NTT, 128, D], F32, kind="ExternalInput").ap()
    gw_d = nc.dram_tensor("gw", [128, NDC, 8], BF16, kind="ExternalInput").ap()
    wg_d = nc.dram_tensor("wg", [NE * NOFE, 128, NDC * 128], F32, kind="ExternalInput").ap()
    wu_d = nc.dram_tensor("wu", [NE * NOFE, 128, NDC * 128], F32, kind="ExternalInput").ap()
    wd_d = nc.dram_tensor("wd", [NE * NOFE, 128, D], F32, kind="ExternalInput").ap()
    y_d = nc.dram_tensor("y", [NTT, 128, D], F32, kind="ExternalOutput").ap()
    cdbg_d = nc.dram_tensor("cdbg", [4, 128, 4, NE], F32, kind="ExternalOutput").ap()

    TB = 512
    NTB = TLOC // TB
    TTB = TB // 128

    from contextlib import ExitStack
    with ExitStack() as st_:
        e_ = st_.enter_context
        xt = e_(nc.sbuf_tensor("xt", [128, 2, D], F32))
        xln = e_(nc.sbuf_tensor("xln", [128, 2, D], BF16))
        xf = e_(nc.sbuf_tensor("xf", [128, NDC, TB], BF16))
        scr = e_(nc.sbuf_tensor("scr", [128, 2, D], BF16))
        stats = e_(nc.sbuf_tensor("stats", [128, 2, 8], F32))
        gwt = e_(nc.sbuf_tensor("gwt", [128, NDC, 8], BF16))
        probs = e_(nc.sbuf_tensor("probs", [128, TTB, 8], F32))
        ptmp = e_(nc.sbuf_tensor("ptmp", [128, 8], F32))
        rstat = e_(nc.sbuf_tensor("rstat", [128, 4], F32))
        comb = e_(nc.sbuf_tensor("comb", [128, TTB, NE], F32))
        wgu = e_(nc.sbuf_tensor("wgu", [128, 2, 2, NDC * 128], BF16))
        hbuf = e_(nc.sbuf_tensor("hbuf", [128, NOFE, TB], BF16))
        silu_t = e_(nc.sbuf_tensor("silu_t", [128, TB], BF16))
        wde = e_(nc.sbuf_tensor("wde", [128, 2, NOFE, D], BF16))
        acc = e_(nc.sbuf_tensor("acc", [128, TTB, D], F32))
        xres = e_(nc.sbuf_tensor("xres", [128, 2, 512], F32))
        ytile = e_(nc.sbuf_tensor("ytile", [128, 2, 512], F32))
        ps = [e_(nc.psum_tensor(f"pb{i}", [128, 512], F32)) for i in range(8)]

        p = Prog()
        x_bufs = [p.buf(f"xt{i}") for i in range(2)]
        xln_bufs = [p.buf(f"xln{i}") for i in range(2)]
        xf_buf = p.buf("xf")
        scr_bufs = [p.buf(f"scr{i}") for i in range(2)]
        st_bufs = [p.buf(f"stats{i}") for i in range(2)]
        gwt_buf = p.buf("gwt")
        probs_buf = p.buf("probs")
        ptmp_buf = p.buf("ptmp")
        rstat_buf = p.buf("rstat")
        comb_buf = p.buf("comb")
        wgu_bufs = [[p.buf(f"wgu{g}_{i}") for i in range(2)] for g in range(2)]
        h_buf = p.buf("h")
        silu_buf = p.buf("silu")
        wde_bufs = [p.buf(f"wde{i}") for i in range(2)]
        acc_buf = p.buf("acc")
        xres_bufs = [p.buf(f"xres{i}") for i in range(2)]
        yt_bufs = [p.buf(f"yt{i}") for i in range(2)]
        ps_bufs = [p.buf(f"ps{i}") for i in range(8)]

        p.step("sync", lambda sy: sy.dma_start(out=gwt[:], in_=gw_d), writes=[gwt_buf], dma=True, tag="ldgw")

        for tb in range(NTB):
            tiles = list(range(tb * TTB, (tb + 1) * TTB))
            emit_ln_phase(p, x_d, xt, x_bufs, xln, xln_bufs, xf, xf_buf, scr, scr_bufs, stats, st_bufs, tiles)

            for tt in range(TTB):
                def t_rt(t, tt=tt):
                    last = None
                    for c in range(NDC):
                        last = t.matmul(out=ps[4][:, 0:8], lhsT=xf[:, c, tt * 128:(tt + 1) * 128],
                                        rhs=gwt[:, c, :], start=(c == 0), stop=(c == NDC - 1))
                    return last
                p.step("tensor", t_rt, reads=[xf_buf, gwt_buf], writes=[ps_bufs[4]], tag=f"rt_mm{tt}")
                p.step("vector", lambda v: v.reduce_max(out=rstat[:, 0:1], in_=ps[4][:, 0:8], axis=AX.X, negate=True),
                       reads=[ps_bufs[4]], writes=[rstat_buf], tag="rt_max")
                p.step("scalar", lambda s, tt=tt: s.activation(out=probs[:, tt, :], in_=ps[4][:, 0:8], func=AF.Exp,
                                                               bias=rstat[:, 0:1], accum_out=rstat[:, 1:2]),
                       reads=[ps_bufs[4], rstat_buf], writes=[probs_buf, rstat_buf], tag="rt_exp")
                p.step("vector", lambda v: v.reciprocal(out=rstat[:, 1:2], in_=rstat[:, 1:2]),
                       reads=[rstat_buf], writes=[rstat_buf], tag="rt_rec")
                p.step("vector", lambda v, tt=tt: v.tensor_scalar_mul(out=probs[:, tt, :], in0=probs[:, tt, :], scalar1=rstat[:, 1:2]),
                       reads=[rstat_buf, probs_buf], writes=[probs_buf], tag="rt_nrm")
                p.step("vector", lambda v, tt=tt: v.reduce_max(out=rstat[:, 2:3], in_=probs[:, tt, :], axis=AX.X),
                       reads=[probs_buf], writes=[rstat_buf], tag="rt_m1")
                p.step("vector", lambda v, tt=tt: v.tensor_single_scalar(out=ptmp[:], in_=probs[:, tt, :], scalar=rstat[:, 2:3], op=ALU.is_ge),
                       reads=[probs_buf, rstat_buf], writes=[ptmp_buf], tag="rt_msk1")
                p.step("vector", lambda v, tt=tt: v.scalar_tensor_tensor(out=ptmp[:], in0=ptmp[:], scalar=-1e9, in1=probs[:, tt, :], op0=ALU.mult, op1=ALU.add),
                       reads=[ptmp_buf, probs_buf], writes=[ptmp_buf], tag="rt_pen")
                p.step("vector", lambda v: v.reduce_max(out=rstat[:, 3:4], in_=ptmp[:], axis=AX.X),
                       reads=[ptmp_buf], writes=[rstat_buf], tag="rt_m2")
                p.step("vector", lambda v, tt=tt: v.tensor_single_scalar(out=ptmp[:], in_=probs[:, tt, :], scalar=rstat[:, 3:4], op=ALU.is_ge),
                       reads=[probs_buf, rstat_buf], writes=[ptmp_buf], tag="rt_msk2")
                p.step("vector", lambda v, tt=tt: v.tensor_tensor(out=comb[:, tt, 0:8], in0=probs[:, tt, :], in1=ptmp[:], op=ALU.mult),
                       reads=[probs_buf, ptmp_buf], writes=[comb_buf], tag="rt_comb")
            p.step("vector", lambda v: v.memset(comb[:, :, 8:10], 1.0), reads=[], writes=[comb_buf], tag="rt_sh1")

            p.step("sync", lambda sy, tb=tb: sy.dma_start(out=cdbg_d[tb], in_=comb[:]),
                   reads=[comb_buf], writes=[], dma=True, tag=f"cdbg{tb}")

            p.step("vector", lambda v: v.memset(acc[:], 0.0), reads=[], writes=[acc_buf], tag="accz")

            for e in range(NE):
                p.step("gpsimd", lambda gp, e=e: gp.dma_start(
                    out=wde[:, e % 2, :, :], in_=wd_d[e * NOFE:(e + 1) * NOFE].rearrange("k p d -> p k d")),
                    reads=[], writes=[wde_bufs[e % 2]], dma=True, tag=f"ldwd{e}")
                for of in range(NOFE):
                    k = e * NOFE + of
                    bi = of % 2
                    for g in range(2):
                        w_d = wg_d if g == 0 else wu_d
                        p.step("gpsimd", lambda gp, w_d=w_d, g=g, k=k, bi=bi: gp.dma_start(
                            out=wgu[:, g, bi, :], in_=w_d[k]), reads=[], writes=[wgu_bufs[g][bi]], dma=True, tag=f"ldgu{k}")
                    bank_g = of % 2
                    bank_u = 2 + of % 2

                    def t_gu(t, bi=bi, bank_g=bank_g, bank_u=bank_u):
                        last = None
                        for c in range(NDC):
                            t.matmul(out=ps[bank_g][:], lhsT=wgu[:, 0, bi, c * 128:(c + 1) * 128],
                                     rhs=xf[:, c, :], start=(c == 0), stop=(c == NDC - 1))
                        for c in range(NDC):
                            last = t.matmul(out=ps[bank_u][:], lhsT=wgu[:, 1, bi, c * 128:(c + 1) * 128],
                                            rhs=xf[:, c, :], start=(c == 0), stop=(c == NDC - 1))
                        return last
                    p.step("tensor", t_gu, reads=[wgu_bufs[0][bi], wgu_bufs[1][bi], xf_buf],
                           writes=[ps_bufs[bank_g], ps_bufs[bank_u]], tag=f"mm_gu{k}")

                    p.step("scalar", lambda s, bank_g=bank_g: s.activation(
                        out=silu_t[:], in_=ps[bank_g][:], func=AF.Silu),
                        reads=[ps_bufs[bank_g]], writes=[silu_buf], tag=f"silu{k}")

                    p.step("vector", lambda v, of=of, bank_u=bank_u: v.tensor_tensor(
                        out=hbuf[:, of, :], in0=ps[bank_u][:], in1=silu_t[:], op=ALU.mult),
                        reads=[ps_bufs[bank_u], silu_buf], writes=[h_buf], tag=f"hmul{k}")

                for tt in range(TTB):
                    for dc in range(4):
                        bank = 4 + (tt * 4 + dc) % 4

                        def t_dn(t, e=e, tt=tt, dc=dc, bank=bank):
                            last = None
                            for of in range(NOFE):
                                last = t.matmul(out=ps[bank][:], lhsT=hbuf[:, of, tt * 128:(tt + 1) * 128],
                                                rhs=wde[:, e % 2, of, dc * 512:(dc + 1) * 512],
                                                start=(of == 0), stop=(of == NOFE - 1))
                            return last
                        p.step("tensor", t_dn, reads=[h_buf, wde_bufs[e % 2]], writes=[ps_bufs[bank]], tag=f"mm_dn{e}_{tt}_{dc}")

                        p.step("vector", lambda v, e=e, tt=tt, dc=dc, bank=bank: v.scalar_tensor_tensor(
                            out=acc[:, tt, dc * 512:(dc + 1) * 512], in0=ps[bank][:], scalar=comb[:, tt, e:e + 1],
                            in1=acc[:, tt, dc * 512:(dc + 1) * 512], op0=ALU.mult, op1=ALU.add),
                            reads=[ps_bufs[bank], comb_buf, acc_buf], writes=[acc_buf], tag=f"dnacc{e}_{tt}_{dc}")

            for tt in range(TTB):
                gt = tb * TTB + tt
                for dc in range(4):
                    ri = (tt * 4 + dc) % 2
                    p.step("sync", lambda sy, gt=gt, dc=dc, ri=ri: sy.dma_start(
                        out=xres[:, ri, :], in_=x_d[gt][:, dc * 512:(dc + 1) * 512]),
                        reads=[], writes=[xres_bufs[ri]], dma=True, tag=f"ldres{gt}_{dc}")
                    p.step("vector", lambda v, tt=tt, dc=dc, ri=ri: v.tensor_tensor(
                        out=ytile[:, ri, :], in0=acc[:, tt, dc * 512:(dc + 1) * 512], in1=xres[:, ri, :], op=ALU.add),
                        reads=[acc_buf, xres_bufs[ri]], writes=[yt_bufs[ri]], tag=f"yep{gt}_{dc}")
                    p.step("sync", lambda sy, gt=gt, dc=dc, ri=ri: sy.dma_start(
                        out=y_d[gt][:, dc * 512:(dc + 1) * 512], in_=ytile[:, ri, :]),
                        reads=[yt_bufs[ri]], writes=[], dma=True, tag=f"sty{gt}_{dc}")

        with nc.Block() as block:
            p.emit(nc, block)
            p.finish(nc, block)
    return nc


def host_moe_weights(gate_w, w_gate, w_up, w_down, sw_gate, sw_up, sw_down):
    gw = to_bf16(gate_w.reshape(NDC, 128, 8).transpose(1, 0, 2))

    def lhsT_tiles(w):
        A = np.asarray(w)
        H_ = A.shape[1]
        A = A.reshape(NDC, 128, H_ // 128, 128)
        return A.transpose(2, 1, 0, 3)

    def stack_gu(we, ws):
        parts = [lhsT_tiles(we[e]) for e in range(8)]
        parts.append(lhsT_tiles(ws[:, :EH]))
        parts.append(lhsT_tiles(ws[:, EH:]))
        Astk = np.concatenate(parts, axis=0)
        return f32(Astk.reshape(NE * NOFE, 128, NDC * 128))

    wg = stack_gu(w_gate, sw_gate)
    wu = stack_gu(w_up, sw_up)

    def down_tiles(wd):
        H_ = wd.shape[0]
        return np.asarray(wd).reshape(H_ // 128, 128, D)

    wdp = [down_tiles(w_down[e]) for e in range(8)]
    wdp.append(down_tiles(sw_down[:EH]))
    wdp.append(down_tiles(sw_down[EH:]))
    wd = f32(np.concatenate(wdp, axis=0))
    return dict(gw=gw, wg=wg, wu=wu, wd=wd)


def run_moe(x_full, weights, nc_cache={}):
    if "moe" not in nc_cache:
        nc_cache["moe"] = build_moe()
    nc = nc_cache["moe"]
    xr = np.ascontiguousarray(x_full.reshape(NCORES, NTT, 128, D))
    in_maps = []
    for c in range(NCORES):
        m = dict(weights)
        m["x"] = f32(xr[c])
        in_maps.append(m)
    res = run_bass_kernel_spmd(nc, in_maps, core_ids=list(range(NCORES)))
    return np.stack([r["y"] for r in res.results]).reshape(B, T, D)


QR = 768
NQC = QR // 128
KVR = 512
NKC = KVR // 128
HH = 16
QKD = 192
ROPE_D = 64
RMS_EPS = 1e-6


def _emit_rope(p, qv, q_buf, cs, cs_buf, rb, rb_buf, nh, tag, hw=32):
    co1 = cs[:, 0:1, 0:hw].broadcast_to([128, nh, hw])
    si1 = cs[:, 1:2, 0:hw].broadcast_to([128, nh, hw])
    A = qv[:, :, 0:hw]
    Bv = qv[:, :, hw:2 * hw]
    p.step("vector", lambda v: v.tensor_tensor(out=rb[:, :, 0:hw], in0=Bv, in1=si1, op=ALU.mult),
           reads=[q_buf, cs_buf], writes=[rb_buf], tag=f"{tag}_bs")
    p.step("vector", lambda v: v.tensor_tensor(out=rb[:, :, hw:2 * hw], in0=A, in1=si1, op=ALU.mult),
           reads=[q_buf, cs_buf], writes=[rb_buf], tag=f"{tag}_as")
    p.step("vector", lambda v: v.tensor_tensor(out=A, in0=A, in1=co1, op=ALU.mult),
           reads=[q_buf, cs_buf], writes=[q_buf], tag=f"{tag}_ac")
    p.step("vector", lambda v: v.tensor_tensor(out=A, in0=A, in1=rb[:, :, 0:hw], op=ALU.subtract),
           reads=[q_buf, rb_buf], writes=[q_buf], tag=f"{tag}_a2")
    p.step("vector", lambda v: v.tensor_tensor(out=Bv, in0=Bv, in1=co1, op=ALU.mult),
           reads=[q_buf, cs_buf], writes=[q_buf], tag=f"{tag}_bc")
    p.step("vector", lambda v: v.tensor_tensor(out=Bv, in0=Bv, in1=rb[:, :, hw:2 * hw], op=ALU.add),
           reads=[q_buf, rb_buf], writes=[q_buf], tag=f"{tag}_b2")


def build_mla_a():
    nc = bass.Bass("TRN2", target_bir_lowering=False)
    x_d = nc.dram_tensor("x", [NTT, 128, D], F32, kind="ExternalInput").ap()
    wqa_d = nc.dram_tensor("wqa", [NDC, 128, QR], F32, kind="ExternalInput").ap()
    wqb_d = nc.dram_tensor("wqb", [NQC, 128, HH * QKD], F32, kind="ExternalInput").ap()
    wkva_d = nc.dram_tensor("wkva", [NDC, 128, KVR + ROPE_D], F32, kind="ExternalInput").ap()
    wkvb_d = nc.dram_tensor("wkvb", [NKC, 128, HH * 256], F32, kind="ExternalInput").ap()
    gqa_d = nc.dram_tensor("gqa", [NQC, 128, 1], F32, kind="ExternalInput").ap()
    gkva_d = nc.dram_tensor("gkva", [NKC, 128, 1], F32, kind="ExternalInput").ap()
    cs_d = nc.dram_tensor("cs", [NTT, 128, 2, ROPE_D], F32, kind="ExternalInput").ap()
    q_o = nc.dram_tensor("q", [NTT, 128, HH * QKD], BF16, kind="ExternalOutput").ap()
    qn_o = nc.dram_tensor("qn", [NTT, 128, HH], F32, kind="ExternalOutput").ap()
    kn_o = nc.dram_tensor("kn", [NTT, 128, HH * 128], BF16, kind="ExternalOutput").ap()
    kr_o = nc.dram_tensor("kr", [NTT, 128, ROPE_D], BF16, kind="ExternalOutput").ap()
    v_o = nc.dram_tensor("v", [NTT, 128, HH * 128], BF16, kind="ExternalOutput").ap()

    from contextlib import ExitStack
    with ExitStack() as st_:
        e_ = st_.enter_context
        xt = e_(nc.sbuf_tensor("xt", [128, 2, D], F32))
        xln = e_(nc.sbuf_tensor("xln", [128, 2, D], BF16))
        xf = e_(nc.sbuf_tensor("xf", [128, NDC, 128], BF16))
        scr = e_(nc.sbuf_tensor("scr", [128, 2, D], BF16))
        stats = e_(nc.sbuf_tensor("stats", [128, 2, 8], F32))
        w32 = e_(nc.sbuf_tensor("w32", [128, 2048], F32))
        wqa = e_(nc.sbuf_tensor("wqa_s", [128, NDC, QR], BF16))
        wqb = e_(nc.sbuf_tensor("wqb_s", [128, NQC, HH * QKD], BF16))
        wkva = e_(nc.sbuf_tensor("wkva_s", [128, NDC, KVR + ROPE_D], BF16))
        wkvb = e_(nc.sbuf_tensor("wkvb_s", [128, NKC, HH * 256], BF16))
        gsc = e_(nc.sbuf_tensor("gsc", [128, NQC + NKC], F32))
        qcb = e_(nc.sbuf_tensor("qcb", [128, QR], BF16))
        qcf = e_(nc.sbuf_tensor("qcf", [128, NQC, 128], BF16))
        kvb = e_(nc.sbuf_tensor("kvb", [128, KVR], BF16))
        kvf = e_(nc.sbuf_tensor("kvf", [128, NKC, 128], BF16))
        cs = e_(nc.sbuf_tensor("cs_s", [128, 2, ROPE_D], F32))
        qsb = e_(nc.sbuf_tensor("qsb", [128, HH * QKD], F32))
        ksb = e_(nc.sbuf_tensor("ksb", [128, HH * 256], F32))
        krb = e_(nc.sbuf_tensor("krb", [128, ROPE_D], F32))
        rb = e_(nc.sbuf_tensor("rb", [128, HH, ROPE_D], F32))
        obf = e_(nc.sbuf_tensor("obf", [128, HH * 256 + 64], BF16))
        ps = [e_(nc.psum_tensor(f"pb{i}", [128, 512], F32)) for i in range(8)]

        p = Prog()
        x_bufs = [p.buf(f"xt{i}") for i in range(2)]
        xln_bufs = [p.buf(f"xln{i}") for i in range(2)]
        xf_buf = p.buf("xf")
        scr_bufs = [p.buf(f"scr{i}") for i in range(2)]
        st_bufs = [p.buf(f"stats{i}") for i in range(2)]
        w32_buf = p.buf("w32")
        wqa_buf, wqb_buf, wkva_buf, wkvb_buf = p.buf("wqa"), p.buf("wqb"), p.buf("wkva"), p.buf("wkvb")
        gsc_buf = p.buf("gsc")
        qcb_buf, qcf_buf = p.buf("qcb"), p.buf("qcf")
        kvb_buf, kvf_buf = p.buf("kvb"), p.buf("kvf")
        cs_buf = p.buf("cs")
        qsb_buf, ksb_buf, krb_buf, rb_buf = p.buf("qsb"), p.buf("ksb"), p.buf("krb"), p.buf("rb")
        obf_buf = p.buf("obf")
        ps_bufs = [p.buf(f"ps{i}") for i in range(8)]

        p.step("gpsimd", lambda g: g.dma_start(out=wqa[:], in_=wqa_d.rearrange("c p f -> p c f")),
               writes=[wqa_buf], dma=True, tag="ld_wqa")
        p.step("gpsimd", lambda g: g.dma_start(out=wkva[:], in_=wkva_d.rearrange("c p f -> p c f")),
               writes=[wkva_buf], dma=True, tag="ld_wkva")
        p.step("sync", lambda sy: sy.dma_start(out=gsc[:, 0:NQC], in_=gqa_d.rearrange("c p o -> p (c o)"), allow_slow_non_contiguous=True),
               writes=[gsc_buf], dma=True, tag="ld_gqa")
        p.step("sync", lambda sy: sy.dma_start(out=gsc[:, NQC:], in_=gkva_d.rearrange("c p o -> p (c o)"), allow_slow_non_contiguous=True),
               writes=[gsc_buf], dma=True, tag="ld_gkva")
        for c in range(NQC):
            for hlf in range(2):
                w = HH * QKD // 2
                p.step("sync", lambda sy, c=c, hlf=hlf, w=w: sy.dma_start(out=w32[:, 0:w], in_=wqb_d[c][:, hlf * w:(hlf + 1) * w]),
                       writes=[w32_buf], dma=True, tag=f"ld_wqb{c}_{hlf}")
                p.step("vector", lambda v, c=c, hlf=hlf, w=w: v.tensor_scalar_mul(out=wqb[:, c, hlf * w:(hlf + 1) * w], in0=w32[:, 0:w], scalar1=gsc[:, c:c + 1]),
                       reads=[w32_buf, gsc_buf], writes=[wqb_buf], tag=f"sc_wqb{c}_{hlf}")
        for c in range(NKC):
            for hlf in range(2):
                w = HH * 128
                p.step("sync", lambda sy, c=c, hlf=hlf, w=w: sy.dma_start(out=w32[:, 0:w], in_=wkvb_d[c][:, hlf * w:(hlf + 1) * w]),
                       writes=[w32_buf], dma=True, tag=f"ld_wkvb{c}_{hlf}")
                p.step("vector", lambda v, c=c, hlf=hlf, w=w: v.tensor_scalar_mul(out=wkvb[:, c, hlf * w:(hlf + 1) * w], in0=w32[:, 0:w], scalar1=gsc[:, NQC + c:NQC + c + 1]),
                       reads=[w32_buf, gsc_buf], writes=[wkvb_buf], tag=f"sc_wkvb{c}_{hlf}")

        for gt in range(NTT):
            bi = gt % 2
            p.step("sync", lambda sy, gt=gt, bi=bi: sy.dma_start(out=xt[:, bi, :], in_=x_d[gt]),
                   writes=[x_bufs[bi]], dma=True, tag=f"ldx{gt}")
            p.step("sync", lambda sy, gt=gt: sy.dma_start(out=cs[:], in_=cs_d[gt]),
                   writes=[cs_buf], dma=True, tag=f"ldcs{gt}")
            emit_ln_tile(p, xt[:, bi, :], x_bufs[bi], xln[:, bi, :], xln_bufs[bi],
                         scr[:, bi, :], scr_bufs[bi], stats[:, bi, :], st_bufs[bi])
            p.step("sync", lambda sy, bi=bi: sy.dma_start(
                out=xf[:], in_=xln[:, bi, :].rearrange("t (c d) -> t c d", c=NDC), transpose=True),
                reads=[xln_bufs[bi]], writes=[xf_buf], dma=True, tag=f"trx{gt}")

            def t_qa(t):
                last = None
                for c in range(NDC):
                    t.matmul(out=ps[0][:], lhsT=xf[:, c, :], rhs=wqa[:, c, 0:512], start=(c == 0), stop=(c == NDC - 1))
                for c in range(NDC):
                    last = t.matmul(out=ps[1][:, 0:256], lhsT=xf[:, c, :], rhs=wqa[:, c, 512:768], start=(c == 0), stop=(c == NDC - 1))
                return last
            p.step("tensor", t_qa, reads=[xf_buf, wqa_buf], writes=[ps_bufs[0], ps_bufs[1]], tag=f"mm_qa{gt}")
            p.step("scalar", lambda s: s.activation(out=scr[:, 0, 0:512], in_=ps[0][:], func=AF.Square, accum_out=stats[:, 0, 4:5]),
                   reads=[ps_bufs[0]], writes=[scr_bufs[0], st_bufs[0]], tag=f"q_sq0{gt}")
            p.step("scalar", lambda s: s.activation(out=scr[:, 0, 0:256], in_=ps[1][:, 0:256], func=AF.Square, accum_out=stats[:, 0, 5:6]),
                   reads=[ps_bufs[1]], writes=[scr_bufs[0], st_bufs[0]], tag=f"q_sq1{gt}")
            p.step("vector", lambda v: v.tensor_tensor(out=stats[:, 0, 4:5], in0=stats[:, 0, 4:5], in1=stats[:, 0, 5:6], op=ALU.add),
                   reads=[st_bufs[0]], writes=[st_bufs[0]], tag=f"q_ss{gt}")
            p.step("vector", lambda v: v.tensor_scalar(out=stats[:, 0, 4:5], in0=stats[:, 0, 4:5], scalar1=1.0 / QR, scalar2=RMS_EPS, op0=ALU.mult, op1=ALU.add),
                   reads=[st_bufs[0]], writes=[st_bufs[0]], tag=f"q_var{gt}")
            p.step("scalar", lambda s: s.activation(out=stats[:, 0, 6:7], in_=stats[:, 0, 4:5], func=AF.Sqrt),
                   reads=[st_bufs[0]], writes=[st_bufs[0]], tag=f"q_sqrt{gt}")
            p.step("vector", lambda v: v.reciprocal(out=stats[:, 0, 6:7], in_=stats[:, 0, 6:7]),
                   reads=[st_bufs[0]], writes=[st_bufs[0]], tag=f"q_rec{gt}")
            p.step("vector", lambda v: v.tensor_scalar_mul(out=qcb[:, 0:512], in0=ps[0][:], scalar1=stats[:, 0, 6:7]),
                   reads=[ps_bufs[0], st_bufs[0]], writes=[qcb_buf], tag=f"q_n0{gt}")
            p.step("vector", lambda v: v.tensor_scalar_mul(out=qcb[:, 512:768], in0=ps[1][:, 0:256], scalar1=stats[:, 0, 6:7]),
                   reads=[ps_bufs[1], st_bufs[0]], writes=[qcb_buf], tag=f"q_n1{gt}")
            p.step("sync", lambda sy: sy.dma_start(out=qcf[:], in_=qcb[:].rearrange("t (c d) -> t c d", c=NQC), transpose=True),
                   reads=[qcb_buf], writes=[qcf_buf], dma=True, tag=f"trqc{gt}")

            def t_kva(t):
                last = None
                for c in range(NDC):
                    t.matmul(out=ps[2][:], lhsT=xf[:, c, :], rhs=wkva[:, c, 0:512], start=(c == 0), stop=(c == NDC - 1))
                for c in range(NDC):
                    last = t.matmul(out=ps[3][:, 0:ROPE_D], lhsT=xf[:, c, :], rhs=wkva[:, c, 512:576], start=(c == 0), stop=(c == NDC - 1))
                return last
            p.step("tensor", t_kva, reads=[xf_buf, wkva_buf], writes=[ps_bufs[2], ps_bufs[3]], tag=f"mm_kva{gt}")
            p.step("scalar", lambda s: s.activation(out=scr[:, 1, 0:512], in_=ps[2][:], func=AF.Square, accum_out=stats[:, 1, 4:5]),
                   reads=[ps_bufs[2]], writes=[scr_bufs[1], st_bufs[1]], tag=f"kv_sq{gt}")
            p.step("vector", lambda v: v.tensor_scalar(out=stats[:, 1, 4:5], in0=stats[:, 1, 4:5], scalar1=1.0 / KVR, scalar2=RMS_EPS, op0=ALU.mult, op1=ALU.add),
                   reads=[st_bufs[1]], writes=[st_bufs[1]], tag=f"kv_var{gt}")
            p.step("scalar", lambda s: s.activation(out=stats[:, 1, 6:7], in_=stats[:, 1, 4:5], func=AF.Sqrt),
                   reads=[st_bufs[1]], writes=[st_bufs[1]], tag=f"kv_sqrt{gt}")
            p.step("vector", lambda v: v.reciprocal(out=stats[:, 1, 6:7], in_=stats[:, 1, 6:7]),
                   reads=[st_bufs[1]], writes=[st_bufs[1]], tag=f"kv_rec{gt}")
            p.step("vector", lambda v: v.tensor_scalar_mul(out=kvb[:], in0=ps[2][:], scalar1=stats[:, 1, 6:7]),
                   reads=[ps_bufs[2], st_bufs[1]], writes=[kvb_buf], tag=f"kv_n{gt}")
            p.step("sync", lambda sy: sy.dma_start(out=kvf[:], in_=kvb[:].rearrange("t (c d) -> t c d", c=NKC), transpose=True),
                   reads=[kvb_buf], writes=[kvf_buf], dma=True, tag=f"trkv{gt}")
            p.step("vector", lambda v: v.tensor_copy(out=krb[:], in_=ps[3][:, 0:ROPE_D]),
                   reads=[ps_bufs[3]], writes=[krb_buf], tag=f"kr_cp{gt}")
            _emit_rope(p, krb[:].rearrange("t (h d) -> t h d", h=1), krb_buf, cs, cs_buf, rb[:, 0:1, :], rb_buf, 1, f"krope{gt}")
            p.step("vector", lambda v: v.tensor_copy(out=obf[:, HH * 256:HH * 256 + ROPE_D], in_=krb[:]),
                   reads=[krb_buf], writes=[obf_buf], tag=f"kr_bf{gt}")
            p.step("sync", lambda sy, gt=gt: sy.dma_start(out=kr_o[gt], in_=obf[:, HH * 256:HH * 256 + ROPE_D]),
                   reads=[obf_buf], writes=[], dma=True, tag=f"st_kr{gt}")

            def t_qb(t):
                last = None
                for j in range(6):
                    for c in range(NQC):
                        last = t.matmul(out=ps[j][:], lhsT=qcf[:, c, :], rhs=wqb[:, c, j * 512:(j + 1) * 512], start=(c == 0), stop=(c == NQC - 1))
                return last
            p.step("tensor", t_qb, reads=[qcf_buf, wqb_buf], writes=[ps_bufs[0], ps_bufs[1], ps_bufs[2], ps_bufs[3], ps_bufs[4], ps_bufs[5]], tag=f"mm_qb{gt}")
            for j in range(6):
                p.step("vector", lambda v, j=j: v.tensor_copy(out=qsb[:, j * 512:(j + 1) * 512], in_=ps[j][:]),
                       reads=[ps_bufs[j]], writes=[qsb_buf], tag=f"q_ev{j}_{gt}")
            _emit_rope(p, qsb[:].rearrange("t (h d) -> t h d", h=HH)[:, :, 128:192], qsb_buf, cs, cs_buf, rb, rb_buf, HH, f"qrope{gt}")
            p.step("vector", lambda v: v.tensor_copy(out=obf[:, 0:HH * QKD], in_=qsb[:]),
                   reads=[qsb_buf], writes=[obf_buf], tag=f"q_bf{gt}")
            p.step("sync", lambda sy, gt=gt: sy.dma_start(out=q_o[gt], in_=obf[:, 0:HH * QKD]),
                   reads=[obf_buf], writes=[], dma=True, tag=f"st_q{gt}")
            p.step("scalar", lambda s: s.activation(out=ksb[:, 0:HH * QKD], in_=qsb[:], func=AF.Square),
                   reads=[qsb_buf], writes=[ksb_buf], tag=f"qn_sq{gt}")
            p.step("vector", lambda v: v.reduce_sum(out=ksb[:, 4000:4000 + HH], in_=ksb[:, 0:HH * QKD].rearrange("t (h d) -> t h d", h=HH), axis=AX.X),
                   reads=[ksb_buf], writes=[ksb_buf], tag=f"qn_rs{gt}")
            p.step("scalar", lambda s: s.activation(out=ksb[:, 4000 + HH:4000 + 2 * HH], in_=ksb[:, 4000:4000 + HH], func=AF.Sqrt),
                   reads=[ksb_buf], writes=[ksb_buf], tag=f"qn_rt{gt}")
            p.step("sync", lambda sy, gt=gt: sy.dma_start(out=qn_o[gt], in_=ksb[:, 4000 + HH:4000 + 2 * HH]),
                   reads=[ksb_buf], writes=[], dma=True, tag=f"st_qn{gt}")

            def t_kb(t):
                last = None
                for j in range(8):
                    for c in range(NKC):
                        last = t.matmul(out=ps[j][:], lhsT=kvf[:, c, :], rhs=wkvb[:, c, j * 512:(j + 1) * 512], start=(c == 0), stop=(c == NKC - 1))
                return last
            p.step("tensor", t_kb, reads=[kvf_buf, wkvb_buf], writes=ps_bufs, tag=f"mm_kb{gt}")
            for j in range(8):
                p.step("vector", lambda v, j=j: v.tensor_copy(out=ksb[:, j * 512:(j + 1) * 512], in_=ps[j][:]),
                       reads=[ps_bufs[j]], writes=[ksb_buf], tag=f"k_ev{j}_{gt}")
            kview = ksb[:].rearrange("t (h z) -> t h z", h=HH)
            p.step("vector", lambda v: v.tensor_copy(out=obf[:, 0:HH * 128].rearrange("t (h z) -> t h z", h=HH), in_=kview[:, :, 0:128]),
                   reads=[ksb_buf], writes=[obf_buf], tag=f"kn_bf{gt}")
            p.step("sync", lambda sy, gt=gt: sy.dma_start(out=kn_o[gt], in_=obf[:, 0:HH * 128]),
                   reads=[obf_buf], writes=[], dma=True, tag=f"st_kn{gt}")
            p.step("vector", lambda v: v.tensor_copy(out=obf[:, HH * 128:HH * 256].rearrange("t (h z) -> t h z", h=HH), in_=kview[:, :, 128:256]),
                   reads=[ksb_buf], writes=[obf_buf], tag=f"v_bf{gt}")
            p.step("sync", lambda sy, gt=gt: sy.dma_start(out=v_o[gt], in_=obf[:, HH * 128:HH * 256]),
                   reads=[obf_buf], writes=[], dma=True, tag=f"st_v{gt}")

        with nc.Block() as block:
            p.emit(nc, block)
            p.finish(nc, block)
    return nc


NP_MLA = 4
NQT = T // 128
MLA_SCALE = float(QKD) ** -0.5


def build_mla_b():
    nc = bass.Bass("TRN2", target_bir_lowering=False)
    qf1_d = nc.dram_tensor("qf1", [NP_MLA, 128, T], BF16, kind="ExternalInput").ap()
    qf2_d = nc.dram_tensor("qf2", [NP_MLA, 64, T], BF16, kind="ExternalInput").ap()
    kf1_d = nc.dram_tensor("kf1", [NP_MLA, 128, T], BF16, kind="ExternalInput").ap()
    kf2_d = nc.dram_tensor("kf2", [NP_MLA, 64, T], BF16, kind="ExternalInput").ap()
    vt_d = nc.dram_tensor("vt", [NP_MLA, NQT, 128, 128], BF16, kind="ExternalInput").ap()
    qn_d = nc.dram_tensor("qn", [NP_MLA, 128, NQT], F32, kind="ExternalInput").ap()
    mask_d = nc.dram_tensor("mask", [128, 128], F32, kind="ExternalInput").ap()
    onesb_d = nc.dram_tensor("onesb", [128, 1], BF16, kind="ExternalInput").ap()
    onesf_d = nc.dram_tensor("onesf", [1, 128], F32, kind="ExternalInput").ap()
    o_d = nc.dram_tensor("o", [NP_MLA, NQT, 128, 128], BF16, kind="ExternalOutput").ap()

    G = 4

    from contextlib import ExitStack
    with ExitStack() as st_:
        e_ = st_.enter_context
        qf1 = e_(nc.sbuf_tensor("qf1_s", [128, 2, T], BF16))
        qf2 = e_(nc.sbuf_tensor("qf2_s", [64, 2, T], BF16))
        kf1 = e_(nc.sbuf_tensor("kf1_s", [128, 2, T], BF16))
        kf2 = e_(nc.sbuf_tensor("kf2_s", [64, 2, T], BF16))
        vt = e_(nc.sbuf_tensor("vt_s", [128, 2, NQT, 128], BF16))
        qn = e_(nc.sbuf_tensor("qn_s", [128, 2, NQT], F32))
        maskt = e_(nc.sbuf_tensor("maskt", [128, 128], F32))
        onesb = e_(nc.sbuf_tensor("onesb_s", [128, 1], BF16))
        onesf = e_(nc.sbuf_tensor("onesf_s", [1, 128], F32))
        sqk = e_(nc.sbuf_tensor("sqk", [128, T], BF16))
        sqk2 = e_(nc.sbuf_tensor("sqk2", [64, T], BF16))
        kst = e_(nc.sbuf_tensor("kst", [1, 8], F32))
        kmx = e_(nc.sbuf_tensor("kmx", [128, 2], F32))
        bias_c = e_(nc.sbuf_tensor("bias_c", [128, G], F32))
        lsum = e_(nc.sbuf_tensor("lsum", [128, G, 2], F32))
        Pt = e_(nc.sbuf_tensor("Pt", [128, G, 512], BF16))
        PT = e_(nc.sbuf_tensor("PTt", [128, G, 4, 128], BF16))
        obf = e_(nc.sbuf_tensor("obf", [128, G, 128], BF16))
        ps = [e_(nc.psum_tensor(f"pb{i}", [128, 512], F32)) for i in range(8)]

        p = Prog()
        qf1_bufs = [p.buf(f"qf1_{i}") for i in range(2)]
        qf2_bufs = [p.buf(f"qf2_{i}") for i in range(2)]
        kf1_bufs = [p.buf(f"kf1_{i}") for i in range(2)]
        kf2_bufs = [p.buf(f"kf2_{i}") for i in range(2)]
        vt_bufs = [p.buf(f"vt{i}") for i in range(2)]
        qn_bufs = [p.buf(f"qn{i}") for i in range(2)]
        mask_buf = p.buf("mask")
        onesb_buf, onesf_buf = p.buf("onesb"), p.buf("onesf")
        sqk_buf, sqk2_buf, kst_buf, kmx_buf = p.buf("sqk"), p.buf("sqk2"), p.buf("kst"), p.buf("kmx")
        bias_bufs = [p.buf(f"bias{s}") for s in range(G)]
        l_bufs = [p.buf(f"l{s}") for s in range(G)]
        P_bufs = [p.buf(f"P{s}") for s in range(G)]
        PT_bufs = [p.buf(f"PT{s}") for s in range(G)]
        obf_bufs = [p.buf(f"obf{s}") for s in range(G)]
        ps_bufs = [p.buf(f"ps{i}") for i in range(8)]

        p.step("sync", lambda sy: sy.dma_start(out=maskt[:], in_=mask_d), writes=[mask_buf], dma=True, tag="ldmask")
        p.step("sync", lambda sy: sy.dma_start(out=onesb[:], in_=onesb_d), writes=[onesb_buf], dma=True, tag="ldob")
        p.step("sync", lambda sy: sy.dma_start(out=onesf[:], in_=onesf_d), writes=[onesf_buf], dma=True, tag="ldof")

        for pr in range(NP_MLA):
            pb = pr % 2
            p.step("sync", lambda sy, pr=pr, pb=pb: sy.dma_start(out=qf1[:, pb, :], in_=qf1_d[pr]),
                   writes=[qf1_bufs[pb]], dma=True, tag=f"ldq1_{pr}")
            p.step("sync", lambda sy, pr=pr, pb=pb: sy.dma_start(out=qf2[:, pb, :], in_=qf2_d[pr]),
                   writes=[qf2_bufs[pb]], dma=True, tag=f"ldq2_{pr}")
            p.step("sync", lambda sy, pr=pr, pb=pb: sy.dma_start(out=kf1[:, pb, :], in_=kf1_d[pr]),
                   writes=[kf1_bufs[pb]], dma=True, tag=f"ldk1_{pr}")
            p.step("sync", lambda sy, pr=pr, pb=pb: sy.dma_start(out=kf2[:, pb, :], in_=kf2_d[pr]),
                   writes=[kf2_bufs[pb]], dma=True, tag=f"ldk2_{pr}")
            p.step("gpsimd", lambda g, pr=pr, pb=pb: g.dma_start(out=vt[:, pb, :, :], in_=vt_d[pr].rearrange("k p z -> p k z")),
                   writes=[vt_bufs[pb]], dma=True, tag=f"ldv_{pr}")
            p.step("gpsimd", lambda g, pr=pr, pb=pb: g.dma_start(out=qn[:, pb, :], in_=qn_d[pr]),
                   writes=[qn_bufs[pb]], dma=True, tag=f"ldqn_{pr}")

            p.step("scalar", lambda s, pb=pb: s.activation(out=sqk[:], in_=kf1[:, pb, :], func=AF.Square),
                   reads=[kf1_bufs[pb]], writes=[sqk_buf], tag=f"kn1_{pr}")
            p.step("scalar", lambda s, pb=pb: s.activation(out=sqk2[:], in_=kf2[:, pb, :], func=AF.Square),
                   reads=[kf2_bufs[pb]], writes=[sqk2_buf], tag=f"kn2_{pr}")
            p.step("vector", lambda v: v.memset(kst[:, 0:1], 0.0), reads=[], writes=[kst_buf], tag=f"knz_{pr}")
            for j in range(16):
                def t_kn(t, j=j):
                    t.matmul(out=ps[6][0:1, 0:512], lhsT=onesb[:], rhs=sqk[:, j * 512:(j + 1) * 512], start=True, stop=False)
                    return t.matmul(out=ps[6][0:1, 0:512], lhsT=onesb[0:64, :], rhs=sqk2[:, j * 512:(j + 1) * 512], start=False, stop=True)
                p.step("tensor", t_kn, reads=[sqk_buf, sqk2_buf, onesb_buf], writes=[ps_bufs[6]], tag=f"knm{pr}_{j}")
                p.step("vector", lambda v: v.reduce_max(out=kst[:, 2:3], in_=ps[6][0:1, 0:512], axis=AX.X),
                       reads=[ps_bufs[6]], writes=[kst_buf], tag=f"knr{pr}_{j}")
                p.step("vector", lambda v: v.tensor_tensor(out=kst[:, 0:1], in0=kst[:, 0:1], in1=kst[:, 2:3], op=ALU.max),
                       reads=[kst_buf], writes=[kst_buf], tag=f"knx{pr}_{j}")
            p.step("scalar", lambda s: s.activation(out=kst[:, 1:2], in_=kst[:, 0:1], func=AF.Sqrt),
                   reads=[kst_buf], writes=[kst_buf], tag=f"knsq{pr}")
            p.step("vector", lambda v: v.tensor_scalar_mul(out=kst[:, 1:2], in0=kst[:, 1:2], scalar1=-MLA_SCALE),
                   reads=[kst_buf], writes=[kst_buf], tag=f"knsc{pr}")
            p.step("tensor", lambda t: t.matmul(out=ps[6][:, 0:1], lhsT=onesf[:], rhs=kst[0:1, 1:2], start=True, stop=True),
                   reads=[kst_buf, onesf_buf], writes=[ps_bufs[6]], tag=f"knbc{pr}")
            p.step("vector", lambda v: v.tensor_copy(out=kmx[:, 0:1], in_=ps[6][:, 0:1]),
                   reads=[ps_bufs[6]], writes=[kmx_buf], tag=f"knbc2{pr}")

            for qg in range(NQT // G):
                qts = [qg * G + s for s in range(G)]
                for s, qt in enumerate(qts):
                    p.step("vector", lambda v, s=s, qt=qt, pb=pb: v.tensor_tensor(
                        out=bias_c[:, s:s + 1], in0=qn[:, pb, qt:qt + 1], in1=kmx[:, 0:1], op=ALU.mult),
                        reads=[qn_bufs[pb], kmx_buf], writes=[bias_bufs[s]], tag=f"bc{pr}_{qt}")
                    p.step("vector", lambda v, s=s: v.memset(lsum[:, s, 0:1], 0.0),
                           reads=[], writes=[l_bufs[s]], tag=f"lz{pr}_{qt}")
                nchs = [qt // 4 + 1 for qt in qts]
                for ci in range(max(nchs)):
                    for s, qt in enumerate(qts):
                        if ci >= nchs[s]:
                            continue
                        nfull = qt // 4
                        w = 512 if ci < nfull else (qt % 4 + 1) * 128
                        nkt = w // 128
                        bank_s = s
                        bank_o = 4 + s

                        def t_qk(t, pb=pb, qt=qt, ci=ci, w=w, bank_s=bank_s):
                            t.matmul(out=ps[bank_s][:, 0:w], lhsT=qf1[:, pb, qt * 128:(qt + 1) * 128],
                                     rhs=kf1[:, pb, ci * 512:ci * 512 + w], start=True, stop=False)
                            return t.matmul(out=ps[bank_s][:, 0:w], lhsT=qf2[:, pb, qt * 128:(qt + 1) * 128],
                                            rhs=kf2[:, pb, ci * 512:ci * 512 + w], start=False, stop=True)
                        p.step("tensor", t_qk, reads=[qf1_bufs[pb], qf2_bufs[pb], kf1_bufs[pb], kf2_bufs[pb]],
                               writes=[ps_bufs[bank_s]], tag=f"qk{pr}_{qt}_{ci}")
                        if ci == nfull:
                            p.step("vector", lambda v, bank_s=bank_s, w=w: v.tensor_tensor(
                                out=ps[bank_s][:, w - 128:w], in0=ps[bank_s][:, w - 128:w], in1=maskt[:], op=ALU.add),
                                reads=[mask_buf, ps_bufs[bank_s]], writes=[ps_bufs[bank_s]], tag=f"msk{pr}_{qt}")
                        p.step("scalar", lambda sc, bank_s=bank_s, w=w, s=s: sc.activation(
                            out=Pt[:, s, 0:w], in_=ps[bank_s][:, 0:w], func=AF.Exp, bias=bias_c[:, s:s + 1], scale=MLA_SCALE,
                            accum_out=lsum[:, s, 1:2]),
                            reads=[ps_bufs[bank_s], bias_bufs[s]], writes=[P_bufs[s], l_bufs[s]], tag=f"exp{pr}_{qt}_{ci}")
                        p.step("vector", lambda v, s=s: v.tensor_tensor(out=lsum[:, s, 0:1], in0=lsum[:, s, 0:1], in1=lsum[:, s, 1:2], op=ALU.add),
                               reads=[l_bufs[s]], writes=[l_bufs[s]], tag=f"la{pr}_{qt}_{ci}")
                        p.step("sync", lambda sy, s=s, w=w, nkt=nkt: sy.dma_start(
                            out=PT[:, s, 0:nkt, :], in_=Pt[:, s, 0:w].rearrange("q (j k) -> q j k", j=nkt), transpose=True),
                            reads=[P_bufs[s]], writes=[PT_bufs[s]], dma=True, tag=f"tp{pr}_{qt}_{ci}")

                        islast = (ci == nchs[s] - 1)

                        def t_pv(t, pb=pb, ci=ci, nkt=nkt, s=s, bank_o=bank_o, islast=islast):
                            last = None
                            for j in range(nkt):
                                last = t.matmul(out=ps[bank_o][:, 0:128], lhsT=PT[:, s, j, :],
                                                rhs=vt[:, pb, ci * 4 + j, :], start=(ci == 0 and j == 0),
                                                stop=(islast and j == nkt - 1))
                            return last
                        p.step("tensor", t_pv, reads=[PT_bufs[s], vt_bufs[pb]], writes=[ps_bufs[bank_o]], tag=f"pv{pr}_{qt}_{ci}")
                for s, qt in enumerate(qts):
                    bank_o = 4 + s
                    p.step("vector", lambda v, s=s: v.reciprocal(out=lsum[:, s, 1:2], in_=lsum[:, s, 0:1]),
                           reads=[l_bufs[s]], writes=[l_bufs[s]], tag=f"fi{pr}_{qt}")
                    p.step("vector", lambda v, s=s, bank_o=bank_o: v.tensor_scalar_mul(
                        out=obf[:, s, :], in0=ps[bank_o][:, 0:128], scalar1=lsum[:, s, 1:2]),
                        reads=[ps_bufs[bank_o], l_bufs[s]], writes=[obf_bufs[s]], tag=f"fo{pr}_{qt}")
                    p.step("sync", lambda sy, pr=pr, qt=qt, s=s: sy.dma_start(out=o_d[pr, qt], in_=obf[:, s, :]),
                           reads=[obf_bufs[s]], writes=[], dma=True, tag=f"sto{pr}_{qt}")

        with nc.Block() as block:
            p.emit(nc, block)
            p.finish(nc, block)
    return nc



def build_mla_c():
    nc = bass.Bass("TRN2", target_bir_lowering=False)
    x_d = nc.dram_tensor("x", [NTT, 128, D], F32, kind="ExternalInput").ap()
    of_d = nc.dram_tensor("of", [NDC, 128, TLOC], BF16, kind="ExternalInput").ap()
    wo_d = nc.dram_tensor("wo", [NDC, 128, D], F32, kind="ExternalInput").ap()
    y_d = nc.dram_tensor("y", [NTT, 128, D], F32, kind="ExternalOutput").ap()

    from contextlib import ExitStack
    with ExitStack() as st_:
        e_ = st_.enter_context
        of = e_(nc.sbuf_tensor("of_s", [128, NDC, TLOC], BF16))
        wo = e_(nc.sbuf_tensor("wo_s", [128, NDC, D], BF16))
        xres = e_(nc.sbuf_tensor("xres", [128, 2, 512], F32))
        ytile = e_(nc.sbuf_tensor("ytile", [128, 2, 512], F32))
        ps = [e_(nc.psum_tensor(f"pb{i}", [128, 512], F32)) for i in range(8)]

        p = Prog()
        of_buf = p.buf("of")
        wo_buf = p.buf("wo")
        xres_bufs = [p.buf(f"xr{i}") for i in range(2)]
        yt_bufs = [p.buf(f"yt{i}") for i in range(2)]
        ps_bufs = [p.buf(f"ps{i}") for i in range(8)]

        p.step("sync", lambda sy: sy.dma_start(out=of[:], in_=of_d.rearrange("c p t -> p c t")),
               writes=[of_buf], dma=True, tag="ld_of")
        p.step("gpsimd", lambda g: g.dma_start(out=wo[:], in_=wo_d.rearrange("c p d -> p c d")),
               writes=[wo_buf], dma=True, tag="ld_wo")

        for gt in range(NTT):
            for dc in range(4):
                bank = (gt * 4 + dc) % 8
                ri = (gt * 4 + dc) % 2
                p.step("sync", lambda sy, gt=gt, dc=dc, ri=ri: sy.dma_start(
                    out=xres[:, ri, :], in_=x_d[gt][:, dc * 512:(dc + 1) * 512]),
                    writes=[xres_bufs[ri]], dma=True, tag=f"ldres{gt}_{dc}")

                def t_wo(t, gt=gt, dc=dc, bank=bank):
                    last = None
                    for c in range(NDC):
                        last = t.matmul(out=ps[bank][:], lhsT=of[:, c, gt * 128:(gt + 1) * 128],
                                        rhs=wo[:, c, dc * 512:(dc + 1) * 512], start=(c == 0), stop=(c == NDC - 1))
                    return last
                p.step("tensor", t_wo, reads=[of_buf, wo_buf], writes=[ps_bufs[bank]], tag=f"mm{gt}_{dc}")
                p.step("vector", lambda v, bank=bank, ri=ri: v.tensor_tensor(
                    out=ytile[:, ri, :], in0=ps[bank][:], in1=xres[:, ri, :], op=ALU.add),
                    reads=[ps_bufs[bank], xres_bufs[ri]], writes=[yt_bufs[ri]], tag=f"ep{gt}_{dc}")
                p.step("sync", lambda sy, gt=gt, dc=dc, ri=ri: sy.dma_start(
                    out=y_d[gt][:, dc * 512:(dc + 1) * 512], in_=ytile[:, ri, :]),
                    reads=[yt_bufs[ri]], writes=[], dma=True, tag=f"sty{gt}_{dc}")

        with nc.Block() as block:
            p.emit(nc, block)
            p.finish(nc, block)
    return nc


def host_rope_tables():
    inv = 1.0 / (10000.0 ** (np.arange(0, ROPE_D, 2, dtype=np.float64) / ROPE_D))
    ang = np.arange(T, dtype=np.float64)[:, None] * inv[None, :]
    emb = np.concatenate([ang, ang], axis=-1)
    return np.cos(emb).astype(np.float32), np.sin(emb).astype(np.float32)


def host_mla_weights(Wqa, g_qa, Wqb, Wkva, g_kva, Wkvb, Wo):
    cos, sin = host_rope_tables()
    cs_all = []
    for c in range(NCORES):
        pos = (c % 4) * TLOC + np.arange(TLOC)
        cs = np.stack([cos[pos], sin[pos]], axis=1).reshape(NTT, 128, 2, ROPE_D)
        cs_all.append(f32(cs))
    wqa = f32(np.asarray(Wqa).reshape(NDC, 128, QR))
    wqb = f32(np.asarray(Wqb).reshape(NQC, 128, HH * QKD))
    wkva = f32(np.asarray(Wkva).reshape(NDC, 128, KVR + ROPE_D))
    wkvb = f32(np.asarray(Wkvb).reshape(NKC, 128, HH * 256))
    gqa = f32(np.asarray(g_qa).reshape(NQC, 128, 1))
    gkva = f32(np.asarray(g_kva).reshape(NKC, 128, 1))
    wo = f32(np.asarray(Wo).reshape(NDC, 128, D))
    mask = np.triu(np.ones((128, 128), np.float32), k=1) * -1e30
    return dict(wqa=wqa, wqb=wqb, wkva=wkva, wkvb=wkvb, gqa=gqa, gkva=gkva, wo=wo,
                mask=f32(mask), cs_all=cs_all, onesb=to_bf16(np.ones((128, 1))),
                onesf=f32(np.ones((1, 128))))


def run_mla(x_full, W, nc_cache={}):
    if "mla_a" not in nc_cache:
        nc_cache["mla_a"] = build_mla_a()
        nc_cache["mla_b"] = build_mla_b()
        nc_cache["mla_c"] = build_mla_c()
    xr = np.ascontiguousarray(x_full.reshape(NCORES, NTT, 128, D))
    in_maps = []
    for c in range(NCORES):
        m = dict(x=f32(xr[c]), wqa=W["wqa"], wqb=W["wqb"], wkva=W["wkva"], wkvb=W["wkvb"],
                 gqa=W["gqa"], gkva=W["gkva"], cs=W["cs_all"][c])
        in_maps.append(m)
    res = run_bass_kernel_spmd(nc_cache["mla_a"], in_maps, core_ids=list(range(NCORES)))
    q = np.stack([r["q"] for r in res.results]).reshape(B, T, HH, QKD)
    kn = np.stack([r["kn"] for r in res.results]).reshape(B, T, HH, 128)
    kr = np.stack([r["kr"] for r in res.results]).reshape(B, T, ROPE_D)
    v = np.stack([r["v"] for r in res.results]).reshape(B, T, HH, 128)
    qnorm = np.stack([r["qn"] for r in res.results]).reshape(B, T, HH)

    pairs = [(b, h) for b in range(B) for h in range(HH)]
    in_maps_b = []
    for c in range(NCORES):
        myp = pairs[c * 4:(c + 1) * 4]
        qf1 = np.stack([np.ascontiguousarray(q[b, :, h, 0:128].T) for (b, h) in myp])
        qf2 = np.stack([np.ascontiguousarray(q[b, :, h, 128:192].T) for (b, h) in myp])
        kf1 = np.stack([np.ascontiguousarray(kn[b, :, h, :].T) for (b, h) in myp])
        kf2 = np.stack([np.ascontiguousarray(kr[b, :, :].T) for (b, h) in myp])
        vt = np.stack([np.ascontiguousarray(v[b, :, h, :].reshape(NQT, 128, 128)) for (b, h) in myp])
        qnp = np.stack([np.ascontiguousarray(qnorm[b, :, h].reshape(NQT, 128).T) for (b, h) in myp])
        in_maps_b.append(dict(qf1=qf1, qf2=qf2, kf1=kf1, kf2=kf2, vt=vt, qn=f32(qnp),
                              mask=W["mask"], onesb=W["onesb"], onesf=W["onesf"]))
    res_b = run_bass_kernel_spmd(nc_cache["mla_b"], in_maps_b, core_ids=list(range(NCORES)))
    o = np.zeros((B, T, HH, 128), dtype=BF)
    for c in range(NCORES):
        for i, (b, h) in enumerate(pairs[c * 4:(c + 1) * 4]):
            o[b, :, h, :] = res_b.results[c]["o"][i].reshape(T, 128)

    of_full = o.reshape(B * T, HH * 128)
    in_maps_c = []
    for c in range(NCORES):
        sl = of_full[c * TLOC:(c + 1) * TLOC]
        off = np.ascontiguousarray(sl.T.reshape(NDC, 128, TLOC))
        in_maps_c.append(dict(x=f32(xr[c]), of=off, wo=W["wo"]))
    res_c = run_bass_kernel_spmd(nc_cache["mla_c"], in_maps_c, core_ids=list(range(NCORES)))
    return np.stack([r["y"] for r in res_c.results]).reshape(B, T, D)


SWA_D = 128
SWA_H = 16
SWA_KV = 4
SWA_EPS = 1e-6
SWA_SCALE = SWA_D ** -0.5
SWA_CAP = 50.0


def _emit_qknorm(p, qv, q_buf, nh, sq, sq_buf, nst, nst_buf, scb, scb_buf, tag):
    p.step("scalar", lambda s: s.activation(out=sq, in_=qv.rearrange("t h z -> t (h z)"), func=AF.Square),
           reads=[q_buf], writes=[sq_buf], tag=f"{tag}_sq")
    p.step("vector", lambda v: v.reduce_sum(out=nst[:, 0:nh], in_=sq.rearrange("t (h z) -> t h z", h=nh), axis=AX.X),
           reads=[sq_buf], writes=[nst_buf], tag=f"{tag}_ss")
    p.step("scalar", lambda s: s.activation(out=nst[:, 8:8 + nh], in_=nst[:, 0:nh], func=AF.Sqrt),
           reads=[nst_buf], writes=[nst_buf], tag=f"{tag}_rt")
    p.step("vector", lambda v: v.tensor_scalar_add(out=nst[:, 8:8 + nh], in0=nst[:, 8:8 + nh], scalar1=SWA_EPS),
           reads=[nst_buf], writes=[nst_buf], tag=f"{tag}_ep")
    p.step("vector", lambda v: v.reciprocal(out=nst[:, 8:8 + nh], in_=nst[:, 8:8 + nh]),
           reads=[nst_buf], writes=[nst_buf], tag=f"{tag}_rc")
    p.step("vector", lambda v: v.tensor_tensor(
        out=qv, in0=qv, in1=nst[:, 8:8 + nh].rearrange("t (h o) -> t h o", o=1).broadcast_to([128, nh, SWA_D]), op=ALU.mult),
        reads=[nst_buf, q_buf], writes=[q_buf], tag=f"{tag}_nm")
    p.step("vector", lambda v: v.tensor_tensor(
        out=qv, in0=qv, in1=scb.rearrange("t (o z) -> t o z", o=1).broadcast_to([128, nh, SWA_D]), op=ALU.mult),
        reads=[scb_buf, q_buf], writes=[q_buf], tag=f"{tag}_sc")


def build_swa_a():
    nc = bass.Bass("TRN2", target_bir_lowering=False)
    x_d = nc.dram_tensor("x", [NTT, 128, D], F32, kind="ExternalInput").ap()
    wq_d = nc.dram_tensor("wq", [NDC, 128, D], F32, kind="ExternalInput").ap()
    wk_d = nc.dram_tensor("wk", [NDC, 128, 512], F32, kind="ExternalInput").ap()
    wv_d = nc.dram_tensor("wv", [NDC, 128, 512], F32, kind="ExternalInput").ap()
    qs_d = nc.dram_tensor("qs", [1, 128], F32, kind="ExternalInput").ap()
    ks_d = nc.dram_tensor("ks", [1, 128], F32, kind="ExternalInput").ap()
    ones_d = nc.dram_tensor("ones", [1, 128], F32, kind="ExternalInput").ap()
    cs_d = nc.dram_tensor("cs", [NTT, 128, 2, SWA_D], F32, kind="ExternalInput").ap()
    q_o = nc.dram_tensor("q", [NTT, 128, SWA_H * SWA_D], BF16, kind="ExternalOutput").ap()
    k_o = nc.dram_tensor("k", [NTT, 128, SWA_KV * SWA_D], BF16, kind="ExternalOutput").ap()
    v_o = nc.dram_tensor("v", [NTT, 128, SWA_KV * SWA_D], BF16, kind="ExternalOutput").ap()

    from contextlib import ExitStack
    with ExitStack() as st_:
        e_ = st_.enter_context
        xt = e_(nc.sbuf_tensor("xt", [128, 2, D], F32))
        xln = e_(nc.sbuf_tensor("xln", [128, 2, D], BF16))
        xf = e_(nc.sbuf_tensor("xf", [128, NDC, 128], BF16))
        scr = e_(nc.sbuf_tensor("scr", [128, 2, D], BF16))
        stats = e_(nc.sbuf_tensor("stats", [128, 2, 8], F32))
        wq = e_(nc.sbuf_tensor("wq_s", [128, NDC, D], BF16))
        wk = e_(nc.sbuf_tensor("wk_s", [128, NDC, 512], BF16))
        wv = e_(nc.sbuf_tensor("wv_s", [128, NDC, 512], BF16))
        qsb = e_(nc.sbuf_tensor("qsb", [128, SWA_H * SWA_D], F32))
        kvb = e_(nc.sbuf_tensor("kvb", [128, 2, SWA_KV * SWA_D], F32))
        sq = e_(nc.sbuf_tensor("sq", [128, SWA_H * SWA_D], BF16))
        nst = e_(nc.sbuf_tensor("nst", [128, 24], F32))
        cs = e_(nc.sbuf_tensor("cs_s", [128, 2, SWA_D], F32))
        rb = e_(nc.sbuf_tensor("rb", [128, SWA_H, SWA_D], F32))
        qscb = e_(nc.sbuf_tensor("qscb", [128, 128], F32))
        kscb = e_(nc.sbuf_tensor("kscb", [128, 128], F32))
        srow = e_(nc.sbuf_tensor("srow", [1, 128], F32))
        ones_s = e_(nc.sbuf_tensor("ones_s", [1, 128], F32))
        obf = e_(nc.sbuf_tensor("obf", [128, SWA_H * SWA_D], BF16))
        ps = [e_(nc.psum_tensor(f"pb{i}", [128, 512], F32)) for i in range(8)]

        p = Prog()
        x_bufs = [p.buf(f"xt{i}") for i in range(2)]
        xln_bufs = [p.buf(f"xln{i}") for i in range(2)]
        xf_buf = p.buf("xf")
        scr_bufs = [p.buf(f"scr{i}") for i in range(2)]
        st_bufs = [p.buf(f"stats{i}") for i in range(2)]
        wq_buf, wk_buf, wv_buf = p.buf("wq"), p.buf("wk"), p.buf("wv")
        qsb_buf = p.buf("qsb")
        kvb_bufs = [p.buf(f"kvb{i}") for i in range(2)]
        sq_buf = p.buf("sq")
        nst_buf = p.buf("nst")
        cs_buf = p.buf("cs")
        rb_buf = p.buf("rb")
        qscb_buf, kscb_buf = p.buf("qscb"), p.buf("kscb")
        srow_buf, ones_buf = p.buf("srow"), p.buf("ones")
        obf_buf = p.buf("obf")
        ps_bufs = [p.buf(f"ps{i}") for i in range(8)]

        p.step("gpsimd", lambda g: g.dma_start(out=wq[:], in_=wq_d.rearrange("c p f -> p c f")), writes=[wq_buf], dma=True, tag="ldwq")
        p.step("gpsimd", lambda g: g.dma_start(out=wk[:], in_=wk_d.rearrange("c p f -> p c f")), writes=[wk_buf], dma=True, tag="ldwk")
        p.step("gpsimd", lambda g: g.dma_start(out=wv[:], in_=wv_d.rearrange("c p f -> p c f")), writes=[wv_buf], dma=True, tag="ldwv")
        p.step("sync", lambda sy: sy.dma_start(out=ones_s[:], in_=ones_d), writes=[ones_buf], dma=True, tag="ldon")
        for nm, d_ap, sb, sbuf_ in (("qs", qs_d, qscb, qscb_buf), ("ks", ks_d, kscb, kscb_buf)):
            p.step("sync", lambda sy, d_ap=d_ap: sy.dma_start(out=srow[:], in_=d_ap), writes=[srow_buf], dma=True, tag=f"ld{nm}")
            p.step("tensor", lambda t: t.matmul(out=ps[6][:, 0:128], lhsT=ones_s[:], rhs=srow[:], start=True, stop=True),
                   reads=[ones_buf, srow_buf], writes=[ps_bufs[6]], tag=f"bc{nm}")
            p.step("vector", lambda v, sb=sb: v.tensor_copy(out=sb[:], in_=ps[6][:, 0:128]),
                   reads=[ps_bufs[6]], writes=[sbuf_], tag=f"cp{nm}")

        for gt in range(NTT):
            bi = gt % 2
            p.step("sync", lambda sy, gt=gt, bi=bi: sy.dma_start(out=xt[:, bi, :], in_=x_d[gt]),
                   writes=[x_bufs[bi]], dma=True, tag=f"ldx{gt}")
            p.step("sync", lambda sy, gt=gt: sy.dma_start(out=cs[:], in_=cs_d[gt]), writes=[cs_buf], dma=True, tag=f"ldcs{gt}")
            emit_ln_tile(p, xt[:, bi, :], x_bufs[bi], xln[:, bi, :], xln_bufs[bi],
                         scr[:, bi, :], scr_bufs[bi], stats[:, bi, :], st_bufs[bi])
            p.step("sync", lambda sy, bi=bi: sy.dma_start(
                out=xf[:], in_=xln[:, bi, :].rearrange("t (c d) -> t c d", c=NDC), transpose=True),
                reads=[xln_bufs[bi]], writes=[xf_buf], dma=True, tag=f"trx{gt}")

            def t_q(t):
                last = None
                for j in range(4):
                    for c in range(NDC):
                        last = t.matmul(out=ps[j][:], lhsT=xf[:, c, :], rhs=wq[:, c, j * 512:(j + 1) * 512],
                                        start=(c == 0), stop=(c == NDC - 1))
                return last
            p.step("tensor", t_q, reads=[xf_buf, wq_buf], writes=[ps_bufs[0], ps_bufs[1], ps_bufs[2], ps_bufs[3]], tag=f"mmq{gt}")

            def t_kv(t):
                last = None
                for c in range(NDC):
                    t.matmul(out=ps[4][:], lhsT=xf[:, c, :], rhs=wk[:, c, :], start=(c == 0), stop=(c == NDC - 1))
                for c in range(NDC):
                    last = t.matmul(out=ps[5][:], lhsT=xf[:, c, :], rhs=wv[:, c, :], start=(c == 0), stop=(c == NDC - 1))
                return last
            p.step("tensor", t_kv, reads=[xf_buf, wk_buf, wv_buf], writes=[ps_bufs[4], ps_bufs[5]], tag=f"mmkv{gt}")

            for j in range(4):
                p.step("vector", lambda v, j=j: v.tensor_copy(out=qsb[:, j * 512:(j + 1) * 512], in_=ps[j][:]),
                       reads=[ps_bufs[j]], writes=[qsb_buf], tag=f"qe{j}_{gt}")
            p.step("vector", lambda v: v.tensor_copy(out=kvb[:, 0, :], in_=ps[4][:]),
                   reads=[ps_bufs[4]], writes=[kvb_bufs[0]], tag=f"ke{gt}")
            p.step("vector", lambda v: v.tensor_copy(out=kvb[:, 1, :], in_=ps[5][:]),
                   reads=[ps_bufs[5]], writes=[kvb_bufs[1]], tag=f"ve{gt}")

            qv = qsb[:].rearrange("t (h z) -> t h z", h=SWA_H)
            _emit_rope(p, qv, qsb_buf, cs, cs_buf, rb, rb_buf, SWA_H, f"qr{gt}", hw=64)
            _emit_qknorm(p, qv, qsb_buf, SWA_H, sq[:], sq_buf, nst, nst_buf, qscb, qscb_buf, f"qn{gt}")
            kv_ = kvb[:, 0, :].rearrange("t (h z) -> t h z", h=SWA_KV)
            _emit_rope(p, kv_, kvb_bufs[0], cs, cs_buf, rb[:, 0:SWA_KV, :], rb_buf, SWA_KV, f"kr{gt}", hw=64)
            _emit_qknorm(p, kv_, kvb_bufs[0], SWA_KV, sq[:, 0:SWA_KV * SWA_D], sq_buf, nst, nst_buf, kscb, kscb_buf, f"kn{gt}")

            p.step("vector", lambda v: v.tensor_copy(out=obf[:], in_=qsb[:]), reads=[qsb_buf], writes=[obf_buf], tag=f"qb{gt}")
            p.step("sync", lambda sy, gt=gt: sy.dma_start(out=q_o[gt], in_=obf[:]), reads=[obf_buf], writes=[], dma=True, tag=f"stq{gt}")
            p.step("vector", lambda v: v.tensor_copy(out=obf[:, 0:512], in_=kvb[:, 0, :]), reads=[kvb_bufs[0]], writes=[obf_buf], tag=f"kb{gt}")
            p.step("sync", lambda sy, gt=gt: sy.dma_start(out=k_o[gt], in_=obf[:, 0:512]), reads=[obf_buf], writes=[], dma=True, tag=f"stk{gt}")
            p.step("vector", lambda v: v.tensor_copy(out=obf[:, 512:1024], in_=kvb[:, 1, :]), reads=[kvb_bufs[1]], writes=[obf_buf], tag=f"vb{gt}")
            p.step("sync", lambda sy, gt=gt: sy.dma_start(out=v_o[gt], in_=obf[:, 512:1024]), reads=[obf_buf], writes=[], dma=True, tag=f"stv{gt}")

        with nc.Block() as block:
            p.emit(nc, block)
            p.finish(nc, block)
    return nc



def build_swa_b():
    nc = bass.Bass("TRN2", target_bir_lowering=False)
    qf_d = nc.dram_tensor("qf", [4, 128, T], BF16, kind="ExternalInput").ap()
    kf_d = nc.dram_tensor("kf", [4, 128, T], BF16, kind="ExternalInput").ap()
    vt_d = nc.dram_tensor("vt", [4, NQT, 128, 128], BF16, kind="ExternalInput").ap()
    ml_d = nc.dram_tensor("maskL", [128, 128], F32, kind="ExternalInput").ap()
    mu_d = nc.dram_tensor("maskU", [128, 128], F32, kind="ExternalInput").ap()
    o_d = nc.dram_tensor("o", [4, NQT, 128, 128], BF16, kind="ExternalOutput").ap()

    G = 4
    SWA_BIAS = -12.0

    from contextlib import ExitStack
    with ExitStack() as st_:
        e_ = st_.enter_context
        qf = e_(nc.sbuf_tensor("qf_s", [128, 2, T], BF16))
        kf = e_(nc.sbuf_tensor("kf_s", [128, 2, T], BF16))
        vt = e_(nc.sbuf_tensor("vt_s", [128, 2, NQT, 128], BF16))
        mlt = e_(nc.sbuf_tensor("mlt", [128, 128], F32))
        mut = e_(nc.sbuf_tensor("mut", [128, 128], F32))
        bcol = e_(nc.sbuf_tensor("bcol", [128, 1], F32))
        cap = e_(nc.sbuf_tensor("cap", [128, G, 512], F32))
        lsum = e_(nc.sbuf_tensor("lsum", [128, G, 2], F32))
        Pt = e_(nc.sbuf_tensor("Pt", [128, G, 512], BF16))
        PT = e_(nc.sbuf_tensor("PTt", [128, G, 4, 128], BF16))
        obf = e_(nc.sbuf_tensor("obf", [128, G, 128], BF16))
        ps = [e_(nc.psum_tensor(f"pb{i}", [128, 512], F32)) for i in range(8)]

        p = Prog()
        qf_bufs = [p.buf(f"qf{i}") for i in range(2)]
        kf_bufs = [p.buf(f"kf{i}") for i in range(2)]
        vt_bufs = [p.buf(f"vt{i}") for i in range(2)]
        ml_buf, mu_buf, b_buf = p.buf("ml"), p.buf("mu"), p.buf("bcol")
        cap_bufs = [p.buf(f"cap{s}") for s in range(G)]
        l_bufs = [p.buf(f"l{s}") for s in range(G)]
        P_bufs = [p.buf(f"P{s}") for s in range(G)]
        PT_bufs = [p.buf(f"PT{s}") for s in range(G)]
        obf_bufs = [p.buf(f"obf{s}") for s in range(G)]
        ps_bufs = [p.buf(f"ps{i}") for i in range(8)]

        p.step("sync", lambda sy: sy.dma_start(out=mlt[:], in_=ml_d), writes=[ml_buf], dma=True, tag="ldml")
        p.step("sync", lambda sy: sy.dma_start(out=mut[:], in_=mu_d), writes=[mu_buf], dma=True, tag="ldmu")
        p.step("vector", lambda v: v.memset(bcol[:], SWA_BIAS), writes=[b_buf], tag="bset")

        for pr in range(4):
            pb = pr % 2
            p.step("sync", lambda sy, pr=pr, pb=pb: sy.dma_start(out=qf[:, pb, :], in_=qf_d[pr]),
                   writes=[qf_bufs[pb]], dma=True, tag=f"ldq{pr}")
            p.step("sync", lambda sy, pr=pr, pb=pb: sy.dma_start(out=kf[:, pb, :], in_=kf_d[pr]),
                   writes=[kf_bufs[pb]], dma=True, tag=f"ldk{pr}")
            p.step("gpsimd", lambda g, pr=pr, pb=pb: g.dma_start(out=vt[:, pb, :, :], in_=vt_d[pr].rearrange("k p z -> p k z")),
                   writes=[vt_bufs[pb]], dma=True, tag=f"ldv{pr}")

            for qg in range(NQT // G):
                qts = [qg * G + s for s in range(G)]
                chs = []
                for qt in qts:
                    b0 = max(0, qt - 8)
                    b1 = min(NQT - 1, qt + 8)
                    cl = []
                    kt = b0
                    while kt <= b1:
                        nkt = min(4, b1 - kt + 1)
                        cl.append((kt, nkt))
                        kt += nkt
                    chs.append(cl)
                for s in range(G):
                    p.step("vector", lambda v, s=s: v.memset(lsum[:, s, 0:1], 0.0),
                           reads=[], writes=[l_bufs[s]], tag=f"lz{pr}_{qts[s]}")
                for ci in range(max(len(c) for c in chs)):
                    for s, qt in enumerate(qts):
                        if ci >= len(chs[s]):
                            continue
                        kt, nkt = chs[s][ci]
                        w = nkt * 128
                        bank_s = s
                        bank_o = 4 + s
                        islast = (ci == len(chs[s]) - 1)

                        p.step("tensor", lambda t, pb=pb, qt=qt, kt=kt, w=w, bank_s=bank_s: t.matmul(
                            out=ps[bank_s][:, 0:w], lhsT=qf[:, pb, qt * 128:(qt + 1) * 128],
                            rhs=kf[:, pb, kt * 128:kt * 128 + w], start=True, stop=True),
                            reads=[qf_bufs[pb], kf_bufs[pb]], writes=[ps_bufs[bank_s]], tag=f"qk{pr}_{qt}_{ci}")
                        p.step("scalar", lambda sc, bank_s=bank_s, w=w, s=s: sc.activation(
                            out=cap[:, s, 0:w], in_=ps[bank_s][:, 0:w], func=AF.Tanh, scale=SWA_SCALE / SWA_CAP),
                            reads=[ps_bufs[bank_s]], writes=[cap_bufs[s]], tag=f"cp{pr}_{qt}_{ci}")
                        if kt <= qt - 8 < kt + nkt:
                            off = (qt - 8 - kt) * 128
                            p.step("vector", lambda v, s=s, off=off: v.tensor_tensor(
                                out=cap[:, s, off:off + 128], in0=cap[:, s, off:off + 128], in1=mlt[:], op=ALU.add),
                                reads=[ml_buf, cap_bufs[s]], writes=[cap_bufs[s]], tag=f"mkl{pr}_{qt}")
                        if kt <= qt + 8 < kt + nkt:
                            off = (qt + 8 - kt) * 128
                            p.step("vector", lambda v, s=s, off=off: v.tensor_tensor(
                                out=cap[:, s, off:off + 128], in0=cap[:, s, off:off + 128], in1=mut[:], op=ALU.add),
                                reads=[mu_buf, cap_bufs[s]], writes=[cap_bufs[s]], tag=f"mku{pr}_{qt}")
                        p.step("scalar", lambda sc, s=s, w=w: sc.activation(
                            out=Pt[:, s, 0:w], in_=cap[:, s, 0:w], func=AF.Exp, bias=bcol[:, 0:1], scale=SWA_CAP,
                            accum_out=lsum[:, s, 1:2]),
                            reads=[cap_bufs[s], b_buf], writes=[P_bufs[s], l_bufs[s]], tag=f"ex{pr}_{qt}_{ci}")
                        p.step("vector", lambda v, s=s: v.tensor_tensor(out=lsum[:, s, 0:1], in0=lsum[:, s, 0:1], in1=lsum[:, s, 1:2], op=ALU.add),
                               reads=[l_bufs[s]], writes=[l_bufs[s]], tag=f"la{pr}_{qt}_{ci}")
                        p.step("sync", lambda sy, s=s, w=w, nkt=nkt: sy.dma_start(
                            out=PT[:, s, 0:nkt, :], in_=Pt[:, s, 0:w].rearrange("q (j k) -> q j k", j=nkt), transpose=True),
                            reads=[P_bufs[s]], writes=[PT_bufs[s]], dma=True, tag=f"tp{pr}_{qt}_{ci}")

                        def t_pv(t, pb=pb, kt=kt, nkt=nkt, s=s, bank_o=bank_o, ci=ci, islast=islast):
                            last = None
                            for j in range(nkt):
                                last = t.matmul(out=ps[bank_o][:, 0:128], lhsT=PT[:, s, j, :],
                                                rhs=vt[:, pb, kt + j, :], start=(ci == 0 and j == 0),
                                                stop=(islast and j == nkt - 1))
                            return last
                        p.step("tensor", t_pv, reads=[PT_bufs[s], vt_bufs[pb]], writes=[ps_bufs[bank_o]], tag=f"pv{pr}_{qt}_{ci}")
                for s, qt in enumerate(qts):
                    bank_o = 4 + s
                    p.step("vector", lambda v, s=s: v.reciprocal(out=lsum[:, s, 1:2], in_=lsum[:, s, 0:1]),
                           reads=[l_bufs[s]], writes=[l_bufs[s]], tag=f"fi{pr}_{qt}")
                    p.step("vector", lambda v, s=s, bank_o=bank_o: v.tensor_scalar_mul(
                        out=obf[:, s, :], in0=ps[bank_o][:, 0:128], scalar1=lsum[:, s, 1:2]),
                        reads=[ps_bufs[bank_o], l_bufs[s]], writes=[obf_bufs[s]], tag=f"fo{pr}_{qt}")
                    p.step("sync", lambda sy, pr=pr, qt=qt, s=s: sy.dma_start(out=o_d[pr, qt], in_=obf[:, s, :]),
                           reads=[obf_bufs[s]], writes=[], dma=True, tag=f"sto{pr}_{qt}")

        with nc.Block() as block:
            p.emit(nc, block)
            p.finish(nc, block)
    return nc


def host_swa_weights(wq, wk, wv, wo, query_scale, key_scale):
    cs_all = []
    inv = 1.0 / (10000.0 ** (np.arange(0, SWA_D // 2, dtype=np.float64) / (SWA_D // 2)))
    ang = np.arange(T, dtype=np.float64)[:, None] * inv[None, :]
    emb = np.concatenate([ang, ang], axis=-1)
    cos, sin = np.cos(emb).astype(np.float32), np.sin(emb).astype(np.float32)
    for c in range(NCORES):
        pos = (c % 4) * TLOC + np.arange(TLOC)
        cs = np.stack([cos[pos], sin[pos]], axis=1).reshape(NTT, 128, 2, SWA_D)
        cs_all.append(f32(cs))
    BIG = -3e4
    maskL = np.tril(np.ones((128, 128), np.float32), k=-1) * BIG
    maskU = np.triu(np.ones((128, 128), np.float32), k=1) * BIG
    return dict(wq=f32(np.asarray(wq).reshape(NDC, 128, D)),
                wk=f32(np.asarray(wk).reshape(NDC, 128, 512)),
                wv=f32(np.asarray(wv).reshape(NDC, 128, 512)),
                wo=f32(np.asarray(wo).reshape(NDC, 128, D)),
                qs=f32(query_scale.reshape(1, 128)), ks=f32(key_scale.reshape(1, 128)),
                ones=f32(np.ones((1, 128))), maskL=f32(maskL), maskU=f32(maskU), cs_all=cs_all)


def run_swa(x_full, W, nc_cache={}):
    if "swa_a" not in nc_cache:
        nc_cache["swa_a"] = build_swa_a()
        nc_cache["swa_b"] = build_swa_b()
        nc_cache["mla_c"] = nc_cache.get("mla_c") or build_mla_c()
    xr = np.ascontiguousarray(x_full.reshape(NCORES, NTT, 128, D))
    in_maps = [dict(x=f32(xr[c]), wq=W["wq"], wk=W["wk"], wv=W["wv"], qs=W["qs"], ks=W["ks"],
                    ones=W["ones"], cs=W["cs_all"][c]) for c in range(NCORES)]
    res = run_bass_kernel_spmd(nc_cache["swa_a"], in_maps, core_ids=list(range(NCORES)))
    q = np.stack([r["q"] for r in res.results]).reshape(B, T, SWA_H, SWA_D)
    k = np.stack([r["k"] for r in res.results]).reshape(B, T, SWA_KV, SWA_D)
    v = np.stack([r["v"] for r in res.results]).reshape(B, T, SWA_KV, SWA_D)

    pairs = [(b, h) for b in range(B) for h in range(SWA_H)]
    in_maps_b = []
    for c in range(NCORES):
        myp = pairs[c * 4:(c + 1) * 4]
        qfm = np.stack([np.ascontiguousarray(q[b, :, h, :].T) for (b, h) in myp])
        kfm = np.stack([np.ascontiguousarray(k[b, :, h // 4, :].T) for (b, h) in myp])
        vtm = np.stack([np.ascontiguousarray(v[b, :, h // 4, :].reshape(NQT, 128, 128)) for (b, h) in myp])
        in_maps_b.append(dict(qf=qfm, kf=kfm, vt=vtm, maskL=W["maskL"], maskU=W["maskU"]))
    res_b = run_bass_kernel_spmd(nc_cache["swa_b"], in_maps_b, core_ids=list(range(NCORES)))
    o = np.zeros((B, T, SWA_H, SWA_D), dtype=BF)
    for c in range(NCORES):
        for i, (b, h) in enumerate(pairs[c * 4:(c + 1) * 4]):
            o[b, :, h, :] = res_b.results[c]["o"][i].reshape(T, SWA_D)

    of_full = o.reshape(B * T, SWA_H * SWA_D)
    in_maps_c = []
    for c in range(NCORES):
        sl = of_full[c * TLOC:(c + 1) * TLOC]
        off = np.ascontiguousarray(sl.T.reshape(NDC, 128, TLOC))
        in_maps_c.append(dict(x=f32(xr[c]), of=off, wo=W["wo"]))
    res_c = run_bass_kernel_spmd(nc_cache["mla_c"], in_maps_c, core_ids=list(range(NCORES)))
    return np.stack([r["y"] for r in res_c.results]).reshape(B, T, D)


DIN = 4096
PROJ = 8480
NJP = 17


def build_mamba_a():
    nc = bass.Bass("TRN2", target_bir_lowering=False)
    x_d = nc.dram_tensor("x", [NTT, 128, D], F32, kind="ExternalInput").ap()
    win_d = nc.dram_tensor("win", [NJP, 128, NDC, 512], F32, kind="ExternalInput").ap()
    ns_d = nc.dram_tensor("ns", [1, D], F32, kind="ExternalInput").ap()
    ones_d = nc.dram_tensor("ones", [1, 128], F32, kind="ExternalInput").ap()
    gate_o = nc.dram_tensor("gate", [NTT, 128, DIN], BF16, kind="ExternalOutput").ap()
    ssm_o = nc.dram_tensor("ssm", [NTT, 128, DIN], BF16, kind="ExternalOutput").ap()
    dbc_o = nc.dram_tensor("dbc", [NTT, 128, 288], F32, kind="ExternalOutput").ap()

    from contextlib import ExitStack
    with ExitStack() as st_:
        e_ = st_.enter_context
        xt = e_(nc.sbuf_tensor("xt", [128, 2, D], F32))
        xln = e_(nc.sbuf_tensor("xln", [128, 2, D], F32))
        hrb = e_(nc.sbuf_tensor("hrb", [128, 2, D], BF16))
        xf = e_(nc.sbuf_tensor("xf", [128, NDC, TLOC], BF16))
        scr = e_(nc.sbuf_tensor("scr", [128, 2, D], BF16))
        stats = e_(nc.sbuf_tensor("stats", [128, 2, 8], F32))
        wj = e_(nc.sbuf_tensor("wj", [128, 2, NDC, 512], BF16))
        w32 = e_(nc.sbuf_tensor("w32", [128, NDC, 512], F32))
        nsb = e_(nc.sbuf_tensor("nsb", [128, D], F32))
        nsrow = e_(nc.sbuf_tensor("nsrow", [1, D], F32))
        ones_s = e_(nc.sbuf_tensor("ones_s", [1, 128], F32))
        ob = e_(nc.sbuf_tensor("ob", [128, 2, 512], BF16))
        ob32 = e_(nc.sbuf_tensor("ob32", [128, 2, 512], F32))
        ps = [e_(nc.psum_tensor(f"pb{i}", [128, 512], F32)) for i in range(8)]

        p = Prog()
        x_bufs = [p.buf(f"xt{i}") for i in range(2)]
        xln_bufs = [p.buf(f"xln{i}") for i in range(2)]
        hr_bufs = [p.buf(f"hr{i}") for i in range(2)]
        xf_buf = p.buf("xf")
        scr_bufs = [p.buf(f"scr{i}") for i in range(2)]
        st_bufs = [p.buf(f"st{i}") for i in range(2)]
        wj_bufs = [p.buf(f"wj{i}") for i in range(2)]
        w32_buf = p.buf("w32")
        nsb_buf, nsrow_buf, ones_buf = p.buf("nsb"), p.buf("nsrow"), p.buf("ones")
        ob_bufs = [p.buf(f"ob{i}") for i in range(2)]
        ob32_bufs = [p.buf(f"ob32{i}") for i in range(2)]
        ps_bufs = [p.buf(f"ps{i}") for i in range(8)]

        p.step("sync", lambda sy: sy.dma_start(out=ones_s[:], in_=ones_d), writes=[ones_buf], dma=True, tag="ldon")
        p.step("sync", lambda sy: sy.dma_start(out=nsrow[:], in_=ns_d), writes=[nsrow_buf], dma=True, tag="ldns")
        emit_bcast_row(p, ones_s[:], ones_buf, nsrow, nsrow_buf, nsb, nsb_buf, ps, ps_bufs, [0, 1, 2, 3], D)

        for gt in range(NTT):
            bi = gt % 2
            p.step("sync", lambda sy, gt=gt, bi=bi: sy.dma_start(out=xt[:, bi, :], in_=x_d[gt]),
                   writes=[x_bufs[bi]], dma=True, tag=f"ldx{gt}")
            emit_ln_tile(p, xt[:, bi, :], x_bufs[bi], xln[:, bi, :], xln_bufs[bi],
                         scr[:, bi, :], scr_bufs[bi], stats[:, bi, :], st_bufs[bi])
            p.step("scalar", lambda s, bi=bi: s.activation(out=scr[:, bi, :], in_=xln[:, bi, :], func=AF.Square, accum_out=stats[:, bi, 4:5]),
                   reads=[xln_bufs[bi]], writes=[scr_bufs[bi], st_bufs[bi]], tag=f"msq{gt}")
            p.step("vector", lambda v, bi=bi: v.tensor_scalar(out=stats[:, bi, 4:5], in0=stats[:, bi, 4:5], scalar1=1.0 / D, scalar2=RMS_EPS, op0=ALU.mult, op1=ALU.add),
                   reads=[st_bufs[bi]], writes=[st_bufs[bi]], tag=f"mv{gt}")
            p.step("scalar", lambda s, bi=bi: s.activation(out=stats[:, bi, 5:6], in_=stats[:, bi, 4:5], func=AF.Sqrt),
                   reads=[st_bufs[bi]], writes=[st_bufs[bi]], tag=f"msr{gt}")
            p.step("vector", lambda v, bi=bi: v.reciprocal(out=stats[:, bi, 5:6], in_=stats[:, bi, 5:6]),
                   reads=[st_bufs[bi]], writes=[st_bufs[bi]], tag=f"mrc{gt}")
            p.step("vector", lambda v, bi=bi: v.tensor_scalar_mul(out=xln[:, bi, :], in0=xln[:, bi, :], scalar1=stats[:, bi, 5:6]),
                   reads=[st_bufs[bi]], writes=[xln_bufs[bi]], tag=f"mh{gt}")
            p.step("vector", lambda v, bi=bi: v.tensor_tensor(out=hrb[:, bi, :], in0=xln[:, bi, :], in1=nsb[:], op=ALU.mult),
                   reads=[xln_bufs[bi], nsb_buf], writes=[hr_bufs[bi]], tag=f"mns{gt}")
            p.step("sync", lambda sy, gt=gt, bi=bi: sy.dma_start(
                out=xf[:, :, gt * 128:(gt + 1) * 128], in_=hrb[:, bi, :].rearrange("t (c d) -> t c d", c=NDC), transpose=True),
                reads=[hr_bufs[bi]], writes=[xf_buf], dma=True, tag=f"trx{gt}")

        for j in range(NJP):
            bj = j % 2
            p.step("gpsimd", lambda g, j=j, bj=bj: g.dma_start(out=w32[:], in_=win_d[j]),
                   reads=[], writes=[w32_buf], dma=True, tag=f"ldw{j}")
            p.step("vector", lambda v, bj=bj: v.tensor_copy(out=wj[:, bj, :, :], in_=w32[:]),
                   reads=[w32_buf], writes=[wj_bufs[bj]], tag=f"cw{j}")
            for gt in range(NTT):
                bank = (gt % 4) + (4 if j % 2 else 0)
                ri = gt % 2

                def t_pj(t, j=j, bj=bj, gt=gt, bank=bank):
                    last = None
                    for c in range(NDC):
                        last = t.matmul(out=ps[bank][:], lhsT=xf[:, c, gt * 128:(gt + 1) * 128],
                                        rhs=wj[:, bj, c, :], start=(c == 0), stop=(c == NDC - 1))
                    return last
                p.step("tensor", t_pj, reads=[xf_buf, wj_bufs[bj]], writes=[ps_bufs[bank]], tag=f"mm{j}_{gt}")
                if j < 8:
                    p.step("vector", lambda v, bank=bank, ri=ri: v.tensor_copy(out=ob[:, ri, :], in_=ps[bank][:]),
                           reads=[ps_bufs[bank]], writes=[ob_bufs[ri]], tag=f"ev{j}_{gt}")
                    p.step("sync", lambda sy, j=j, gt=gt, ri=ri: sy.dma_start(
                        out=gate_o[gt][:, j * 512:(j + 1) * 512], in_=ob[:, ri, :]),
                        reads=[ob_bufs[ri]], writes=[], dma=True, tag=f"sg{j}_{gt}")
                elif j < 16:
                    p.step("vector", lambda v, bank=bank, ri=ri: v.tensor_copy(out=ob[:, ri, :], in_=ps[bank][:]),
                           reads=[ps_bufs[bank]], writes=[ob_bufs[ri]], tag=f"ev{j}_{gt}")
                    p.step("sync", lambda sy, j=j, gt=gt, ri=ri: sy.dma_start(
                        out=ssm_o[gt][:, (j - 8) * 512:(j - 7) * 512], in_=ob[:, ri, :]),
                        reads=[ob_bufs[ri]], writes=[], dma=True, tag=f"ss{j}_{gt}")
                else:
                    p.step("vector", lambda v, bank=bank, ri=ri: v.tensor_copy(out=ob32[:, ri, 0:288], in_=ps[bank][:, 0:288]),
                           reads=[ps_bufs[bank]], writes=[ob32_bufs[ri]], tag=f"ev{j}_{gt}")
                    p.step("sync", lambda sy, gt=gt, ri=ri: sy.dma_start(
                        out=dbc_o[gt], in_=ob32[:, ri, 0:288]),
                        reads=[ob32_bufs[ri]], writes=[], dma=True, tag=f"sd{gt}")

        with nc.Block() as block:
            p.emit(nc, block)
            p.finish(nc, block)
    return nc



NCH = T // 128


def build_mamba_b():
    nc = bass.Bass("TRN2", target_bir_lowering=False)
    xh_d = nc.dram_tensor("xh", [8, 128, T], BF16, kind="ExternalInput").ap()
    cw_d = nc.dram_tensor("cw", [8, 128, 4], F32, kind="ExternalInput").ap()
    cb_d = nc.dram_tensor("cb", [8, 128, 1], F32, kind="ExternalInput").ap()
    dtr_d = nc.dram_tensor("dtr", [8, 128, NCH], F32, kind="ExternalInput").ap()
    alog_d = nc.dram_tensor("alog", [1, 8], F32, kind="ExternalInput").ap()
    dtb_d = nc.dram_tensor("dtb", [1, 8], F32, kind="ExternalInput").ap()
    dd_d = nc.dram_tensor("dd", [8, 1, 128], F32, kind="ExternalInput").ap()
    btm_d = nc.dram_tensor("btm", [2, NCH, 128, 128], BF16, kind="ExternalInput").ap()
    ctm_d = nc.dram_tensor("ctm", [2, NCH, 128, 128], BF16, kind="ExternalInput").ap()
    tri_d = nc.dram_tensor("tri", [128, 128], F32, kind="ExternalInput").ap()
    mask_d = nc.dram_tensor("maskc", [128, 128], F32, kind="ExternalInput").ap()
    sel_d = nc.dram_tensor("sel127", [128, 1], F32, kind="ExternalInput").ap()
    ones_d = nc.dram_tensor("ones", [1, 128], F32, kind="ExternalInput").ap()
    identf_d = nc.dram_tensor("identf", [128, 128], F32, kind="ExternalInput").ap()
    y_o = nc.dram_tensor("y", [8, NCH, 128, 128], BF16, kind="ExternalOutput").ap()

    from contextlib import ExitStack
    with ExitStack() as st_:
        e_ = st_.enter_context
        xh = e_(nc.sbuf_tensor("xh_s", [128, 2, 4 + T], BF16))
        btm = e_(nc.sbuf_tensor("btm_s", [128, 2, NCH, 128], BF16))
        ctm = e_(nc.sbuf_tensor("ctm_s", [128, 2, NCH, 128], BF16))
        acc = e_(nc.sbuf_tensor("acc_m", [128, T // 2], F32))
        sf = e_(nc.sbuf_tensor("sf_m", [128, T], BF16))
        sftm = e_(nc.sbuf_tensor("sftm", [128, 2, NCH, 128], BF16))
        cwv = e_(nc.sbuf_tensor("cw_s", [128, 2, 4], F32))
        cbv = e_(nc.sbuf_tensor("cb_s", [128, 2, 1], F32))
        dtr = e_(nc.sbuf_tensor("dtr_s", [128, 2, NCH], F32))
        dtv = e_(nc.sbuf_tensor("dtv", [128, 2, NCH], F32))
        tmpn = e_(nc.sbuf_tensor("tmpn", [128, NCH], F32))
        cal = e_(nc.sbuf_tensor("cal", [128, 2, NCH], F32))
        clb = e_(nc.sbuf_tensor("clb", [128, 2, NCH], F32))
        dte = e_(nc.sbuf_tensor("dte", [128, 2, NCH], F32))
        dfs = e_(nc.sbuf_tensor("dfs", [128, 2, NCH], F32))
        cdk = e_(nc.sbuf_tensor("cdk", [128, 2, NCH], F32))
        clr = e_(nc.sbuf_tensor("clr", [1, NCH], F32))
        ab = e_(nc.sbuf_tensor("ab", [128, 8], F32))
        dtbb = e_(nc.sbuf_tensor("dtbb", [128, 8], F32))
        ddb = e_(nc.sbuf_tensor("ddb", [128, 2, 128], F32))
        ddrow = e_(nc.sbuf_tensor("ddrow", [1, 8, 128], F32))
        arow = e_(nc.sbuf_tensor("arow", [1, 16], F32))
        tri = e_(nc.sbuf_tensor("tri_s", [128, 128], F32))
        maskc = e_(nc.sbuf_tensor("maskc_s", [128, 128], F32))
        sel = e_(nc.sbuf_tensor("sel_s", [128, 1], F32))
        ones_s = e_(nc.sbuf_tensor("ones_s", [1, 128], F32))
        identf = e_(nc.sbuf_tensor("identf_s", [128, 128], F32))
        crow = e_(nc.sbuf_tensor("crow", [1, 128], F32))
        expo = e_(nc.sbuf_tensor("expo", [128, 2, 128], F32))
        Mm = e_(nc.sbuf_tensor("Mm", [128, 2, 128], BF16))
        MT = e_(nc.sbuf_tensor("MT", [128, 2, 128], BF16))
        ctil = e_(nc.sbuf_tensor("ctil", [128, 2, 128], BF16))
        ctf = e_(nc.sbuf_tensor("ctf", [128, 2, 128], BF16))
        crf = e_(nc.sbuf_tensor("crf", [128, 2, 128], BF16))
        bff = e_(nc.sbuf_tensor("bff", [128, 2, 128], BF16))
        btil = e_(nc.sbuf_tensor("btil", [128, 2, 128], BF16))
        xdt = e_(nc.sbuf_tensor("xdt", [128, 2, 128], BF16))
        Ssb = e_(nc.sbuf_tensor("Ssb", [128, 128], F32))
        Sbf = e_(nc.sbuf_tensor("Sbf", [128, 128], BF16))
        dxb = e_(nc.sbuf_tensor("dxb", [128, 2, 128], F32))
        yb = e_(nc.sbuf_tensor("yb", [128, 2, 128], BF16))
        ps = [e_(nc.psum_tensor(f"pb{i}", [128, 512], F32)) for i in range(8)]

        p = Prog()
        names = (["xh0", "xh1", "btm0", "btm1", "ctm0", "ctm1", "acc", "sf", "sftm0", "sftm1",
                  "cw0", "cw1", "cb0", "cb1", "dtr0", "dtr1", "dt0", "dt1", "tmpn",
                  "cal0", "cal1", "clb0", "clb1", "dte0", "dte1", "dfs0", "dfs1", "cdk0", "cdk1",
                  "clr", "ab", "dtbb", "ddb0", "ddb1", "ddrow", "arow", "tri", "maskc", "sel",
                  "ones", "identf", "crow", "expo0", "expo1", "Mm0", "Mm1", "MT0", "MT1",
                  "ctil0", "ctil1", "ctf0", "ctf1", "crf0", "crf1", "bff0", "bff1",
                  "btil0", "btil1", "xdt0", "xdt1", "S", "Sbf", "dxb0", "dxb1", "yb0", "yb1"])
        bufs = {n: p.buf(n) for n in names}
        ps_bufs = [p.buf(f"ps{i}") for i in range(8)]

        for nm, sb, dd in (("tri", tri, tri_d), ("maskc", maskc, mask_d), ("sel", sel, sel_d),
                           ("ones", ones_s, ones_d), ("identf", identf, identf_d)):
            p.step("sync", lambda sy, sb=sb, dd=dd: sy.dma_start(out=sb[:], in_=dd), writes=[bufs[nm]], dma=True, tag=f"ld{nm}")
        p.step("sync", lambda sy: sy.dma_start(out=arow[:, 0:8], in_=alog_d), writes=[bufs["arow"]], dma=True, tag="ldal")
        p.step("sync", lambda sy: sy.dma_start(out=arow[:, 8:16], in_=dtb_d), writes=[bufs["arow"]], dma=True, tag="lddtb")
        p.step("sync", lambda sy: sy.dma_start(out=ddrow[:], in_=dd_d.rearrange("u o z -> o u z")), writes=[bufs["ddrow"]], dma=True, tag="lddd")
        p.step("scalar", lambda s: s.activation(out=arow[:, 0:8], in_=arow[:, 0:8], func=AF.Exp),
               reads=[bufs["arow"]], writes=[bufs["arow"]], tag="aexp")
        p.step("vector", lambda v: v.tensor_scalar_mul(out=arow[:, 0:8], in0=arow[:, 0:8], scalar1=-1.0),
               reads=[bufs["arow"]], writes=[bufs["arow"]], tag="aneg")
        p.step("tensor", lambda t: t.matmul(out=ps[6][:, 0:16], lhsT=ones_s[:], rhs=arow[:, 0:16], start=True, stop=True),
               reads=[bufs["ones"], bufs["arow"]], writes=[ps_bufs[6]], tag="abc")
        p.step("vector", lambda v: v.tensor_copy(out=ab[:], in_=ps[6][:, 0:8]), reads=[ps_bufs[6]], writes=[bufs["ab"]], tag="abcp")
        p.step("vector", lambda v: v.tensor_copy(out=dtbb[:], in_=ps[6][:, 8:16]), reads=[ps_bufs[6]], writes=[bufs["dtbb"]], tag="dtbcp")

        for u in range(8):
            ub = u % 2
            xh_b, btm_b, ctm_b = bufs[f"xh{ub}"], bufs[f"btm{ub}"], bufs[f"ctm{ub}"]
            p.step("vector", lambda v, ub=ub: v.memset(xh[:, ub, 0:4], 0.0), reads=[], writes=[xh_b], tag=f"xz{u}")
            p.step("sync", lambda sy, u=u, ub=ub: sy.dma_start(out=xh[:, ub, 4:], in_=xh_d[u]), writes=[xh_b], dma=True, tag=f"ldxh{u}")
            p.step("gpsimd", lambda g, u=u, ub=ub: g.dma_start(out=btm[:, ub, :, :], in_=btm_d[u // 4].rearrange("k p z -> p k z")),
                   writes=[btm_b], dma=True, tag=f"ldbt{u}")
            p.step("gpsimd", lambda g, u=u, ub=ub: g.dma_start(out=ctm[:, ub, :, :], in_=ctm_d[u // 4].rearrange("k p z -> p k z")),
                   writes=[ctm_b], dma=True, tag=f"ldct{u}")
            p.step("sync", lambda sy, u=u, ub=ub: sy.dma_start(out=cwv[:, ub, :], in_=cw_d[u]), writes=[bufs[f"cw{ub}"]], dma=True, tag=f"ldcw{u}")
            p.step("sync", lambda sy, u=u, ub=ub: sy.dma_start(out=cbv[:, ub, :], in_=cb_d[u]), writes=[bufs[f"cb{ub}"]], dma=True, tag=f"ldcb{u}")
            p.step("sync", lambda sy, u=u, ub=ub: sy.dma_start(out=dtr[:, ub, :], in_=dtr_d[u]), writes=[bufs[f"dtr{ub}"]], dma=True, tag=f"lddtr{u}")
            p.step("tensor", lambda t, u=u: t.matmul(out=ps[6][:, 0:128], lhsT=ones_s[:], rhs=ddrow[:, u, :], start=True, stop=True),
                   reads=[bufs["ones"], bufs["ddrow"]], writes=[ps_bufs[6]], tag=f"dbc{u}")
            p.step("vector", lambda v, ub=ub: v.tensor_copy(out=ddb[:, ub, :], in_=ps[6][:, 0:128]), reads=[ps_bufs[6]], writes=[bufs[f"ddb{ub}"]], tag=f"dbcp{u}")
            p.step("vector", lambda v: v.memset(Ssb[:], 0.0), reads=[], writes=[bufs["S"]], tag=f"sz{u}")

            for hf in range(2):
                o0 = hf * (T // 2)
                p.step("vector", lambda v, ub=ub, o0=o0: v.tensor_scalar_mul(
                    out=acc[:], in0=xh[:, ub, 4 + o0 - 3:4 + o0 - 3 + T // 2], scalar1=cwv[:, ub, 0:1]),
                    reads=[xh_b, bufs[f"cw{ub}"]], writes=[bufs["acc"]], tag=f"cv0_{u}_{hf}")
                for i in range(1, 4):
                    p.step("vector", lambda v, ub=ub, o0=o0, i=i: v.scalar_tensor_tensor(
                        out=acc[:], in0=xh[:, ub, 4 + o0 - 3 + i:4 + o0 - 3 + i + T // 2], scalar=cwv[:, ub, i:i + 1],
                        in1=acc[:], op0=ALU.mult, op1=ALU.add),
                        reads=[xh_b, bufs[f"cw{ub}"], bufs["acc"]], writes=[bufs["acc"]], tag=f"cv{i}_{u}_{hf}")
                p.step("scalar", lambda s, ub=ub, o0=o0: s.activation(out=sf[:, o0:o0 + T // 2], in_=acc[:], func=AF.Silu, bias=cbv[:, ub, 0:1]),
                       reads=[bufs["acc"], bufs[f"cb{ub}"]], writes=[bufs["sf"]], tag=f"cvs{u}_{hf}")
            p.step("sync", lambda sy, ub=ub: sy.dma_start(out=sftm[:, ub, :, :], in_=sf[:].rearrange("p (k t) -> p k t", k=NCH), transpose=True),
                   reads=[bufs["sf"]], writes=[bufs[f"sftm{ub}"]], dma=True, tag=f"cvt{u}")

            p.step("vector", lambda v, ub=ub, u=u: v.tensor_scalar(out=dtv[:, ub, :], in0=dtr[:, ub, :], scalar1=dtbb[:, u:u + 1], scalar2=None, op0=ALU.add),
                   reads=[bufs[f"dtr{ub}"], bufs["dtbb"]], writes=[bufs[f"dt{ub}"]], tag=f"dt0{u}")
            p.step("scalar", lambda s, ub=ub: s.activation(out=tmpn[:], in_=dtv[:, ub, :], func=AF.Abs),
                   reads=[bufs[f"dt{ub}"]], writes=[bufs["tmpn"]], tag=f"dt1{u}")
            p.step("scalar", lambda s: s.activation(out=tmpn[:], in_=tmpn[:], func=AF.Exp, scale=-1.0),
                   reads=[bufs["tmpn"]], writes=[bufs["tmpn"]], tag=f"dt2{u}")
            p.step("vector", lambda v: v.tensor_scalar_add(out=tmpn[:], in0=tmpn[:], scalar1=1.0),
                   reads=[bufs["tmpn"]], writes=[bufs["tmpn"]], tag=f"dt3{u}")
            p.step("scalar", lambda s: s.activation(out=tmpn[:], in_=tmpn[:], func=AF.Ln),
                   reads=[bufs["tmpn"]], writes=[bufs["tmpn"]], tag=f"dt4{u}")
            p.step("vector", lambda v, ub=ub: v.tensor_relu(out=dtv[:, ub, :], in_=dtv[:, ub, :]),
                   reads=[bufs[f"dt{ub}"]], writes=[bufs[f"dt{ub}"]], tag=f"dt5{u}")
            p.step("vector", lambda v, ub=ub: v.tensor_tensor(out=dtv[:, ub, :], in0=dtv[:, ub, :], in1=tmpn[:], op=ALU.add),
                   reads=[bufs[f"dt{ub}"], bufs["tmpn"]], writes=[bufs[f"dt{ub}"]], tag=f"dt6{u}")
            p.step("vector", lambda v, ub=ub, u=u: v.tensor_scalar_mul(out=tmpn[:], in0=dtv[:, ub, :], scalar1=ab[:, u:u + 1]),
                   reads=[bufs[f"dt{ub}"], bufs["ab"]], writes=[bufs["tmpn"]], tag=f"la{u}")
            p.step("tensor", lambda t: t.matmul(out=ps[6][:, 0:NCH], lhsT=tri[:], rhs=tmpn[:], start=True, stop=True),
                   reads=[bufs["tri"], bufs["tmpn"]], writes=[ps_bufs[6]], tag=f"cum{u}")
            p.step("vector", lambda v, ub=ub: v.tensor_copy(out=cal[:, ub, :], in_=ps[6][:, 0:NCH]),
                   reads=[ps_bufs[6]], writes=[bufs[f"cal{ub}"]], tag=f"cc{u}")
            p.step("tensor", lambda t, ub=ub: t.matmul(out=ps[6][0:1, 0:NCH], lhsT=sel[:], rhs=cal[:, ub, :], start=True, stop=True),
                   reads=[bufs["sel"], bufs[f"cal{ub}"]], writes=[ps_bufs[6]], tag=f"cl{u}")
            p.step("vector", lambda v: v.tensor_copy(out=clr[:], in_=ps[6][0:1, 0:NCH]),
                   reads=[ps_bufs[6]], writes=[bufs["clr"]], tag=f"clc{u}")
            p.step("tensor", lambda t: t.matmul(out=ps[6][:, 0:NCH], lhsT=ones_s[:], rhs=clr[:], start=True, stop=True),
                   reads=[bufs["ones"], bufs["clr"]], writes=[ps_bufs[6]], tag=f"clb{u}")
            p.step("vector", lambda v, ub=ub: v.tensor_copy(out=clb[:, ub, :], in_=ps[6][:, 0:NCH]),
                   reads=[ps_bufs[6]], writes=[bufs[f"clb{ub}"]], tag=f"clbc{u}")
            p.step("vector", lambda v, ub=ub: v.tensor_tensor(out=dte[:, ub, :], in0=clb[:, ub, :], in1=cal[:, ub, :], op=ALU.subtract),
                   reads=[bufs[f"clb{ub}"], bufs[f"cal{ub}"]], writes=[bufs[f"dte{ub}"]], tag=f"d0{u}")
            p.step("scalar", lambda s, ub=ub: s.activation(out=dte[:, ub, :], in_=dte[:, ub, :], func=AF.Exp),
                   reads=[bufs[f"dte{ub}"]], writes=[bufs[f"dte{ub}"]], tag=f"d1{u}")
            p.step("vector", lambda v, ub=ub: v.tensor_tensor(out=dte[:, ub, :], in0=dte[:, ub, :], in1=dtv[:, ub, :], op=ALU.mult),
                   reads=[bufs[f"dte{ub}"], bufs[f"dt{ub}"]], writes=[bufs[f"dte{ub}"]], tag=f"d2{u}")
            p.step("scalar", lambda s, ub=ub: s.activation(out=dfs[:, ub, :], in_=cal[:, ub, :], func=AF.Exp),
                   reads=[bufs[f"cal{ub}"]], writes=[bufs[f"dfs{ub}"]], tag=f"d3{u}")
            p.step("scalar", lambda s, ub=ub: s.activation(out=cdk[:, ub, :], in_=clb[:, ub, :], func=AF.Exp),
                   reads=[bufs[f"clb{ub}"]], writes=[bufs[f"cdk{ub}"]], tag=f"d4{u}")

            for k in range(NCH):
                kb = k % 2
                p.step("tensor", lambda t, ub=ub, k=k: t.transpose(out=ps[5][0:1, 0:128], in_=cal[:, ub, k:k + 1], identity=identf[:]),
                       reads=[bufs[f"cal{ub}"], bufs["identf"]], writes=[ps_bufs[5]], tag=f"ct{u}_{k}")
                p.step("vector", lambda v: v.tensor_copy(out=crow[:], in_=ps[5][0:1, 0:128]),
                       reads=[ps_bufs[5]], writes=[bufs["crow"]], tag=f"cr{u}_{k}")
                p.step("tensor", lambda t: t.matmul(out=ps[5][:, 0:128], lhsT=ones_s[:], rhs=crow[:], start=True, stop=True),
                       reads=[bufs["ones"], bufs["crow"]], writes=[ps_bufs[5]], tag=f"cb2{u}_{k}")
                p.step("vector", lambda v, ub=ub, k=k, kb=kb: v.tensor_scalar(out=expo[:, kb, :], in0=ps[5][:, 0:128], scalar1=cal[:, ub, k:k + 1], scalar2=-1.0, op0=ALU.subtract, op1=ALU.mult),
                       reads=[ps_bufs[5], bufs[f"cal{ub}"]], writes=[bufs[f"expo{kb}"]], tag=f"ex0{u}_{k}")
                p.step("vector", lambda v, kb=kb: v.tensor_tensor(out=expo[:, kb, :], in0=expo[:, kb, :], in1=maskc[:], op=ALU.add),
                       reads=[bufs[f"expo{kb}"], bufs["maskc"]], writes=[bufs[f"expo{kb}"]], tag=f"ex1{u}_{k}")
                p.step("scalar", lambda s, kb=kb: s.activation(out=expo[:, kb, :], in_=expo[:, kb, :], func=AF.Exp),
                       reads=[bufs[f"expo{kb}"]], writes=[bufs[f"expo{kb}"]], tag=f"ex2{u}_{k}")
                p.step("sync", lambda sy, ub=ub, k=k, kb=kb: sy.dma_start(out=crf[:, kb, :], in_=ctm[:, ub, k, :], transpose=True),
                       reads=[ctm_b], writes=[bufs[f"crf{kb}"]], dma=True, tag=f"ctf{u}_{k}")
                p.step("sync", lambda sy, ub=ub, k=k, kb=kb: sy.dma_start(out=bff[:, kb, :], in_=btm[:, ub, k, :], transpose=True),
                       reads=[btm_b], writes=[bufs[f"bff{kb}"]], dma=True, tag=f"btf{u}_{k}")
                p.step("tensor", lambda t, kb=kb: t.matmul(out=ps[6][:, 0:128], lhsT=crf[:, kb, :], rhs=bff[:, kb, :], start=True, stop=True),
                       reads=[bufs[f"crf{kb}"], bufs[f"bff{kb}"]], writes=[ps_bufs[6]], tag=f"cb{u}_{k}")
                p.step("vector", lambda v, kb=kb: v.tensor_tensor(out=Mm[:, kb, :], in0=expo[:, kb, :], in1=ps[6][:, 0:128], op=ALU.mult),
                       reads=[bufs[f"expo{kb}"], ps_bufs[6]], writes=[bufs[f"Mm{kb}"]], tag=f"m{u}_{k}")
                p.step("sync", lambda sy, kb=kb: sy.dma_start(out=MT[:, kb, :], in_=Mm[:, kb, :], transpose=True),
                       reads=[bufs[f"Mm{kb}"]], writes=[bufs[f"MT{kb}"]], dma=True, tag=f"mt{u}_{k}")
                p.step("vector", lambda v, ub=ub, k=k, kb=kb: v.tensor_scalar_mul(out=ctil[:, kb, :], in0=ctm[:, ub, k, :], scalar1=dfs[:, ub, k:k + 1]),
                       reads=[ctm_b, bufs[f"dfs{ub}"]], writes=[bufs[f"ctil{kb}"]], tag=f"cti{u}_{k}")
                p.step("sync", lambda sy, kb=kb: sy.dma_start(out=ctf[:, kb, :], in_=ctil[:, kb, :], transpose=True),
                       reads=[bufs[f"ctil{kb}"]], writes=[bufs[f"ctf{kb}"]], dma=True, tag=f"ctf2{u}_{k}")
                p.step("vector", lambda v, ub=ub, k=k, kb=kb: v.tensor_scalar_mul(out=xdt[:, kb, :], in0=sftm[:, ub, k, :], scalar1=dtv[:, ub, k:k + 1]),
                       reads=[bufs[f"sftm{ub}"], bufs[f"dt{ub}"]], writes=[bufs[f"xdt{kb}"]], tag=f"xd{u}_{k}")
                p.step("vector", lambda v: v.tensor_copy(out=Sbf[:], in_=Ssb[:]),
                       reads=[bufs["S"]], writes=[bufs["Sbf"]], tag=f"sb{u}_{k}")
                bank_y = kb

                def t_y(t, kb=kb, bank_y=bank_y):
                    t.matmul(out=ps[bank_y][:, 0:128], lhsT=MT[:, kb, :], rhs=xdt[:, kb, :], start=True, stop=False)
                    return t.matmul(out=ps[bank_y][:, 0:128], lhsT=ctf[:, kb, :], rhs=Sbf[:], start=False, stop=True)
                p.step("tensor", t_y, reads=[bufs[f"MT{kb}"], bufs[f"xdt{kb}"], bufs[f"ctf{kb}"], bufs["Sbf"]], writes=[ps_bufs[bank_y]], tag=f"y{u}_{k}")
                p.step("vector", lambda v, ub=ub, k=k, kb=kb: v.tensor_scalar_mul(out=btil[:, kb, :], in0=btm[:, ub, k, :], scalar1=dte[:, ub, k:k + 1]),
                       reads=[btm_b, bufs[f"dte{ub}"]], writes=[bufs[f"btil{kb}"]], tag=f"bt{u}_{k}")
                p.step("tensor", lambda t, kb=kb: t.matmul(out=ps[7][:, 0:128], lhsT=btil[:, kb, :], rhs=xdt[:, kb, :], start=True, stop=True),
                       reads=[bufs[f"btil{kb}"], bufs[f"xdt{kb}"]], writes=[ps_bufs[7]], tag=f"ds{u}_{k}")
                p.step("vector", lambda v, ub=ub, k=k: v.scalar_tensor_tensor(out=Ssb[:], in0=Ssb[:], scalar=cdk[:, ub, k:k + 1], in1=ps[7][:, 0:128], op0=ALU.mult, op1=ALU.add),
                       reads=[ps_bufs[7], bufs[f"cdk{ub}"], bufs["S"]], writes=[bufs["S"]], tag=f"su{u}_{k}")
                p.step("vector", lambda v, ub=ub, k=k, kb=kb: v.tensor_tensor(out=dxb[:, kb, :], in0=sftm[:, ub, k, :], in1=ddb[:, ub, :], op=ALU.mult),
                       reads=[bufs[f"sftm{ub}"], bufs[f"ddb{ub}"]], writes=[bufs[f"dxb{kb}"]], tag=f"dx{u}_{k}")
                p.step("vector", lambda v, kb=kb, bank_y=bank_y: v.tensor_tensor(out=yb[:, kb, :], in0=ps[bank_y][:, 0:128], in1=dxb[:, kb, :], op=ALU.add),
                       reads=[ps_bufs[bank_y], bufs[f"dxb{kb}"]], writes=[bufs[f"yb{kb}"]], tag=f"ye{u}_{k}")
                p.step("sync", lambda sy, u=u, k=k, kb=kb: sy.dma_start(out=y_o[u, k], in_=yb[:, kb, :]),
                       reads=[bufs[f"yb{kb}"]], writes=[], dma=True, tag=f"sy{u}_{k}")

        with nc.Block() as block:
            p.emit(nc, block)
            p.finish(nc, block)
    return nc



NHC = DIN // 128


def build_mamba_c():
    nc = bass.Bass("TRN2", target_bir_lowering=False)
    x_d = nc.dram_tensor("x", [NTT, 128, D], F32, kind="ExternalInput").ap()
    gate_d = nc.dram_tensor("gate", [NTT, 128, DIN], BF16, kind="ExternalInput").ap()
    yb_d = nc.dram_tensor("yb", [NTT, 128, DIN], BF16, kind="ExternalInput").ap()
    wout_d = nc.dram_tensor("wout", [NHC, 128, D], F32, kind="ExternalInput").ap()
    y_d = nc.dram_tensor("y", [NTT, 128, D], F32, kind="ExternalOutput").ap()

    TTB = 4
    NTB = NTT // TTB

    from contextlib import ExitStack
    with ExitStack() as st_:
        e_ = st_.enter_context
        xt = e_(nc.sbuf_tensor("xt", [128, 2, D], F32))
        xln = e_(nc.sbuf_tensor("xln", [128, 4, D], F32))
        scr = e_(nc.sbuf_tensor("scr", [128, 2, D], BF16))
        stats = e_(nc.sbuf_tensor("stats", [128, 2, 8], F32))
        gb = e_(nc.sbuf_tensor("gb", [128, DIN], BF16))
        ybt = e_(nc.sbuf_tensor("ybt", [128, DIN], BF16))
        sg = e_(nc.sbuf_tensor("sg", [128, DIN], BF16))
        zb = e_(nc.sbuf_tensor("zb", [128, DIN], BF16))
        zf = e_(nc.sbuf_tensor("zf", [128, NHC, TTB * 128], BF16))
        wo = e_(nc.sbuf_tensor("wo_s", [128, 2, NHC, 512], BF16))
        xres = e_(nc.sbuf_tensor("xres", [128, 2, 512], F32))
        ytile = e_(nc.sbuf_tensor("ytile", [128, 2, 512], F32))
        ps = [e_(nc.psum_tensor(f"pb{i}", [128, 512], F32)) for i in range(8)]

        p = Prog()
        x_bufs = [p.buf(f"xt{i}") for i in range(2)]
        xln_bufs = [p.buf(f"xln{i}") for i in range(4)]
        scr_bufs = [p.buf(f"scr{i}") for i in range(2)]
        st_bufs = [p.buf(f"st{i}") for i in range(2)]
        gb_buf, yb_buf, sg_buf, zb_buf, zf_buf = p.buf("gb"), p.buf("yb"), p.buf("sg"), p.buf("zb"), p.buf("zf")
        wo_bufs = [p.buf(f"wo{i}") for i in range(2)]
        xres_bufs = [p.buf(f"xr{i}") for i in range(2)]
        yt_bufs = [p.buf(f"yt{i}") for i in range(2)]
        ps_bufs = [p.buf(f"ps{i}") for i in range(8)]

        for tb in range(NTB):
            for tt in range(TTB):
                gt = tb * TTB + tt
                bi = gt % 2
                p.step("sync", lambda sy, gt=gt, bi=bi: sy.dma_start(out=xt[:, bi, :], in_=x_d[gt]),
                       writes=[x_bufs[bi]], dma=True, tag=f"ldx{gt}")
                emit_ln_tile(p, xt[:, bi, :], x_bufs[bi], xln[:, tt, :], xln_bufs[tt],
                             scr[:, bi, :], scr_bufs[bi], stats[:, bi, :], st_bufs[bi])
                p.step("sync", lambda sy, gt=gt: sy.dma_start(out=gb[:], in_=gate_d[gt]),
                       writes=[gb_buf], dma=True, tag=f"ldg{gt}")
                p.step("sync", lambda sy, gt=gt: sy.dma_start(out=ybt[:], in_=yb_d[gt]),
                       writes=[yb_buf], dma=True, tag=f"ldy{gt}")
                p.step("scalar", lambda s: s.activation(out=sg[:], in_=gb[:], func=AF.Silu),
                       reads=[gb_buf], writes=[sg_buf], tag=f"sl{gt}")
                p.step("vector", lambda v: v.tensor_tensor(out=zb[:], in0=ybt[:], in1=sg[:], op=ALU.mult),
                       reads=[yb_buf, sg_buf], writes=[zb_buf], tag=f"z{gt}")
                p.step("sync", lambda sy, tt=tt: sy.dma_start(
                    out=zf[:, :, tt * 128:(tt + 1) * 128], in_=zb[:].rearrange("t (c d) -> t c d", c=NHC), transpose=True),
                    reads=[zb_buf], writes=[zf_buf], dma=True, tag=f"trz{gt}")
            for dc in range(4):
                wbi = dc % 2
                p.step("gpsimd", lambda g, dc=dc, wbi=wbi: g.dma_start(
                    out=wo[:, wbi, :, :], in_=wout_d[:, :, dc * 512:(dc + 1) * 512].rearrange("c p f -> p c f")),
                    writes=[wo_bufs[wbi]], dma=True, tag=f"ldw{tb}_{dc}")
                for tt in range(TTB):
                    gt = tb * TTB + tt
                    bank = (tt * 4 + dc) % 8
                    ri = (tt + dc) % 2
                    p.step("sync", lambda sy, gt=gt, dc=dc, ri=ri: sy.dma_start(
                        out=xres[:, ri, :], in_=x_d[gt][:, dc * 512:(dc + 1) * 512]),
                        writes=[xres_bufs[ri]], dma=True, tag=f"lr{gt}_{dc}")

                    def t_o(t, tt=tt, dc=dc, wbi=wbi, bank=bank):
                        last = None
                        for c in range(NHC):
                            last = t.matmul(out=ps[bank][:], lhsT=zf[:, c, tt * 128:(tt + 1) * 128],
                                            rhs=wo[:, wbi, c, :], start=(c == 0), stop=(c == NHC - 1))
                        return last
                    p.step("tensor", t_o, reads=[zf_buf, wo_bufs[wbi]], writes=[ps_bufs[bank]], tag=f"mm{gt}_{dc}")
                    p.step("vector", lambda v, bank=bank, ri=ri, tt=tt, dc=dc: v.tensor_tensor(
                        out=ytile[:, ri, :], in0=ps[bank][:], in1=xln[:, tt, dc * 512:(dc + 1) * 512], op=ALU.add),
                        reads=[ps_bufs[bank], xln_bufs[tt]], writes=[yt_bufs[ri]], tag=f"e1{gt}_{dc}")
                    p.step("vector", lambda v, ri=ri: v.tensor_tensor(
                        out=ytile[:, ri, :], in0=ytile[:, ri, :], in1=xres[:, ri, :], op=ALU.add),
                        reads=[yt_bufs[ri], xres_bufs[ri]], writes=[yt_bufs[ri]], tag=f"e2{gt}_{dc}")
                    p.step("sync", lambda sy, gt=gt, dc=dc, ri=ri: sy.dma_start(
                        out=y_d[gt][:, dc * 512:(dc + 1) * 512], in_=ytile[:, ri, :]),
                        reads=[yt_bufs[ri]], writes=[], dma=True, tag=f"st{gt}_{dc}")

        with nc.Block() as block:
            p.emit(nc, block)
            p.finish(nc, block)
    return nc


def host_mamba_weights(norm_scale, in_proj_w, conv_weight, conv_bias, A_log, dt_bias, Dv, out_proj_w):
    A = np.asarray(in_proj_w)
    Ap = np.zeros((D, NJP * 512), np.float32)
    Ap[:, :PROJ] = A
    Ap = Ap.reshape(NDC, 128, NJP, 512)
    win = f32(Ap.transpose(2, 1, 0, 3))
    cw_ = np.asarray(conv_weight)
    cb_ = np.asarray(conv_bias)
    wout = f32(np.asarray(out_proj_w).reshape(NHC, 128, D))
    tri = f32(np.triu(np.ones((128, 128), np.float32)))
    maskc = f32(np.triu(np.ones((128, 128), np.float32), k=1) * -3e4)
    ones = f32(np.ones((1, 128)))
    sel127 = np.zeros((128, 1), np.float32)
    sel127[127, 0] = 1.0
    return dict(win=win, ns=f32(np.asarray(norm_scale).reshape(1, D)), cw=cw_, cb=cb_,
                alog=f32(np.asarray(A_log)), dtb=f32(np.asarray(dt_bias)), dd=f32(np.asarray(Dv)),
                wout=wout, tri=tri, maskc=maskc, ones=ones, sel127=f32(sel127), identf=f32(np.eye(128)))


def run_mamba(x_full, W, nc_cache={}):
    if "mam_a" not in nc_cache:
        nc_cache["mam_a"] = build_mamba_a()
        nc_cache["mam_b"] = build_mamba_b()
        nc_cache["mam_c"] = build_mamba_c()
    xr = np.ascontiguousarray(x_full.reshape(NCORES, NTT, 128, D))
    in_maps = [dict(x=f32(xr[c]), win=W["win"], ns=W["ns"], ones=W["ones"]) for c in range(NCORES)]
    res = run_bass_kernel_spmd(nc_cache["mam_a"], in_maps, core_ids=list(range(NCORES)))
    gate = np.stack([r["gate"] for r in res.results]).reshape(B, T, DIN)
    ssm = np.stack([r["ssm"] for r in res.results]).reshape(B, T, DIN)
    dbc = np.stack([r["dbc"] for r in res.results]).reshape(B, T, 288)

    in_maps_b = []
    for c in range(NCORES):
        units = [(bb, c * 4 + hh) for bb in range(2) for hh in range(4)]
        xh = np.stack([np.ascontiguousarray(ssm[bb, :, h * 128:(h + 1) * 128].T) for (bb, h) in units]).astype(BF)
        cwl = np.stack([np.ascontiguousarray(W["cw"][:, h * 128:(h + 1) * 128].T) for (bb, h) in units])
        cbl = np.stack([W["cb"][h * 128:(h + 1) * 128][:, None] for (bb, h) in units])
        dtr = np.stack([np.ascontiguousarray(dbc[bb, :, h].reshape(NCH, 128).T) for (bb, h) in units])
        alog = np.array([[W["alog"][h] for (bb, h) in units]], np.float32)
        dtb = np.array([[W["dtb"][h] for (bb, h) in units]], np.float32)
        dd = np.stack([W["dd"][h * 128:(h + 1) * 128][None, :] for (bb, h) in units])
        btm = np.stack([np.ascontiguousarray(dbc[bb, :, 32:160].reshape(NCH, 128, 128)) for bb in range(2)]).astype(BF)
        ctm = np.stack([np.ascontiguousarray(dbc[bb, :, 160:288].reshape(NCH, 128, 128)) for bb in range(2)]).astype(BF)
        in_maps_b.append(dict(xh=xh, cw=f32(cwl), cb=f32(cbl), dtr=f32(dtr), alog=alog, dtb=dtb,
                              dd=f32(dd), btm=btm, ctm=ctm, tri=W["tri"], maskc=W["maskc"],
                              sel127=W["sel127"], ones=W["ones"], identf=W["identf"]))
    res_b = run_bass_kernel_spmd(nc_cache["mam_b"], in_maps_b, core_ids=list(range(NCORES)))
    yss = np.zeros((B, T, 32, 128), dtype=BF)
    for c in range(NCORES):
        units = [(bb, c * 4 + hh) for bb in range(2) for hh in range(4)]
        for i, (bb, h) in enumerate(units):
            yss[bb, :, h, :] = res_b.results[c]["y"][i].reshape(T, 128)
    yflat = yss.reshape(B, T, DIN)

    in_maps_c = []
    gre = gate.reshape(NCORES, NTT, 128, DIN)
    yre = yflat.reshape(NCORES, NTT, 128, DIN)
    for c in range(NCORES):
        in_maps_c.append(dict(x=f32(xr[c]), gate=np.ascontiguousarray(gre[c]), yb=np.ascontiguousarray(yre[c]), wout=W["wout"]))
    res_c = run_bass_kernel_spmd(nc_cache["mam_c"], in_maps_c, core_ids=list(range(NCORES)))
    return np.stack([r["y"] for r in res_c.results]).reshape(B, T, D)


GM = 64
GR = 128


def build_gla_a():
    nc = bass.Bass("TRN2", target_bir_lowering=False)
    x_d = nc.dram_tensor("x", [NTT, 128, D], F32, kind="ExternalInput").ap()
    xh_d = nc.dram_tensor("xh", [128, D], F32, kind="ExternalInput").ap()
    inp_d = nc.dram_tensor("inp", [NDC, 128, 128], F32, kind="ExternalInput").ap()
    mgw_d = nc.dram_tensor("mgw", [NDC, 128, 128], F32, kind="ExternalInput").ap()
    w0_d = nc.dram_tensor("w0", [NDC, 128, GR], F32, kind="ExternalInput").ap()
    w1_d = nc.dram_tensor("w1", [GR, D], F32, kind="ExternalInput").ap()
    b1_d = nc.dram_tensor("b1", [1, D], F32, kind="ExternalInput").ap()
    mgb_d = nc.dram_tensor("mgb", [1, 128], F32, kind="ExternalInput").ap()
    cw_d = nc.dram_tensor("cwg", [NDC, 128, 4], F32, kind="ExternalInput").ap()
    cb_d = nc.dram_tensor("cbg", [NDC, 128, 1], F32, kind="ExternalInput").ap()
    ones_d = nc.dram_tensor("ones", [1, 128], F32, kind="ExternalInput").ap()
    rkh_o = nc.dram_tensor("rkh", [NTT, 128, 128], BF16, kind="ExternalOutput").ap()
    g_o = nc.dram_tensor("g", [NTT, 128, GM], F32, kind="ExternalOutput").ap()
    u_o = nc.dram_tensor("u", [NTT, 128, D], BF16, kind="ExternalOutput").ap()
    sh_o = nc.dram_tensor("sh", [NTT, 128, D], BF16, kind="ExternalOutput").ap()

    from contextlib import ExitStack
    with ExitStack() as st_:
        e_ = st_.enter_context
        xt = e_(nc.sbuf_tensor("xt", [128, 2, D], F32))
        xln = e_(nc.sbuf_tensor("xln", [128, 2, D], BF16))
        xlf = e_(nc.sbuf_tensor("xlf", [128, NDC, 128 + TLOC], BF16))
        scr = e_(nc.sbuf_tensor("scr", [128, 1, D], BF16))
        stats = e_(nc.sbuf_tensor("stats", [128, 2, 8], F32))
        shf = e_(nc.sbuf_tensor("shf", [128, NDC, TLOC], BF16))
        conv = e_(nc.sbuf_tensor("conv", [128, 128], F32))
        inp = e_(nc.sbuf_tensor("inp_s", [128, NDC, 128], BF16))
        mgw = e_(nc.sbuf_tensor("mgw_s", [128, NDC, 128], BF16))
        w0 = e_(nc.sbuf_tensor("w0_s", [128, NDC, GR], BF16))
        w1 = e_(nc.sbuf_tensor("w1_s", [128, D], BF16))
        b1b = e_(nc.sbuf_tensor("b1b", [128, D], BF16))
        mgbb = e_(nc.sbuf_tensor("mgbb", [128, 128], F32))
        cw = e_(nc.sbuf_tensor("cw_s", [128, NDC, 4], F32))
        cb = e_(nc.sbuf_tensor("cb_s", [128, NDC], F32))
        row = e_(nc.sbuf_tensor("row", [1, D], F32))
        ones_s = e_(nc.sbuf_tensor("ones_s", [1, 128], F32))
        ch0 = e_(nc.sbuf_tensor("ch0", [128, GR], BF16))
        ch0f = e_(nc.sbuf_tensor("ch0f", [128, GR], BF16))
        chs = e_(nc.sbuf_tensor("chs", [128, D], BF16))
        mgt = e_(nc.sbuf_tensor("mgt", [128, 128], F32))
        selt = e_(nc.sbuf_tensor("selt", [128, GM * 3], F32))
        rkt = e_(nc.sbuf_tensor("rkt", [128, 128], F32))
        nstt = e_(nc.sbuf_tensor("nstt", [128, 4], F32))
        obf = e_(nc.sbuf_tensor("obf", [128, D], BF16))
        ps = [e_(nc.psum_tensor(f"pb{i}", [128, 512], F32)) for i in range(8)]

        p = Prog()
        bufs = {n: p.buf(n) for n in ["xt0", "xt1", "xln0", "xln1", "xlf", "scr0", "scr1", "st0", "st1",
                                      "shf", "conv", "inp", "mgw", "w0", "w1", "b1b", "mgbb", "cw", "cb",
                                      "row", "ones", "ch0", "ch0f", "chs", "mgt", "selt", "rkt", "nstt", "obf"]}
        ps_bufs = [p.buf(f"ps{i}") for i in range(8)]

        p.step("sync", lambda sy: sy.dma_start(out=ones_s[:], in_=ones_d), writes=[bufs["ones"]], dma=True, tag="ldon")
        p.step("gpsimd", lambda g: g.dma_start(out=inp[:], in_=inp_d.rearrange("c p f -> p c f")), writes=[bufs["inp"]], dma=True, tag="ldin")
        p.step("gpsimd", lambda g: g.dma_start(out=mgw[:], in_=mgw_d.rearrange("c p f -> p c f")), writes=[bufs["mgw"]], dma=True, tag="ldmg")
        p.step("gpsimd", lambda g: g.dma_start(out=w0[:], in_=w0_d.rearrange("c p f -> p c f")), writes=[bufs["w0"]], dma=True, tag="ldw0")
        p.step("gpsimd", lambda g: g.dma_start(out=w1[:], in_=w1_d), writes=[bufs["w1"]], dma=True, tag="ldw1")
        p.step("sync", lambda sy: sy.dma_start(out=cw[:], in_=cw_d.rearrange("c p f -> p c f")), writes=[bufs["cw"]], dma=True, tag="ldcw")
        p.step("sync", lambda sy: sy.dma_start(out=cb[:], in_=cb_d.rearrange("c p o -> p (c o)"), allow_slow_non_contiguous=True), writes=[bufs["cb"]], dma=True, tag="ldcb")
        p.step("sync", lambda sy: sy.dma_start(out=row[:], in_=b1_d), writes=[bufs["row"]], dma=True, tag="ldb1")
        emit_bcast_row(p, ones_s[:], bufs["ones"], row, bufs["row"], b1b, bufs["b1b"], ps, ps_bufs, [0, 1, 2, 3], D)
        p.step("sync", lambda sy: sy.dma_start(out=row[:, 0:128], in_=mgb_d), writes=[bufs["row"]], dma=True, tag="ldmb")
        emit_bcast_row(p, ones_s[:], bufs["ones"], row[:, 0:128].rearrange("o f -> o f"), bufs["row"], mgbb, bufs["mgbb"], ps, ps_bufs, [4], 128)

        for ti in range(NTT + 1):
            bi = ti % 2
            src_ap = xh_d if ti == 0 else x_d[ti - 1]
            p.step("sync", lambda sy, src_ap=src_ap, bi=bi: sy.dma_start(out=xt[:, bi, :], in_=src_ap),
                   writes=[bufs[f"xt{bi}"]], dma=True, tag=f"ldx{ti}")
            emit_ln_tile(p, xt[:, bi, :], bufs[f"xt{bi}"], xln[:, bi, :], bufs[f"xln{bi}"],
                         scr[:, 0, :], bufs["scr0"], stats[:, bi, :], bufs[f"st{bi}"])
            p.step("sync", lambda sy, ti=ti, bi=bi: sy.dma_start(
                out=xlf[:, :, ti * 128:(ti + 1) * 128], in_=xln[:, bi, :].rearrange("t (c d) -> t c d", c=NDC), transpose=True),
                reads=[bufs[f"xln{bi}"]], writes=[bufs["xlf"]], dma=True, tag=f"trx{ti}")
            if ti >= 1:
                gt = ti - 1

                def t_ch0(t, ti=ti):
                    last = None
                    for c in range(NDC):
                        last = t.matmul(out=ps[7][:, 0:GR], lhsT=xlf[:, c, ti * 128:(ti + 1) * 128], rhs=w0[:, c, :], start=(c == 0), stop=(c == NDC - 1))
                    return last
                p.step("tensor", t_ch0, reads=[bufs["xlf"], bufs["w0"]], writes=[ps_bufs[7]], tag=f"mmc0{gt}")
                p.step("vector", lambda v: v.tensor_copy(out=ch0[:], in_=ps[7][:, 0:GR]),
                       reads=[ps_bufs[7]], writes=[bufs["ch0"]], tag=f"ch0{gt}")
                p.step("sync", lambda sy: sy.dma_start(out=ch0f[:], in_=ch0[:], transpose=True),
                       reads=[bufs["ch0"]], writes=[bufs["ch0f"]], dma=True, tag=f"trc{gt}")

                def t_ch1(t):
                    last = None
                    for j in range(4):
                        last = t.matmul(out=ps[j][:], lhsT=ch0f[:], rhs=w1[:, j * 512:(j + 1) * 512], start=True, stop=True)
                    return last
                p.step("tensor", t_ch1, reads=[bufs["ch0f"], bufs["w1"]], writes=[ps_bufs[0], ps_bufs[1], ps_bufs[2], ps_bufs[3]], tag=f"mmch{gt}")
                for j in range(4):
                    p.step("vector", lambda v, j=j: v.tensor_tensor(out=chs[:, j * 512:(j + 1) * 512], in0=ps[j][:], in1=b1b[:, j * 512:(j + 1) * 512], op=ALU.add),
                           reads=[ps_bufs[j], bufs["b1b"]], writes=[bufs["chs"]], tag=f"chb{j}_{gt}")
                p.step("scalar", lambda s: s.activation(out=chs[:], in_=chs[:], func=AF.Sigmoid),
                       reads=[bufs["chs"]], writes=[bufs["chs"]], tag=f"chg{gt}")
                p.step("vector", lambda v, bi=bi: v.tensor_tensor(out=obf[:], in0=chs[:], in1=xln[:, bi, :], op=ALU.mult),
                       reads=[bufs["chs"], bufs[f"xln{bi}"]], writes=[bufs["obf"]], tag=f"umul{gt}")
                p.step("sync", lambda sy, gt=gt: sy.dma_start(out=u_o[gt], in_=obf[:]),
                       reads=[bufs["obf"]], writes=[], dma=True, tag=f"stu{gt}")

        for c in range(NDC):
            for tt in range(NTT):
                base = 128 + tt * 128 - 3
                p.step("vector", lambda v, c=c, base=base: v.tensor_scalar_mul(
                    out=conv[:], in0=xlf[:, c, base:base + 128], scalar1=cw[:, c, 0:1]),
                    reads=[bufs["xlf"], bufs["cw"]], writes=[bufs["conv"]], tag=f"cv0_{c}_{tt}")
                for i in range(1, 4):
                    p.step("vector", lambda v, c=c, i=i, base=base: v.scalar_tensor_tensor(
                        out=conv[:], in0=xlf[:, c, base + i:base + i + 128], scalar=cw[:, c, i:i + 1],
                        in1=conv[:], op0=ALU.mult, op1=ALU.add),
                        reads=[bufs["xlf"], bufs["cw"], bufs["conv"]], writes=[bufs["conv"]], tag=f"cv{i}_{c}_{tt}")
                p.step("scalar", lambda s, c=c, tt=tt: s.activation(out=shf[:, c, tt * 128:(tt + 1) * 128], in_=conv[:], func=AF.Silu, bias=cb[:, c:c + 1]),
                       reads=[bufs["conv"], bufs["cb"]], writes=[bufs["shf"]], tag=f"cvs{c}_{tt}")

        for gt in range(NTT):
            for c in range(NDC):
                p.step("sync", lambda sy, gt=gt, c=c: sy.dma_start(
                    out=obf[:, c * 128:(c + 1) * 128], in_=shf[:, c, gt * 128:(gt + 1) * 128], transpose=True),
                    reads=[bufs["shf"]], writes=[bufs["obf"]], dma=True, tag=f"trs{gt}_{c}")
            p.step("sync", lambda sy, gt=gt: sy.dma_start(out=sh_o[gt], in_=obf[:]),
                   reads=[bufs["obf"]], writes=[], dma=True, tag=f"sts{gt}")

            def t_rk(t, gt=gt):
                last = None
                for c in range(NDC):
                    t.matmul(out=ps[5][:, 0:128], lhsT=shf[:, c, gt * 128:(gt + 1) * 128], rhs=inp[:, c, :], start=(c == 0), stop=(c == NDC - 1))
                for c in range(NDC):
                    last = t.matmul(out=ps[6][:, 0:128], lhsT=shf[:, c, gt * 128:(gt + 1) * 128], rhs=mgw[:, c, :], start=(c == 0), stop=(c == NDC - 1))
                return last
            p.step("tensor", t_rk, reads=[bufs["shf"], bufs["inp"], bufs["mgw"]],
                   writes=[ps_bufs[5], ps_bufs[6]], tag=f"mmrk{gt}")

            p.step("vector", lambda v: v.tensor_tensor(out=mgt[:], in0=ps[6][:, 0:128], in1=mgbb[:], op=ALU.add),
                   reads=[ps_bufs[6], bufs["mgbb"]], writes=[bufs["mgt"]], tag=f"mgb{gt}")
            p.step("scalar", lambda s: s.activation(out=selt[:, 0:GM], in_=mgt[:, 0:GM], func=AF.Abs),
                   reads=[bufs["mgt"]], writes=[bufs["selt"]], tag=f"sp0{gt}")
            p.step("scalar", lambda s: s.activation(out=selt[:, 0:GM], in_=selt[:, 0:GM], func=AF.Exp, scale=-1.0),
                   reads=[bufs["selt"]], writes=[bufs["selt"]], tag=f"sp1{gt}")
            p.step("vector", lambda v: v.tensor_scalar_add(out=selt[:, 0:GM], in0=selt[:, 0:GM], scalar1=1.0),
                   reads=[bufs["selt"]], writes=[bufs["selt"]], tag=f"sp2{gt}")
            p.step("scalar", lambda s: s.activation(out=selt[:, 0:GM], in_=selt[:, 0:GM], func=AF.Ln),
                   reads=[bufs["selt"]], writes=[bufs["selt"]], tag=f"sp3{gt}")
            p.step("vector", lambda v: v.tensor_relu(out=selt[:, GM:2 * GM], in_=mgt[:, 0:GM]),
                   reads=[bufs["mgt"]], writes=[bufs["selt"]], tag=f"sp4{gt}")
            p.step("vector", lambda v: v.tensor_tensor(out=selt[:, 0:GM], in0=selt[:, 0:GM], in1=selt[:, GM:2 * GM], op=ALU.add),
                   reads=[bufs["selt"]], writes=[bufs["selt"]], tag=f"sp5{gt}")
            p.step("scalar", lambda s: s.activation(out=selt[:, GM:2 * GM], in_=mgt[:, GM:128], func=AF.Sigmoid),
                   reads=[bufs["mgt"]], writes=[bufs["selt"]], tag=f"tau{gt}")
            p.step("vector", lambda v: v.tensor_tensor(out=selt[:, 2 * GM:3 * GM], in0=selt[:, 0:GM], in1=selt[:, GM:2 * GM], op=ALU.mult),
                   reads=[bufs["selt"]], writes=[bufs["selt"]], tag=f"g0{gt}")
            p.step("vector", lambda v: v.tensor_scalar_mul(out=selt[:, 2 * GM:3 * GM], in0=selt[:, 2 * GM:3 * GM], scalar1=-1.0),
                   reads=[bufs["selt"]], writes=[bufs["selt"]], tag=f"g1{gt}")
            p.step("sync", lambda sy, gt=gt: sy.dma_start(out=g_o[gt], in_=selt[:, 2 * GM:3 * GM]),
                   reads=[bufs["selt"]], writes=[], dma=True, tag=f"stg{gt}")
            p.step("scalar", lambda s: s.activation(out=rkt[:, 0:GM], in_=selt[:, 0:GM], func=AF.Ln),
                   reads=[bufs["selt"]], writes=[bufs["rkt"]], tag=f"it0{gt}")
            p.step("vector", lambda v: v.tensor_tensor(out=rkt[:, 0:GM], in0=rkt[:, 0:GM], in1=selt[:, GM:2 * GM], op=ALU.mult),
                   reads=[bufs["rkt"], bufs["selt"]], writes=[bufs["rkt"]], tag=f"it1{gt}")
            p.step("scalar", lambda s: s.activation(out=rkt[:, 0:GM], in_=rkt[:, 0:GM], func=AF.Exp),
                   reads=[bufs["rkt"]], writes=[bufs["rkt"]], tag=f"it2{gt}")
            p.step("scalar", lambda s: s.activation(out=rkt[:, GM:128], in_=ps[5][:, GM:128], func=AF.Square, accum_out=nstt[:, 0:1]),
                   reads=[ps_bufs[5]], writes=[bufs["rkt"], bufs["nstt"]], tag=f"kn0{gt}")
            p.step("scalar", lambda s: s.activation(out=nstt[:, 1:2], in_=nstt[:, 0:1], func=AF.Sqrt),
                   reads=[bufs["nstt"]], writes=[bufs["nstt"]], tag=f"kn1{gt}")
            p.step("vector", lambda v: v.tensor_scalar_max(out=nstt[:, 1:2], in0=nstt[:, 1:2], scalar1=1e-12),
                   reads=[bufs["nstt"]], writes=[bufs["nstt"]], tag=f"kn2{gt}")
            p.step("vector", lambda v: v.reciprocal(out=nstt[:, 1:2], in_=nstt[:, 1:2]),
                   reads=[bufs["nstt"]], writes=[bufs["nstt"]], tag=f"kn3{gt}")
            p.step("vector", lambda v: v.tensor_scalar_mul(out=rkt[:, GM:128], in0=ps[5][:, GM:128], scalar1=nstt[:, 1:2]),
                   reads=[ps_bufs[5], bufs["nstt"]], writes=[bufs["rkt"]], tag=f"kh0{gt}")
            p.step("vector", lambda v: v.tensor_tensor(out=obf[:, GM:128], in0=rkt[:, GM:128], in1=rkt[:, 0:GM], op=ALU.mult),
                   reads=[bufs["rkt"]], writes=[bufs["obf"]], tag=f"kh1{gt}")
            p.step("vector", lambda v: v.tensor_copy(out=obf[:, 0:GM], in_=ps[5][:, 0:GM]),
                   reads=[ps_bufs[5]], writes=[bufs["obf"]], tag=f"rc{gt}")
            p.step("sync", lambda sy, gt=gt: sy.dma_start(out=rkh_o[gt], in_=obf[:, 0:128]),
                   reads=[bufs["obf"]], writes=[], dma=True, tag=f"strk{gt}")

        with nc.Block() as block:
            p.emit(nc, block)
            p.finish(nc, block)
    return nc



GSC = 1.0 / 8.0


def build_gla_b():
    nc = bass.Bass("TRN2", target_bir_lowering=False)
    rkh_d = nc.dram_tensor("rkh", [128, 128, 128], BF16, kind="ExternalInput").ap()
    g_d = nc.dram_tensor("g", [128, 128, GM], F32, kind="ExternalInput").ap()
    u_d = nc.dram_tensor("u", [128, 128, 256], BF16, kind="ExternalInput").ap()
    sh_d = nc.dram_tensor("sh", [128, 128, 256], BF16, kind="ExternalInput").ap()
    x_d = nc.dram_tensor("x", [128, 128, 256], F32, kind="ExternalInput").ap()
    rw_d = nc.dram_tensor("rw", [1, 256], F32, kind="ExternalInput").ap()
    tri_d = nc.dram_tensor("tri", [128, 128], F32, kind="ExternalInput").ap()
    sel_d = nc.dram_tensor("sel127", [128, 1], F32, kind="ExternalInput").ap()
    msk_d = nc.dram_tensor("msks", [4, 32, 128], F32, kind="ExternalInput").ap()
    identb_d = nc.dram_tensor("identb", [128, 128], BF16, kind="ExternalInput").ap()
    identf_d = nc.dram_tensor("identf", [128, 128], F32, kind="ExternalInput").ap()
    ones_d = nc.dram_tensor("ones", [1, 128], F32, kind="ExternalInput").ap()
    y_o = nc.dram_tensor("y", [128, 128, 256], F32, kind="ExternalOutput").ap()

    from contextlib import ExitStack
    with ExitStack() as st_:
        e_ = st_.enter_context
        rkh = e_(nc.sbuf_tensor("rkh_s", [128, 2, 128], BF16))
        gts = e_(nc.sbuf_tensor("gts", [128, 2, GM], F32))
        ut = e_(nc.sbuf_tensor("ut", [128, 2, 256], BF16))
        sht = e_(nc.sbuf_tensor("sht", [128, 2, 256], BF16))
        xts = e_(nc.sbuf_tensor("xts", [128, 2, 256], F32))
        rwb = e_(nc.sbuf_tensor("rwb", [128, 256], F32))
        rwrow = e_(nc.sbuf_tensor("rwrow", [1, 256], F32))
        tri = e_(nc.sbuf_tensor("tri_s", [128, 128], F32))
        sel = e_(nc.sbuf_tensor("sel_s", [128, 1], F32))
        msks = e_(nc.sbuf_tensor("msks_s", [128, 128], F32))
        identb = e_(nc.sbuf_tensor("identb_s", [128, 128], BF16))
        identf = e_(nc.sbuf_tensor("identf_s", [128, 128], F32))
        ones_s = e_(nc.sbuf_tensor("ones_s", [1, 128], F32))
        csb = e_(nc.sbuf_tensor("csb", [128, GM], F32))
        clb = e_(nc.sbuf_tensor("clb", [128, GM], F32))
        ekt = e_(nc.sbuf_tensor("ekt", [128, GM], F32))
        kht = e_(nc.sbuf_tensor("kht", [128, GM], BF16))
        cf = e_(nc.sbuf_tensor("cf", [GM, 128], F32))
        rf = e_(nc.sbuf_tensor("rf", [GM, 128], BF16))
        kf = e_(nc.sbuf_tensor("kf_g", [GM, 128], BF16))
        rtf = e_(nc.sbuf_tensor("rtf", [GM, 128], BF16))
        ektf = e_(nc.sbuf_tensor("ektf", [GM, 128], F32))
        ktf = e_(nc.sbuf_tensor("ktf", [GM, 128], BF16))
        Am = e_(nc.sbuf_tensor("Am", [32, 128], BF16))
        AmT = e_(nc.sbuf_tensor("AmT", [128, 128], BF16))
        Ssb = e_(nc.sbuf_tensor("Ssb_g", [GM, 256], F32))
        Sbf = e_(nc.sbuf_tensor("Sbf_g", [GM, 256], BF16))
        yt = e_(nc.sbuf_tensor("yt", [128, 2, 256], F32))
        tmp = e_(nc.sbuf_tensor("tmp", [128, 256], F32))
        ps = [e_(nc.psum_tensor(f"pb{i}", [128, 512], F32)) for i in range(8)]

        p = Prog()
        bufs = {n: p.buf(n) for n in ["rkh0", "rkh1", "g0", "g1", "u0", "u1", "sh0", "sh1", "x0", "x1",
                                      "rwb", "rwrow", "tri", "sel", "msks", "identb", "identf", "ones",
                                      "csb", "clb", "ekt", "kht", "cf", "rf", "kf", "rtf", "ektf", "ktf",
                                      "Am", "AmT", "S", "Sbf", "yt0", "yt1", "tmp"]}
        ps_bufs = [p.buf(f"ps{i}") for i in range(8)]

        for nm, sb, dd in (("tri", tri, tri_d), ("sel", sel, sel_d), ("identb", identb, identb_d),
                           ("identf", identf, identf_d), ("ones", ones_s, ones_d)):
            p.step("sync", lambda sy, sb=sb, dd=dd: sy.dma_start(out=sb[:], in_=dd), writes=[bufs[nm]], dma=True, tag=f"ld{nm}")
        p.step("sync", lambda sy: sy.dma_start(out=msks[:], in_=msk_d.rearrange("i p f -> (i p) f")), writes=[bufs["msks"]], dma=True, tag="ldmsk")
        p.step("sync", lambda sy: sy.dma_start(out=rwrow[:], in_=rw_d), writes=[bufs["rwrow"]], dma=True, tag="ldrw")
        emit_bcast_row(p, ones_s[:], bufs["ones"], rwrow, bufs["rwrow"], rwb, bufs["rwb"], ps, ps_bufs, [0], 256)

        for k in range(128):
            kb = k % 2
            if k % 64 == 0:
                p.step("vector", lambda v: v.memset(Ssb[:], 0.0), reads=[], writes=[bufs["S"]], tag=f"sz{k}")
            p.step("sync", lambda sy, k=k, kb=kb: sy.dma_start(out=rkh[:, kb, :], in_=rkh_d[k]), writes=[bufs[f"rkh{kb}"]], dma=True, tag=f"ldrk{k}")
            p.step("sync", lambda sy, k=k, kb=kb: sy.dma_start(out=gts[:, kb, :], in_=g_d[k]), writes=[bufs[f"g{kb}"]], dma=True, tag=f"ldg{k}")
            p.step("sync", lambda sy, k=k, kb=kb: sy.dma_start(out=ut[:, kb, :], in_=u_d[k]), writes=[bufs[f"u{kb}"]], dma=True, tag=f"ldu{k}")
            p.step("sync", lambda sy, k=k, kb=kb: sy.dma_start(out=sht[:, kb, :], in_=sh_d[k]), writes=[bufs[f"sh{kb}"]], dma=True, tag=f"ldsh{k}")
            p.step("gpsimd", lambda g, k=k, kb=kb: g.dma_start(out=xts[:, kb, :], in_=x_d[k]), writes=[bufs[f"x{kb}"]], dma=True, tag=f"ldx{k}")

            def t_cum(t, kb=kb):
                t.matmul(out=ps[4][:, 0:GM], lhsT=tri[:], rhs=gts[:, kb, :], start=True, stop=True)
                return t.matmul(out=ps[5][0:1, 0:GM], lhsT=sel[:], rhs=ps_dummy_rhs(gts, kb), start=True, stop=True)
            p.step("tensor", lambda t, kb=kb: t.matmul(out=ps[4][:, 0:GM], lhsT=tri[:], rhs=gts[:, kb, :], start=True, stop=True),
                   reads=[bufs["tri"], bufs[f"g{kb}"]], writes=[ps_bufs[4]], tag=f"cum{k}")
            p.step("vector", lambda v: v.tensor_copy(out=csb[:], in_=ps[4][:, 0:GM]),
                   reads=[ps_bufs[4]], writes=[bufs["csb"]], tag=f"cc{k}")
            p.step("tensor", lambda t: t.matmul(out=ps[5][0:1, 0:GM], lhsT=sel[:], rhs=csb[:], start=True, stop=True),
                   reads=[bufs["sel"], bufs["csb"]], writes=[ps_bufs[5]], tag=f"cl{k}")
            p.step("vector", lambda v: v.tensor_copy(out=clb[0:1, :], in_=ps[5][0:1, 0:GM]),
                   reads=[ps_bufs[5]], writes=[bufs["clb"]], tag=f"clc{k}")
            p.step("tensor", lambda t: t.matmul(out=ps[5][:, 0:GM], lhsT=ones_s[:], rhs=clb[0:1, :], start=True, stop=True),
                   reads=[bufs["ones"], bufs["clb"]], writes=[ps_bufs[5]], tag=f"clb{k}")
            p.step("vector", lambda v: v.tensor_tensor(out=ekt[:], in0=ps[5][:, 0:GM], in1=csb[:], op=ALU.subtract),
                   reads=[ps_bufs[5], bufs["csb"]], writes=[bufs["ekt"]], tag=f"ek0{k}")
            p.step("scalar", lambda s: s.activation(out=ekt[:], in_=ekt[:], func=AF.Exp),
                   reads=[bufs["ekt"]], writes=[bufs["ekt"]], tag=f"ek1{k}")
            p.step("vector", lambda v, kb=kb: v.tensor_tensor(out=kht[:], in0=rkh[:, kb, GM:128], in1=ekt[:], op=ALU.mult),
                   reads=[bufs[f"rkh{kb}"], bufs["ekt"]], writes=[bufs["kht"]], tag=f"ek2{k}")
            p.step("tensor", lambda t: t.transpose(out=ps[5][0:GM, 0:128], in_=csb[:], identity=identf[:]),
                   reads=[bufs["csb"], bufs["identf"]], writes=[ps_bufs[5]], tag=f"ctr{k}")
            p.step("vector", lambda v: v.tensor_copy(out=cf[:], in_=ps[5][0:GM, 0:128]),
                   reads=[ps_bufs[5]], writes=[bufs["cf"]], tag=f"cfc{k}")
            p.step("tensor", lambda t, kb=kb: t.transpose(out=ps[6][:].bitcast(BF16)[0:GM, 0:128], in_=rkh[:, kb, 0:GM], identity=identb[:]),
                   reads=[bufs[f"rkh{kb}"], bufs["identb"]], writes=[ps_bufs[6]], tag=f"rtr{k}")
            p.step("vector", lambda v: v.tensor_copy(out=rf[:], in_=ps[6][:].bitcast(BF16)[0:GM, 0:128]),
                   reads=[ps_bufs[6]], writes=[bufs["rf"]], tag=f"rfc{k}")
            p.step("tensor", lambda t, kb=kb: t.transpose(out=ps[6][:].bitcast(BF16)[0:GM, 0:128], in_=rkh[:, kb, GM:128], identity=identb[:]),
                   reads=[bufs[f"rkh{kb}"], bufs["identb"]], writes=[ps_bufs[6]], tag=f"ktr{k}")
            p.step("vector", lambda v: v.tensor_copy(out=kf[:], in_=ps[6][:].bitcast(BF16)[0:GM, 0:128]),
                   reads=[ps_bufs[6]], writes=[bufs["kf"]], tag=f"kfc{k}")
            p.step("scalar", lambda s: s.activation(out=ektf[:], in_=cf[:], func=AF.Exp),
                   reads=[bufs["cf"]], writes=[bufs["ektf"]], tag=f"erf{k}")
            p.step("vector", lambda v: v.tensor_tensor(out=rtf[:], in0=rf[:], in1=ektf[:], op=ALU.mult),
                   reads=[bufs["rf"], bufs["ektf"]], writes=[bufs["rtf"]], tag=f"rt{k}")
            p.step("vector", lambda v: v.tensor_copy(out=Sbf[:], in_=Ssb[:]),
                   reads=[bufs["S"]], writes=[bufs["Sbf"]], tag=f"sb{k}")
            bank_o = kb
            for I in range(4):
                p.step("vector", lambda v, I=I: v.tensor_scalar(out=ektf[:], in0=cf[:], scalar1=cf[:, 32 * I:32 * I + 1], scalar2=-1.0, op0=ALU.subtract, op1=ALU.mult),
                       reads=[bufs["cf"]], writes=[bufs["ektf"]], tag=f"ki0{k}_{I}")
                p.step("vector", lambda v: v.tensor_scalar_min(out=ektf[:], in0=ektf[:], scalar1=75.0),
                       reads=[bufs["ektf"]], writes=[bufs["ektf"]], tag=f"kic{k}_{I}")
                p.step("scalar", lambda s: s.activation(out=ektf[:], in_=ektf[:], func=AF.Exp),
                       reads=[bufs["ektf"]], writes=[bufs["ektf"]], tag=f"ki1{k}_{I}")
                p.step("vector", lambda v: v.tensor_tensor(out=ktf[:], in0=kf[:], in1=ektf[:], op=ALU.mult),
                       reads=[bufs["kf"], bufs["ektf"]], writes=[bufs["ktf"]], tag=f"ki2{k}_{I}")
                p.step("tensor", lambda t, I=I: t.matmul(out=ps[2][0:32, 0:128], lhsT=rtf[:, 32 * I:32 * I + 32], rhs=ktf[:], start=True, stop=True),
                       reads=[bufs["rtf"], bufs["ktf"]], writes=[ps_bufs[2]], tag=f"am{k}_{I}")
                p.step("vector", lambda v, I=I: v.tensor_tensor(out=Am[:], in0=ps[2][0:32, 0:128], in1=msks[32 * I:32 * I + 32, :], op=ALU.mult),
                       reads=[ps_bufs[2], bufs["msks"]], writes=[bufs["Am"]], tag=f"amm{k}_{I}")
                p.step("tensor", lambda t: t.transpose(out=ps[3][:].bitcast(BF16)[:, 0:32], in_=Am[:], identity=identb[0:32, 0:32]),
                       reads=[bufs["Am"], bufs["identb"]], writes=[ps_bufs[3]], tag=f"amt{k}_{I}")
                p.step("vector", lambda v, I=I: v.tensor_copy(out=AmT[:, 32 * I:32 * I + 32], in_=ps[3][:].bitcast(BF16)[:, 0:32]),
                       reads=[ps_bufs[3]], writes=[bufs["AmT"]], tag=f"atc{k}_{I}")
            def t_oi(t, kb=kb, bank_o=bank_o):
                t.matmul(out=ps[bank_o][:, 0:256], lhsT=AmT[:], rhs=ut[:, kb, :], start=True, stop=False)
                return t.matmul(out=ps[bank_o][:, 0:256], lhsT=rtf[:], rhs=Sbf[:], start=False, stop=True)
            p.step("tensor", t_oi, reads=[bufs["AmT"], bufs[f"u{kb}"], bufs["rtf"], bufs["Sbf"]], writes=[ps_bufs[bank_o]], tag=f"oi{k}")
            def t_ds(t, kb=kb):
                return t.matmul(out=ps[6][0:GM, 0:256], lhsT=kht[:], rhs=ut[:, kb, :], start=True, stop=True)
            p.step("tensor", t_ds, reads=[bufs["kht"], bufs[f"u{kb}"]], writes=[ps_bufs[6]], tag=f"ds{k}")
            p.step("scalar", lambda s: s.activation(out=ektf[:, 0:1], in_=cf[:, 127:128], func=AF.Exp),
                   reads=[bufs["cf"]], writes=[bufs["ektf"]], tag=f"dk{k}")
            p.step("vector", lambda v: v.scalar_tensor_tensor(out=Ssb[:], in0=Ssb[:], scalar=ektf[0:GM, 0:1], in1=ps[6][0:GM, 0:256], op0=ALU.mult, op1=ALU.add),
                   reads=[ps_bufs[6], bufs["ektf"], bufs["S"]], writes=[bufs["S"]], tag=f"su{k}")
            p.step("vector", lambda v, kb=kb, bank_o=bank_o: v.scalar_tensor_tensor(
                out=yt[:, kb, :], in0=ps[bank_o][:, 0:256], scalar=GSC, in1=xts[:, kb, :], op0=ALU.mult, op1=ALU.add),
                reads=[ps_bufs[bank_o], bufs[f"x{kb}"]], writes=[bufs[f"yt{kb}"]], tag=f"ye0{k}")
            p.step("vector", lambda v, kb=kb: v.tensor_tensor(out=tmp[:], in0=sht[:, kb, :], in1=rwb[:], op=ALU.mult),
                   reads=[bufs[f"sh{kb}"], bufs["rwb"]], writes=[bufs["tmp"]], tag=f"ye1{k}")
            p.step("vector", lambda v, kb=kb: v.tensor_tensor(out=yt[:, kb, :], in0=yt[:, kb, :], in1=tmp[:], op=ALU.add),
                   reads=[bufs["tmp"], bufs[f"yt{kb}"]], writes=[bufs[f"yt{kb}"]], tag=f"ye2{k}")
            p.step("sync", lambda sy, k=k, kb=kb: sy.dma_start(out=y_o[k], in_=yt[:, kb, :]),
                   reads=[bufs[f"yt{kb}"]], writes=[], dma=True, tag=f"sy{k}")

        with nc.Block() as block:
            p.emit(nc, block)
            p.finish(nc, block)
    return nc

def host_gla_weights(conv_w, conv_b, in_proj_w, ch_gate_w0, ch_gate_w1, ch_gate_b1, mem_gate_w, mem_gate_b, residual_weight):
    inp = f32(np.asarray(in_proj_w).T.reshape(NDC, 128, 128))
    mgw = f32(np.asarray(mem_gate_w).T.reshape(NDC, 128, 128))
    w0 = f32(np.asarray(ch_gate_w0).T.reshape(NDC, 128, GR))
    w1 = f32(np.asarray(ch_gate_w1).T)
    b1 = f32(np.asarray(ch_gate_b1).reshape(1, D))
    mgb = f32(np.asarray(mem_gate_b).reshape(1, 128))
    cw = f32(np.asarray(conv_w).reshape(NDC, 128, 4))
    cb = f32(np.asarray(conv_b).reshape(NDC, 128, 1))
    tri = f32(np.triu(np.ones((128, 128), np.float32)))
    sel127 = np.zeros((128, 1), np.float32)
    sel127[127, 0] = 1.0
    msks = np.zeros((4, 32, 128), np.float32)
    for I in range(4):
        for i in range(32):
            msks[I, i, :32 * I + i + 1] = 1.0
    return dict(inp=inp, mgw=mgw, w0=w0, w1=w1, b1=b1, mgb=mgb, cwg=cw, cbg=cb,
                ones=f32(np.ones((1, 128))), tri=tri, sel127=f32(sel127), msks=f32(msks),
                identb=to_bf16(np.eye(128)), identf=f32(np.eye(128)),
                rw=f32(np.asarray(residual_weight).reshape(1, D)))


def run_gla(x_full, W, nc_cache={}):
    if "gla_a" not in nc_cache:
        nc_cache["gla_a"] = build_gla_a()
        nc_cache["gla_b"] = build_gla_b()
    xr = np.ascontiguousarray(x_full.reshape(NCORES, NTT, 128, D))
    in_maps = []
    for c in range(NCORES):
        if c % 4 == 0:
            xh = np.zeros((128, D), np.float32)
        else:
            xh = xr[c - 1][-1]
        in_maps.append(dict(x=f32(xr[c]), xh=f32(xh), inp=W["inp"], mgw=W["mgw"], w0=W["w0"],
                            w1=W["w1"], b1=W["b1"], mgb=W["mgb"], cwg=W["cwg"], cbg=W["cbg"], ones=W["ones"]))
    res = run_bass_kernel_spmd(nc_cache["gla_a"], in_maps, core_ids=list(range(NCORES)))
    rkh = np.stack([r["rkh"] for r in res.results]).reshape(B * T // 128, 128, 128)
    g = np.stack([r["g"] for r in res.results]).reshape(B * T // 128, 128, GM)
    u = np.stack([r["u"] for r in res.results]).reshape(B * T, D)
    sh = np.stack([r["sh"] for r in res.results]).reshape(B * T, D)

    in_maps_b = []
    xflat = x_full.reshape(B * T, D)
    for c in range(NCORES):
        sl = slice(c * 256, (c + 1) * 256)
        in_maps_b.append(dict(rkh=np.ascontiguousarray(rkh), g=f32(g),
                              u=np.ascontiguousarray(u[:, sl].reshape(128, 128, 256)),
                              sh=np.ascontiguousarray(sh[:, sl].reshape(128, 128, 256)),
                              x=f32(xflat[:, sl].reshape(128, 128, 256)),
                              rw=f32(W["rw"][:, sl]), tri=W["tri"], sel127=W["sel127"], msks=W["msks"],
                              identb=W["identb"], identf=W["identf"], ones=W["ones"]))
    res_b = run_bass_kernel_spmd(nc_cache["gla_b"], in_maps_b, core_ids=list(range(NCORES)))
    y = np.concatenate([r["y"].reshape(B * T, 256) for r in res_b.results], axis=1)
    return y.reshape(B, T, D)


_KCACHE = {}


def kernel(L0_attn_a_hidden_states, L0_attn_a_conv_w, L0_attn_a_conv_b, L0_attn_a_in_proj_w,
           L0_attn_a_ch_gate_w0, L0_attn_a_ch_gate_w1, L0_attn_a_ch_gate_b1, L0_attn_a_mem_gate_w,
           L0_attn_a_mem_gate_b, L0_attn_a_residual_weight, L0_attn_b_norm_scale, L0_attn_b_in_proj_w,
           L0_attn_b_conv_weight, L0_attn_b_conv_bias, L0_attn_b_A_log, L0_attn_b_dt_bias, L0_attn_b_D,
           L0_attn_b_out_proj_w, L0_ffn_wq_gate, L0_ffn_s_gate, L0_ffn_wq_up, L0_ffn_s_up,
           L0_ffn_wq_down, L0_ffn_s_down, L1_attn_a_Wqa, L1_attn_a_g_qa, L1_attn_a_Wqb, L1_attn_a_Wkva,
           L1_attn_a_g_kva, L1_attn_a_Wkvb, L1_attn_a_Wo, L1_attn_b_wq, L1_attn_b_wk, L1_attn_b_wv,
           L1_attn_b_wo, L1_attn_b_query_scale, L1_attn_b_key_scale, L1_ffn_gate_w, L1_ffn_w_gate,
           L1_ffn_w_up, L1_ffn_w_down, L1_ffn_sw_gate, L1_ffn_sw_up, L1_ffn_sw_down):
    W_gla = host_gla_weights(L0_attn_a_conv_w, L0_attn_a_conv_b, L0_attn_a_in_proj_w,
                             L0_attn_a_ch_gate_w0, L0_attn_a_ch_gate_w1, L0_attn_a_ch_gate_b1,
                             L0_attn_a_mem_gate_w, L0_attn_a_mem_gate_b, L0_attn_a_residual_weight)
    W_ffn = host_ffn_weights(L0_ffn_wq_gate, L0_ffn_s_gate, L0_ffn_wq_up, L0_ffn_s_up,
                             L0_ffn_wq_down, L0_ffn_s_down)
    W_mam = host_mamba_weights(L0_attn_b_norm_scale, L0_attn_b_in_proj_w, L0_attn_b_conv_weight,
                               L0_attn_b_conv_bias, L0_attn_b_A_log, L0_attn_b_dt_bias, L0_attn_b_D,
                               L0_attn_b_out_proj_w)
    W_mla = host_mla_weights(L1_attn_a_Wqa, L1_attn_a_g_qa, L1_attn_a_Wqb, L1_attn_a_Wkva,
                             L1_attn_a_g_kva, L1_attn_a_Wkvb, L1_attn_a_Wo)
    W_moe = host_moe_weights(L1_ffn_gate_w, L1_ffn_w_gate, L1_ffn_w_up, L1_ffn_w_down,
                             L1_ffn_sw_gate, L1_ffn_sw_up, L1_ffn_sw_down)
    W_swa = host_swa_weights(L1_attn_b_wq, L1_attn_b_wk, L1_attn_b_wv, L1_attn_b_wo,
                             L1_attn_b_query_scale, L1_attn_b_key_scale)

    x = f32(L0_attn_a_hidden_states)
    x = run_gla(x, W_gla, _KCACHE)
    x = run_ffn_quant(x, W_ffn, _KCACHE)
    x = run_mamba(x, W_mam, _KCACHE)
    x = run_ffn_quant(x, W_ffn, _KCACHE)
    x = run_mla(x, W_mla, _KCACHE)
    x = run_moe(x, W_moe, _KCACHE)
    x = run_swa(x, W_swa, _KCACHE)
    x = run_moe(x, W_moe, _KCACHE)
    return f32(x.reshape(B, T, D))


# version 03
# speedup vs baseline: 1.4152x; 1.4152x over previous
import numpy as np
import ml_dtypes
import concourse.bass as bass
import concourse.mybir as mybir
from concourse.bass_utils import run_bass_kernel_spmd

F32 = mybir.dt.float32
BF16 = mybir.dt.bfloat16
I32 = mybir.dt.int32
AF = mybir.ActivationFunctionType
ALU = mybir.AluOpType
AX = mybir.AxisListType

BF = ml_dtypes.bfloat16

B, T, D = 2, 8192, 2048
NCORES = 8
TLOC = (B * T) // NCORES
NTT = TLOC // 128
NDC = D // 128
LN_EPS = 1e-5


def f32(x):
    return np.ascontiguousarray(np.asarray(x), dtype=np.float32)


def to_bf16(x):
    return np.ascontiguousarray(np.asarray(x).astype(BF))


def i32np(x):
    return np.ascontiguousarray(np.asarray(x), dtype=np.int32)



class Buf:
    __slots__ = ("name", "last_w", "readers")

    def __init__(self, name):
        self.name = name
        self.last_w = None
        self.readers = []

    def __repr__(self):
        return f"Buf({self.name})"


class Step:
    __slots__ = ("sid", "eng", "fn", "deps", "dma", "tag", "sig", "sigcount")

    def __init__(self, sid, eng, fn, deps, dma, tag):
        self.sid = sid
        self.eng = eng
        self.fn = fn
        self.deps = deps
        self.dma = dma
        self.tag = tag
        self.sig = False
        self.sigcount = 0


class Prog:
    ENGS = ("tensor", "vector", "scalar", "gpsimd", "sync")

    def __init__(self):
        self.steps = []
        self.bufs = []

    def buf(self, name):
        b = Buf(name)
        self.bufs.append(b)
        return b

    def step(self, eng, fn, reads=(), writes=(), dma=False, tag=""):
        sid = len(self.steps)
        deps = set()
        for r in reads:
            if r.last_w is not None:
                deps.add(r.last_w)
        for w in writes:
            if w.last_w is not None:
                deps.add(w.last_w)
            deps.update(w.readers)
        for r in reads:
            r.readers.append(sid)
        for w in writes:
            w.last_w = sid
            w.readers = []
        deps.discard(sid)
        self.steps.append(Step(sid, eng, fn, deps, dma, tag))
        return sid

    @staticmethod
    def _ekey(s):
        return s.eng + ":dma" if s.dma else s.eng

    DMA_POOL = 6

    def emit(self, nc, block):
        steps = self.steps
        for s in steps:
            for d in s.deps:
                steps[d].sig = True
        for s in steps:
            if s.dma:
                s.sig = True
        counts = {}
        dma_use = {}
        queue_rr = {}
        sigid = {}
        for s in steps:
            ek = self._ekey(s)
            if not s.sig:
                continue
            if s.dma:
                r = queue_rr.get(ek, 0)
                queue_rr[ek] = (r + 1) % self.DMA_POOL
                semkey = (ek, r)
                j = dma_use.get(semkey, 0)
                dma_use[semkey] = j + 1
                sigid[s.sid] = ("d", semkey, 16 * (j + 1), 16 * j)
            else:
                c = counts.get(ek, 0) + 1
                counts[ek] = c
                sigid[s.sid] = ("c", ek, c, None)
        chans = {}
        for ek in counts:
            chans[ek] = nc.alloc_semaphore(name=f"ch_{ek.replace(':', '_')}")
        for semkey in dma_use:
            ek, r = semkey
            chans[semkey] = nc.alloc_semaphore(name=f"dq_{ek.split(':')[0]}_{r}")
        know = {}
        snap = {}
        waits = {}
        for s in steps:
            ek = self._ekey(s)
            k = know.setdefault(ek, {})
            wl = []
            for d in sorted(s.deps):
                if d not in sigid:
                    dk = self._ekey(steps[d])
                    assert dk == ek, f"non-signaling cross-engine dep {d}->{s.sid}"
                    continue
                kind, key, val, _g = sigid[d]
                if k.get(key, -1) >= val:
                    continue
                wl.append((key, val))
                k[key] = val
                for fk, fv in snap[d].items():
                    if k.get(fk, -1) < fv:
                        k[fk] = fv
            waits[s.sid] = wl
            sn = dict(k)
            if s.sid in sigid:
                kind, key, val, _g = sigid[s.sid]
                if kind == "c":
                    sn[key] = val
            snap[s.sid] = sn
        by_eng = {}
        for s in steps:
            by_eng.setdefault(s.eng, []).append(s)
        self._chans = chans
        self._sigid = sigid
        for eng_name in self.ENGS:
            sl = by_eng.get(eng_name)
            if not sl:
                continue

            def make_body(sl):
                def body(eng):
                    for s in sl:
                        for (key, val) in waits[s.sid]:
                            eng.wait_ge(chans[key], val)
                        if s.sid in sigid and sigid[s.sid][0] == "d":
                            _, semkey, val, gate = sigid[s.sid]
                            if gate > 0:
                                eng.wait_ge(chans[semkey], gate)
                        inst = s.fn(eng)
                        if s.sig:
                            assert inst is not None, f"step {s.tag or s.sid} must return its last instruction"
                            kind, key, val, _g = sigid[s.sid]
                            inst.then_inc(chans[key], 16 if s.dma else 1)
                return body
            getattr(block, eng_name)(make_body(sl))
        fc = dict(counts)
        for semkey, j in dma_use.items():
            fc[semkey] = 16 * j
        self.final_counts = fc

    def finish(self, nc, block):
        chans = self._chans
        counts = self.final_counts

        def body(eng):
            for ek, c in counts.items():
                eng.wait_ge(chans[ek], c)
        block.sync(body)



def emit_ln_tile(p, x_ap, x_buf, xln_ap, xln_buf, scr, scr_buf, stats, st_buf):
    Dd = x_ap.shape[-1]
    p.step("scalar", lambda s: s.activation(out=scr, in_=x_ap, func=AF.Square, accum_out=stats[:, 1:2]),
           reads=[x_buf], writes=[scr_buf, st_buf], tag="ln_sq")
    p.step("vector", lambda v: v.reduce_sum(out=stats[:, 0:1], in_=x_ap, axis=AX.X),
           reads=[x_buf], writes=[st_buf], tag="ln_sum")
    p.step("vector", lambda v: v.tensor_scalar_mul(out=stats[:, 0:1], in0=stats[:, 0:1], scalar1=1.0 / Dd),
           reads=[st_buf], writes=[st_buf], tag="ln_m")
    p.step("vector", lambda v: v.scalar_tensor_tensor(out=stats[:, 4:5], in0=stats[:, 0:1], scalar=-1.0, in1=stats[:, 0:1], op0=ALU.mult, op1=ALU.mult),
           reads=[st_buf], writes=[st_buf], tag="ln_msq")
    p.step("vector", lambda v: v.scalar_tensor_tensor(out=stats[:, 2:3], in0=stats[:, 1:2], scalar=1.0 / Dd, in1=stats[:, 4:5], op0=ALU.mult, op1=ALU.add),
           reads=[st_buf], writes=[st_buf], tag="ln_var")
    p.step("vector", lambda v: v.tensor_scalar_add(out=stats[:, 2:3], in0=stats[:, 2:3], scalar1=LN_EPS),
           reads=[st_buf], writes=[st_buf], tag="ln_eps")
    p.step("scalar", lambda s: s.activation(out=stats[:, 3:4], in_=stats[:, 2:3], func=AF.Sqrt),
           reads=[st_buf], writes=[st_buf], tag="ln_sqrt")
    p.step("vector", lambda v: v.reciprocal(out=stats[:, 3:4], in_=stats[:, 3:4]),
           reads=[st_buf], writes=[st_buf], tag="ln_recip")
    p.step("vector", lambda v: v.tensor_scalar(out=xln_ap, in0=x_ap, scalar1=stats[:, 0:1], scalar2=stats[:, 3:4], op0=ALU.subtract, op1=ALU.mult),
           reads=[x_buf, st_buf], writes=[xln_buf], tag="ln_norm")


def emit_ln_phase(p, x_d, x_sb, x_bufs, xln_sb, xln_bufs, xf_sb, xf_buf, scr, scr_bufs, stats, st_bufs,
                  tiles, tb_off=0):
    NB = len(x_bufs)
    for i, gt in enumerate(tiles):
        bi = i % NB
        p.step("sync", lambda sy, gt=gt, bi=bi: sy.dma_start(out=x_sb[:, bi, :], in_=x_d[gt]),
               reads=[], writes=[x_bufs[bi]], dma=True, tag=f"ldx{gt}")
        emit_ln_tile(p, x_sb[:, bi, :], x_bufs[bi], xln_sb[:, bi, :], xln_bufs[bi],
                     scr[:, bi, :], scr_bufs[bi], stats[:, bi, :], st_bufs[bi])
        p.step("sync", lambda sy, i=i, bi=bi: sy.dma_start(
            out=xf_sb[:, :, (tb_off + i) * 128:(tb_off + i + 1) * 128],
            in_=xln_sb[:, bi, :].rearrange("t (c d) -> t c d", c=NDC), transpose=True),
            reads=[xln_bufs[bi]], writes=[xf_buf], dma=True, tag=f"trx{gt}")


def emit_bcast_row(p, ones_s, ones_buf, row_ap, row_buf, out_sb, out_buf, ps, ps_bufs, banks, width, vdtypecopy=True):
    nch = (width + 511) // 512
    def t_mm(t):
        last = None
        for j in range(nch):
            w = min(512, width - j * 512)
            last = t.matmul(out=ps[banks[j]][:, :w], lhsT=ones_s[:], rhs=row_ap[0:1, j * 512:j * 512 + w], start=True, stop=True)
        return last
    p.step("tensor", t_mm, reads=[ones_buf, row_buf], writes=[ps_bufs[b] for b in banks[:nch]], tag="bcast_mm")

    def v_cp(v):
        last = None
        for j in range(nch):
            w = min(512, width - j * 512)
            last = v.tensor_copy(out=out_sb[:, j * 512:j * 512 + w], in_=ps[banks[j]][:, :w])
        return last
    p.step("vector", v_cp, reads=[ps_bufs[b] for b in banks[:nch]], writes=[out_buf], tag="bcast_cp")



HID = 5632
NOF = HID // 128


def build_ffn_quant():
    nc = bass.Bass("TRN2", target_bir_lowering=False)
    x_d = nc.dram_tensor("x", [NTT, 128, D], F32, kind="ExternalInput").ap()
    wg_d = nc.dram_tensor("wg", [NOF, 128, NDC * 128], I32, kind="ExternalInput").ap()
    wu_d = nc.dram_tensor("wu", [NOF, 128, NDC * 128], I32, kind="ExternalInput").ap()
    wd_d = nc.dram_tensor("wd", [NOF, 128, D], I32, kind="ExternalInput").ap()
    sgu_d = nc.dram_tensor("sgu", [NOF, 128, 2], F32, kind="ExternalInput").ap()
    sd_d = nc.dram_tensor("sd", [1, D], F32, kind="ExternalInput").ap()
    ones_d = nc.dram_tensor("ones", [1, 128], F32, kind="ExternalInput").ap()
    y_d = nc.dram_tensor("y", [NTT, 128, D], F32, kind="ExternalOutput").ap()

    TB = 512
    NTB = TLOC // TB
    TTB = TB // 128

    from contextlib import ExitStack
    with ExitStack() as st_:
        e = st_.enter_context
        xt = e(nc.sbuf_tensor("xt", [128, 2, D], F32))
        xln = e(nc.sbuf_tensor("xln", [128, 2, D], BF16))
        xf = e(nc.sbuf_tensor("xf", [128, NDC, TB], BF16))
        scr = e(nc.sbuf_tensor("scr", [128, 2, D], BF16))
        stats = e(nc.sbuf_tensor("stats", [128, 2, 8], F32))
        w32 = e(nc.sbuf_tensor("w32", [128, 2, NDC * 128], F32))
        wb = e(nc.sbuf_tensor("wb", [128, 2, 2, NDC * 128], BF16))
        sgu = e(nc.sbuf_tensor("sgu_s", [128, 2, 2], F32))
        hbuf = e(nc.sbuf_tensor("hbuf", [128, NOF, TB], BF16))
        silu_t = e(nc.sbuf_tensor("silu_t", [128, 512], BF16))
        wd32 = e(nc.sbuf_tensor("wd32", [128, 2, 512], F32))
        wdb = e(nc.sbuf_tensor("wdb", [128, NOF, 512], BF16))
        sdb = e(nc.sbuf_tensor("sdb", [128, D], F32))
        ones_s = e(nc.sbuf_tensor("ones_s", [1, 128], F32))
        sdrow = e(nc.sbuf_tensor("sdrow", [1, D], F32))
        xres = e(nc.sbuf_tensor("xres", [128, 2, 512], F32))
        ytile = e(nc.sbuf_tensor("ytile", [128, 2, 512], F32))
        ps = [e(nc.psum_tensor(f"pb{i}", [128, 512], F32)) for i in range(8)]

        p = Prog()
        x_bufs = [p.buf(f"xt{i}") for i in range(2)]
        xln_bufs = [p.buf(f"xln{i}") for i in range(2)]
        xf_buf = p.buf("xf")
        scr_bufs = [p.buf(f"scr{i}") for i in range(2)]
        st_bufs = [p.buf(f"stats{i}") for i in range(2)]
        w32_bufs = [p.buf(f"w32_{g}") for g in range(2)]
        wb_bufs = [[p.buf(f"wb_{g}_{i}") for i in range(2)] for g in range(2)]
        sgu_bufs = [p.buf(f"sgu{i}") for i in range(2)]
        h_buf = p.buf("hbuf")
        silu_buf = p.buf("silu")
        wd32_bufs = [p.buf(f"wd32_{i}") for i in range(2)]
        wdb_buf = p.buf("wdb")
        sdb_buf = p.buf("sdb")
        ones_buf = p.buf("ones")
        sdrow_buf = p.buf("sdrow")
        xres_bufs = [p.buf(f"xres{i}") for i in range(2)]
        yt_bufs = [p.buf(f"yt{i}") for i in range(2)]
        ps_bufs = [p.buf(f"ps{i}") for i in range(8)]

        p.step("sync", lambda sy: sy.dma_start(out=ones_s[:], in_=ones_d), writes=[ones_buf], dma=True, tag="ld_ones")
        p.step("sync", lambda sy: sy.dma_start(out=sdrow[:], in_=sd_d), writes=[sdrow_buf], dma=True, tag="ld_sd")
        emit_bcast_row(p, ones_s[:], ones_buf, sdrow, sdrow_buf, sdb, sdb_buf, ps, ps_bufs, [0, 1, 2, 3], D)

        for tb in range(NTB):
            tiles = list(range(tb * TTB, (tb + 1) * TTB))
            emit_ln_phase(p, x_d, xt, x_bufs, xln, xln_bufs, xf, xf_buf, scr, scr_bufs, stats, st_bufs, tiles)

            for of in range(NOF):
                bi = of % 2
                for g in range(2):
                    w_d = wg_d if g == 0 else wu_d
                    p.step("gpsimd", lambda gp, w_d=w_d, g=g, of=of: gp.dma_start(
                        out=w32[:, g, :], in_=w_d[of]), reads=[], writes=[w32_bufs[g]], dma=True, tag=f"ldw{g}_{of}")
                    p.step("vector", lambda v, g=g, bi=bi: v.tensor_scalar_add(
                        out=wb[:, g, bi, :], in0=w32[:, g, :], scalar1=-128.0),
                        reads=[w32_bufs[g]], writes=[wb_bufs[g][bi]], tag=f"dq{g}_{of}")
                p.step("sync", lambda sy, of=of, bi=bi: sy.dma_start(out=sgu[:, bi, :], in_=sgu_d[of]),
                       reads=[], writes=[sgu_bufs[bi]], dma=True, tag=f"ldsgu{of}")
                bank_g = (of % 2) * 2
                bank_u = bank_g + 1

                def t_gu(t, bi=bi, bank_g=bank_g, bank_u=bank_u):
                    last = None
                    for c in range(NDC):
                        t.matmul(out=ps[bank_g][:], lhsT=wb[:, 0, bi, c * 128:(c + 1) * 128],
                                 rhs=xf[:, c, :], start=(c == 0), stop=(c == NDC - 1))
                    for c in range(NDC):
                        last = t.matmul(out=ps[bank_u][:], lhsT=wb[:, 1, bi, c * 128:(c + 1) * 128],
                                        rhs=xf[:, c, :], start=(c == 0), stop=(c == NDC - 1))
                    return last
                p.step("tensor", t_gu, reads=[wb_bufs[0][bi], wb_bufs[1][bi], xf_buf],
                       writes=[ps_bufs[bank_g], ps_bufs[bank_u]], tag=f"mm_gu{of}")

                p.step("scalar", lambda s, bi=bi, bank_g=bank_g: s.activation(
                    out=silu_t[:], in_=ps[bank_g][:], func=AF.Silu, scale=sgu[:, bi, 0:1]),
                    reads=[ps_bufs[bank_g], sgu_bufs[bi]], writes=[silu_buf], tag=f"silu{of}")

                p.step("vector", lambda v, of=of, bi=bi, bank_u=bank_u: v.scalar_tensor_tensor(
                    out=hbuf[:, of, :], in0=ps[bank_u][:], scalar=sgu[:, bi, 1:2],
                    in1=silu_t[:], op0=ALU.mult, op1=ALU.mult),
                    reads=[ps_bufs[bank_u], silu_buf, sgu_bufs[bi]], writes=[h_buf], tag=f"hmul{of}")

            for dc in range(4):
                for of in range(NOF):
                    bi = of % 2
                    p.step("gpsimd", lambda gp, of=of, dc=dc, bi=bi: gp.dma_start(
                        out=wd32[:, bi, :], in_=wd_d[of][:, dc * 512:(dc + 1) * 512]),
                        reads=[], writes=[wd32_bufs[bi]], dma=True, tag=f"ldwd{dc}_{of}")
                    p.step("vector", lambda v, of=of, bi=bi: v.tensor_scalar_add(
                        out=wdb[:, of, :], in0=wd32[:, bi, :], scalar1=-128.0),
                        reads=[wd32_bufs[bi]], writes=[wdb_buf], tag=f"dqd{dc}_{of}")
                for tt in range(TTB):
                    gt = tb * TTB + tt
                    bank = 4 + (tt % 4)
                    ri = tt % 2
                    p.step("sync", lambda sy, gt=gt, dc=dc, ri=ri: sy.dma_start(
                        out=xres[:, ri, :], in_=x_d[gt][:, dc * 512:(dc + 1) * 512]),
                        reads=[], writes=[xres_bufs[ri]], dma=True, tag=f"ldres{gt}_{dc}")

                    def t_dn(t, tt=tt, bank=bank):
                        last = None
                        for of in range(NOF):
                            last = t.matmul(out=ps[bank][:], lhsT=hbuf[:, of, tt * 128:(tt + 1) * 128],
                                            rhs=wdb[:, of, :], start=(of == 0), stop=(of == NOF - 1))
                        return last
                    p.step("tensor", t_dn, reads=[h_buf, wdb_buf], writes=[ps_bufs[bank]], tag=f"mm_dn{gt}_{dc}")

                    p.step("vector", lambda v, bank=bank, ri=ri, dc=dc: v.tensor_tensor(
                        out=ytile[:, ri, :], in0=ps[bank][:], in1=sdb[:, dc * 512:(dc + 1) * 512], op=ALU.mult),
                        reads=[ps_bufs[bank], sdb_buf], writes=[yt_bufs[ri]], tag=f"ep1_{gt}_{dc}")
                    p.step("vector", lambda v, ri=ri: v.tensor_tensor(
                        out=ytile[:, ri, :], in0=ytile[:, ri, :], in1=xres[:, ri, :], op=ALU.add),
                        reads=[xres_bufs[ri], yt_bufs[ri]], writes=[yt_bufs[ri]], tag=f"ep2_{gt}_{dc}")

                    p.step("sync", lambda sy, gt=gt, dc=dc, ri=ri: sy.dma_start(
                        out=y_d[gt][:, dc * 512:(dc + 1) * 512], in_=ytile[:, ri, :]),
                        reads=[yt_bufs[ri]], writes=[], dma=True, tag=f"sty{gt}_{dc}")

        with nc.Block() as block:
            p.emit(nc, block)
            p.finish(nc, block)
    return nc


def host_ffn_weights(wq_gate, s_gate, wq_up, s_up, wq_down, s_down):
    def lhsT_tiles(wq):
        A = np.ascontiguousarray(wq).T
        A = A.reshape(NDC, 128, NOF, 128)
        A = A.transpose(2, 1, 0, 3)
        return i32np(A.reshape(NOF, 128, NDC * 128))
    wg = lhsT_tiles(wq_gate)
    wu = lhsT_tiles(wq_up)
    Ad = np.ascontiguousarray(wq_down).T
    wd = i32np(Ad.reshape(NOF, 128, D))
    sgu = f32(np.stack([s_gate.reshape(NOF, 128), s_up.reshape(NOF, 128)], axis=-1))
    sd = f32(s_down.reshape(1, D))
    ones = f32(np.ones((1, 128)))
    return dict(wg=wg, wu=wu, wd=wd, sgu=sgu, sd=sd, ones=ones)


def run_ffn_quant(x_full, weights, nc_cache={}):
    if "ffn" not in nc_cache:
        nc_cache["ffn"] = build_ffn_quant()
    nc = nc_cache["ffn"]
    xr = np.ascontiguousarray(x_full.reshape(NCORES, NTT, 128, D))
    in_maps = []
    for c in range(NCORES):
        m = dict(weights)
        m["x"] = f32(xr[c])
        in_maps.append(m)
    res = run_bass_kernel_spmd(nc, in_maps, core_ids=list(range(NCORES)))
    y = np.stack([r["y"] for r in res.results]).reshape(B, T, D)
    return y


NE = 10
EH = 1408
NOFE = EH // 128


def build_moe():
    nc = bass.Bass("TRN2", target_bir_lowering=False)
    x_d = nc.dram_tensor("x", [NTT, 128, D], F32, kind="ExternalInput").ap()
    gw_d = nc.dram_tensor("gw", [128, NDC, 8], BF16, kind="ExternalInput").ap()
    wg_d = nc.dram_tensor("wg", [NE * NOFE, 128, NDC * 128], F32, kind="ExternalInput").ap()
    wu_d = nc.dram_tensor("wu", [NE * NOFE, 128, NDC * 128], F32, kind="ExternalInput").ap()
    wd_d = nc.dram_tensor("wd", [NE * NOFE, 128, D], F32, kind="ExternalInput").ap()
    y_d = nc.dram_tensor("y", [NTT, 128, D], F32, kind="ExternalOutput").ap()
    cdbg_d = nc.dram_tensor("cdbg", [4, 128, 4, NE], F32, kind="ExternalOutput").ap()

    TB = 512
    NTB = TLOC // TB
    TTB = TB // 128

    from contextlib import ExitStack
    with ExitStack() as st_:
        e_ = st_.enter_context
        xt = e_(nc.sbuf_tensor("xt", [128, 2, D], F32))
        xln = e_(nc.sbuf_tensor("xln", [128, 2, D], BF16))
        xf = e_(nc.sbuf_tensor("xf", [128, NDC, TB], BF16))
        scr = e_(nc.sbuf_tensor("scr", [128, 2, D], BF16))
        stats = e_(nc.sbuf_tensor("stats", [128, 2, 8], F32))
        gwt = e_(nc.sbuf_tensor("gwt", [128, NDC, 8], BF16))
        probs = e_(nc.sbuf_tensor("probs", [128, TTB, 8], F32))
        ptmp = e_(nc.sbuf_tensor("ptmp", [128, 8], F32))
        rstat = e_(nc.sbuf_tensor("rstat", [128, 4], F32))
        comb = e_(nc.sbuf_tensor("comb", [128, TTB, NE], F32))
        wgu = e_(nc.sbuf_tensor("wgu", [128, 2, 2, NDC * 128], BF16))
        hbuf = e_(nc.sbuf_tensor("hbuf", [128, NOFE, TB], BF16))
        silu_t = e_(nc.sbuf_tensor("silu_t", [128, TB], BF16))
        wde = e_(nc.sbuf_tensor("wde", [128, 2, NOFE, D], BF16))
        acc = e_(nc.sbuf_tensor("acc", [128, TTB, D], F32))
        xres = e_(nc.sbuf_tensor("xres", [128, 2, 512], F32))
        ytile = e_(nc.sbuf_tensor("ytile", [128, 2, 512], F32))
        ps = [e_(nc.psum_tensor(f"pb{i}", [128, 512], F32)) for i in range(8)]

        p = Prog()
        x_bufs = [p.buf(f"xt{i}") for i in range(2)]
        xln_bufs = [p.buf(f"xln{i}") for i in range(2)]
        xf_buf = p.buf("xf")
        scr_bufs = [p.buf(f"scr{i}") for i in range(2)]
        st_bufs = [p.buf(f"stats{i}") for i in range(2)]
        gwt_buf = p.buf("gwt")
        probs_buf = p.buf("probs")
        ptmp_buf = p.buf("ptmp")
        rstat_buf = p.buf("rstat")
        comb_buf = p.buf("comb")
        wgu_bufs = [[p.buf(f"wgu{g}_{i}") for i in range(2)] for g in range(2)]
        h_buf = p.buf("h")
        silu_buf = p.buf("silu")
        wde_bufs = [p.buf(f"wde{i}") for i in range(2)]
        acc_buf = p.buf("acc")
        xres_bufs = [p.buf(f"xres{i}") for i in range(2)]
        yt_bufs = [p.buf(f"yt{i}") for i in range(2)]
        ps_bufs = [p.buf(f"ps{i}") for i in range(8)]

        p.step("sync", lambda sy: sy.dma_start(out=gwt[:], in_=gw_d), writes=[gwt_buf], dma=True, tag="ldgw")

        for tb in range(NTB):
            tiles = list(range(tb * TTB, (tb + 1) * TTB))
            emit_ln_phase(p, x_d, xt, x_bufs, xln, xln_bufs, xf, xf_buf, scr, scr_bufs, stats, st_bufs, tiles)

            for tt in range(TTB):
                def t_rt(t, tt=tt):
                    last = None
                    for c in range(NDC):
                        last = t.matmul(out=ps[4][:, 0:8], lhsT=xf[:, c, tt * 128:(tt + 1) * 128],
                                        rhs=gwt[:, c, :], start=(c == 0), stop=(c == NDC - 1))
                    return last
                p.step("tensor", t_rt, reads=[xf_buf, gwt_buf], writes=[ps_bufs[4]], tag=f"rt_mm{tt}")
                p.step("vector", lambda v: v.reduce_max(out=rstat[:, 0:1], in_=ps[4][:, 0:8], axis=AX.X, negate=True),
                       reads=[ps_bufs[4]], writes=[rstat_buf], tag="rt_max")
                p.step("scalar", lambda s, tt=tt: s.activation(out=probs[:, tt, :], in_=ps[4][:, 0:8], func=AF.Exp,
                                                               bias=rstat[:, 0:1], accum_out=rstat[:, 1:2]),
                       reads=[ps_bufs[4], rstat_buf], writes=[probs_buf, rstat_buf], tag="rt_exp")
                p.step("vector", lambda v: v.reciprocal(out=rstat[:, 1:2], in_=rstat[:, 1:2]),
                       reads=[rstat_buf], writes=[rstat_buf], tag="rt_rec")
                p.step("vector", lambda v, tt=tt: v.tensor_scalar_mul(out=probs[:, tt, :], in0=probs[:, tt, :], scalar1=rstat[:, 1:2]),
                       reads=[rstat_buf, probs_buf], writes=[probs_buf], tag="rt_nrm")
                p.step("vector", lambda v, tt=tt: v.reduce_max(out=rstat[:, 2:3], in_=probs[:, tt, :], axis=AX.X),
                       reads=[probs_buf], writes=[rstat_buf], tag="rt_m1")
                p.step("vector", lambda v, tt=tt: v.tensor_single_scalar(out=ptmp[:], in_=probs[:, tt, :], scalar=rstat[:, 2:3], op=ALU.is_ge),
                       reads=[probs_buf, rstat_buf], writes=[ptmp_buf], tag="rt_msk1")
                p.step("vector", lambda v, tt=tt: v.scalar_tensor_tensor(out=ptmp[:], in0=ptmp[:], scalar=-1e9, in1=probs[:, tt, :], op0=ALU.mult, op1=ALU.add),
                       reads=[ptmp_buf, probs_buf], writes=[ptmp_buf], tag="rt_pen")
                p.step("vector", lambda v: v.reduce_max(out=rstat[:, 3:4], in_=ptmp[:], axis=AX.X),
                       reads=[ptmp_buf], writes=[rstat_buf], tag="rt_m2")
                p.step("vector", lambda v, tt=tt: v.tensor_single_scalar(out=ptmp[:], in_=probs[:, tt, :], scalar=rstat[:, 3:4], op=ALU.is_ge),
                       reads=[probs_buf, rstat_buf], writes=[ptmp_buf], tag="rt_msk2")
                p.step("vector", lambda v, tt=tt: v.tensor_tensor(out=comb[:, tt, 0:8], in0=probs[:, tt, :], in1=ptmp[:], op=ALU.mult),
                       reads=[probs_buf, ptmp_buf], writes=[comb_buf], tag="rt_comb")
            p.step("vector", lambda v: v.memset(comb[:, :, 8:10], 1.0), reads=[], writes=[comb_buf], tag="rt_sh1")

            p.step("sync", lambda sy, tb=tb: sy.dma_start(out=cdbg_d[tb], in_=comb[:]),
                   reads=[comb_buf], writes=[], dma=True, tag=f"cdbg{tb}")

            p.step("vector", lambda v: v.memset(acc[:], 0.0), reads=[], writes=[acc_buf], tag="accz")

            for e in range(NE):
                p.step("gpsimd", lambda gp, e=e: gp.dma_start(
                    out=wde[:, e % 2, :, :], in_=wd_d[e * NOFE:(e + 1) * NOFE].rearrange("k p d -> p k d")),
                    reads=[], writes=[wde_bufs[e % 2]], dma=True, tag=f"ldwd{e}")
                for of in range(NOFE):
                    k = e * NOFE + of
                    bi = of % 2
                    for g in range(2):
                        w_d = wg_d if g == 0 else wu_d
                        p.step("gpsimd", lambda gp, w_d=w_d, g=g, k=k, bi=bi: gp.dma_start(
                            out=wgu[:, g, bi, :], in_=w_d[k]), reads=[], writes=[wgu_bufs[g][bi]], dma=True, tag=f"ldgu{k}")
                    bank_g = of % 2
                    bank_u = 2 + of % 2

                    def t_gu(t, bi=bi, bank_g=bank_g, bank_u=bank_u):
                        last = None
                        for c in range(NDC):
                            t.matmul(out=ps[bank_g][:], lhsT=wgu[:, 0, bi, c * 128:(c + 1) * 128],
                                     rhs=xf[:, c, :], start=(c == 0), stop=(c == NDC - 1))
                        for c in range(NDC):
                            last = t.matmul(out=ps[bank_u][:], lhsT=wgu[:, 1, bi, c * 128:(c + 1) * 128],
                                            rhs=xf[:, c, :], start=(c == 0), stop=(c == NDC - 1))
                        return last
                    p.step("tensor", t_gu, reads=[wgu_bufs[0][bi], wgu_bufs[1][bi], xf_buf],
                           writes=[ps_bufs[bank_g], ps_bufs[bank_u]], tag=f"mm_gu{k}")

                    p.step("scalar", lambda s, bank_g=bank_g: s.activation(
                        out=silu_t[:], in_=ps[bank_g][:], func=AF.Silu),
                        reads=[ps_bufs[bank_g]], writes=[silu_buf], tag=f"silu{k}")

                    p.step("vector", lambda v, of=of, bank_u=bank_u: v.tensor_tensor(
                        out=hbuf[:, of, :], in0=ps[bank_u][:], in1=silu_t[:], op=ALU.mult),
                        reads=[ps_bufs[bank_u], silu_buf], writes=[h_buf], tag=f"hmul{k}")

                for tt in range(TTB):
                    for dc in range(4):
                        bank = 4 + (tt * 4 + dc) % 4

                        def t_dn(t, e=e, tt=tt, dc=dc, bank=bank):
                            last = None
                            for of in range(NOFE):
                                last = t.matmul(out=ps[bank][:], lhsT=hbuf[:, of, tt * 128:(tt + 1) * 128],
                                                rhs=wde[:, e % 2, of, dc * 512:(dc + 1) * 512],
                                                start=(of == 0), stop=(of == NOFE - 1))
                            return last
                        p.step("tensor", t_dn, reads=[h_buf, wde_bufs[e % 2]], writes=[ps_bufs[bank]], tag=f"mm_dn{e}_{tt}_{dc}")

                        p.step("vector", lambda v, e=e, tt=tt, dc=dc, bank=bank: v.scalar_tensor_tensor(
                            out=acc[:, tt, dc * 512:(dc + 1) * 512], in0=ps[bank][:], scalar=comb[:, tt, e:e + 1],
                            in1=acc[:, tt, dc * 512:(dc + 1) * 512], op0=ALU.mult, op1=ALU.add),
                            reads=[ps_bufs[bank], comb_buf, acc_buf], writes=[acc_buf], tag=f"dnacc{e}_{tt}_{dc}")

            for tt in range(TTB):
                gt = tb * TTB + tt
                for dc in range(4):
                    ri = (tt * 4 + dc) % 2
                    p.step("sync", lambda sy, gt=gt, dc=dc, ri=ri: sy.dma_start(
                        out=xres[:, ri, :], in_=x_d[gt][:, dc * 512:(dc + 1) * 512]),
                        reads=[], writes=[xres_bufs[ri]], dma=True, tag=f"ldres{gt}_{dc}")
                    p.step("vector", lambda v, tt=tt, dc=dc, ri=ri: v.tensor_tensor(
                        out=ytile[:, ri, :], in0=acc[:, tt, dc * 512:(dc + 1) * 512], in1=xres[:, ri, :], op=ALU.add),
                        reads=[acc_buf, xres_bufs[ri]], writes=[yt_bufs[ri]], tag=f"yep{gt}_{dc}")
                    p.step("sync", lambda sy, gt=gt, dc=dc, ri=ri: sy.dma_start(
                        out=y_d[gt][:, dc * 512:(dc + 1) * 512], in_=ytile[:, ri, :]),
                        reads=[yt_bufs[ri]], writes=[], dma=True, tag=f"sty{gt}_{dc}")

        with nc.Block() as block:
            p.emit(nc, block)
            p.finish(nc, block)
    return nc


def host_moe_weights(gate_w, w_gate, w_up, w_down, sw_gate, sw_up, sw_down):
    gw = to_bf16(gate_w.reshape(NDC, 128, 8).transpose(1, 0, 2))

    def lhsT_tiles(w):
        A = np.asarray(w)
        H_ = A.shape[1]
        A = A.reshape(NDC, 128, H_ // 128, 128)
        return A.transpose(2, 1, 0, 3)

    def stack_gu(we, ws):
        parts = [lhsT_tiles(we[e]) for e in range(8)]
        parts.append(lhsT_tiles(ws[:, :EH]))
        parts.append(lhsT_tiles(ws[:, EH:]))
        Astk = np.concatenate(parts, axis=0)
        return f32(Astk.reshape(NE * NOFE, 128, NDC * 128))

    wg = stack_gu(w_gate, sw_gate)
    wu = stack_gu(w_up, sw_up)

    def down_tiles(wd):
        H_ = wd.shape[0]
        return np.asarray(wd).reshape(H_ // 128, 128, D)

    wdp = [down_tiles(w_down[e]) for e in range(8)]
    wdp.append(down_tiles(sw_down[:EH]))
    wdp.append(down_tiles(sw_down[EH:]))
    wd = f32(np.concatenate(wdp, axis=0))
    return dict(gw=gw, wg=wg, wu=wu, wd=wd)


def run_moe(x_full, weights, nc_cache={}):
    if "moe" not in nc_cache:
        nc_cache["moe"] = build_moe()
    nc = nc_cache["moe"]
    xr = np.ascontiguousarray(x_full.reshape(NCORES, NTT, 128, D))
    in_maps = []
    for c in range(NCORES):
        m = dict(weights)
        m["x"] = f32(xr[c])
        in_maps.append(m)
    res = run_bass_kernel_spmd(nc, in_maps, core_ids=list(range(NCORES)))
    return np.stack([r["y"] for r in res.results]).reshape(B, T, D)


QR = 768
NQC = QR // 128
KVR = 512
NKC = KVR // 128
HH = 16
QKD = 192
ROPE_D = 64
RMS_EPS = 1e-6


def _emit_rope(p, qv, q_buf, cs, cs_buf, rb, rb_buf, nh, tag, hw=32):
    co1 = cs[:, 0:1, 0:hw].broadcast_to([128, nh, hw])
    si1 = cs[:, 1:2, 0:hw].broadcast_to([128, nh, hw])
    A = qv[:, :, 0:hw]
    Bv = qv[:, :, hw:2 * hw]
    p.step("vector", lambda v: v.tensor_tensor(out=rb[:, :, 0:hw], in0=Bv, in1=si1, op=ALU.mult),
           reads=[q_buf, cs_buf], writes=[rb_buf], tag=f"{tag}_bs")
    p.step("vector", lambda v: v.tensor_tensor(out=rb[:, :, hw:2 * hw], in0=A, in1=si1, op=ALU.mult),
           reads=[q_buf, cs_buf], writes=[rb_buf], tag=f"{tag}_as")
    p.step("vector", lambda v: v.tensor_tensor(out=A, in0=A, in1=co1, op=ALU.mult),
           reads=[q_buf, cs_buf], writes=[q_buf], tag=f"{tag}_ac")
    p.step("vector", lambda v: v.tensor_tensor(out=A, in0=A, in1=rb[:, :, 0:hw], op=ALU.subtract),
           reads=[q_buf, rb_buf], writes=[q_buf], tag=f"{tag}_a2")
    p.step("vector", lambda v: v.tensor_tensor(out=Bv, in0=Bv, in1=co1, op=ALU.mult),
           reads=[q_buf, cs_buf], writes=[q_buf], tag=f"{tag}_bc")
    p.step("vector", lambda v: v.tensor_tensor(out=Bv, in0=Bv, in1=rb[:, :, hw:2 * hw], op=ALU.add),
           reads=[q_buf, rb_buf], writes=[q_buf], tag=f"{tag}_b2")


def build_mla_a():
    nc = bass.Bass("TRN2", target_bir_lowering=False)
    x_d = nc.dram_tensor("x", [NTT, 128, D], F32, kind="ExternalInput").ap()
    wqa_d = nc.dram_tensor("wqa", [NDC, 128, QR], F32, kind="ExternalInput").ap()
    wqb_d = nc.dram_tensor("wqb", [NQC, 128, HH * QKD], F32, kind="ExternalInput").ap()
    wkva_d = nc.dram_tensor("wkva", [NDC, 128, KVR + ROPE_D], F32, kind="ExternalInput").ap()
    wkvb_d = nc.dram_tensor("wkvb", [NKC, 128, HH * 256], F32, kind="ExternalInput").ap()
    gqa_d = nc.dram_tensor("gqa", [NQC, 128, 1], F32, kind="ExternalInput").ap()
    gkva_d = nc.dram_tensor("gkva", [NKC, 128, 1], F32, kind="ExternalInput").ap()
    cs_d = nc.dram_tensor("cs", [NTT, 128, 2, ROPE_D], F32, kind="ExternalInput").ap()
    q_o = nc.dram_tensor("q", [NTT, 128, HH * QKD], BF16, kind="ExternalOutput").ap()
    qn_o = nc.dram_tensor("qn", [NTT, 128, HH], F32, kind="ExternalOutput").ap()
    kn_o = nc.dram_tensor("kn", [NTT, 128, HH * 128], BF16, kind="ExternalOutput").ap()
    kr_o = nc.dram_tensor("kr", [NTT, 128, ROPE_D], BF16, kind="ExternalOutput").ap()
    v_o = nc.dram_tensor("v", [NTT, 128, HH * 128], BF16, kind="ExternalOutput").ap()

    from contextlib import ExitStack
    with ExitStack() as st_:
        e_ = st_.enter_context
        xt = e_(nc.sbuf_tensor("xt", [128, 2, D], F32))
        xln = e_(nc.sbuf_tensor("xln", [128, 2, D], BF16))
        xf = e_(nc.sbuf_tensor("xf", [128, NDC, 128], BF16))
        scr = e_(nc.sbuf_tensor("scr", [128, 2, D], BF16))
        stats = e_(nc.sbuf_tensor("stats", [128, 2, 8], F32))
        w32 = e_(nc.sbuf_tensor("w32", [128, 2048], F32))
        wqa = e_(nc.sbuf_tensor("wqa_s", [128, NDC, QR], BF16))
        wqb = e_(nc.sbuf_tensor("wqb_s", [128, NQC, HH * QKD], BF16))
        wkva = e_(nc.sbuf_tensor("wkva_s", [128, NDC, KVR + ROPE_D], BF16))
        wkvb = e_(nc.sbuf_tensor("wkvb_s", [128, NKC, HH * 256], BF16))
        gsc = e_(nc.sbuf_tensor("gsc", [128, NQC + NKC], F32))
        qcb = e_(nc.sbuf_tensor("qcb", [128, QR], BF16))
        qcf = e_(nc.sbuf_tensor("qcf", [128, NQC, 128], BF16))
        kvb = e_(nc.sbuf_tensor("kvb", [128, KVR], BF16))
        kvf = e_(nc.sbuf_tensor("kvf", [128, NKC, 128], BF16))
        cs = e_(nc.sbuf_tensor("cs_s", [128, 2, ROPE_D], F32))
        qsb = e_(nc.sbuf_tensor("qsb", [128, HH * QKD], F32))
        ksb = e_(nc.sbuf_tensor("ksb", [128, HH * 256], F32))
        krb = e_(nc.sbuf_tensor("krb", [128, ROPE_D], F32))
        rb = e_(nc.sbuf_tensor("rb", [128, HH, ROPE_D], F32))
        obf = e_(nc.sbuf_tensor("obf", [128, HH * 256 + 64], BF16))
        ps = [e_(nc.psum_tensor(f"pb{i}", [128, 512], F32)) for i in range(8)]

        p = Prog()
        x_bufs = [p.buf(f"xt{i}") for i in range(2)]
        xln_bufs = [p.buf(f"xln{i}") for i in range(2)]
        xf_buf = p.buf("xf")
        scr_bufs = [p.buf(f"scr{i}") for i in range(2)]
        st_bufs = [p.buf(f"stats{i}") for i in range(2)]
        w32_buf = p.buf("w32")
        wqa_buf, wqb_buf, wkva_buf, wkvb_buf = p.buf("wqa"), p.buf("wqb"), p.buf("wkva"), p.buf("wkvb")
        gsc_buf = p.buf("gsc")
        qcb_buf, qcf_buf = p.buf("qcb"), p.buf("qcf")
        kvb_buf, kvf_buf = p.buf("kvb"), p.buf("kvf")
        cs_buf = p.buf("cs")
        qsb_buf, ksb_buf, krb_buf, rb_buf = p.buf("qsb"), p.buf("ksb"), p.buf("krb"), p.buf("rb")
        obf_buf = p.buf("obf")
        ps_bufs = [p.buf(f"ps{i}") for i in range(8)]

        p.step("gpsimd", lambda g: g.dma_start(out=wqa[:], in_=wqa_d.rearrange("c p f -> p c f")),
               writes=[wqa_buf], dma=True, tag="ld_wqa")
        p.step("gpsimd", lambda g: g.dma_start(out=wkva[:], in_=wkva_d.rearrange("c p f -> p c f")),
               writes=[wkva_buf], dma=True, tag="ld_wkva")
        p.step("sync", lambda sy: sy.dma_start(out=gsc[:, 0:NQC], in_=gqa_d.rearrange("c p o -> p (c o)"), allow_slow_non_contiguous=True),
               writes=[gsc_buf], dma=True, tag="ld_gqa")
        p.step("sync", lambda sy: sy.dma_start(out=gsc[:, NQC:], in_=gkva_d.rearrange("c p o -> p (c o)"), allow_slow_non_contiguous=True),
               writes=[gsc_buf], dma=True, tag="ld_gkva")
        for c in range(NQC):
            for hlf in range(2):
                w = HH * QKD // 2
                p.step("sync", lambda sy, c=c, hlf=hlf, w=w: sy.dma_start(out=w32[:, 0:w], in_=wqb_d[c][:, hlf * w:(hlf + 1) * w]),
                       writes=[w32_buf], dma=True, tag=f"ld_wqb{c}_{hlf}")
                p.step("vector", lambda v, c=c, hlf=hlf, w=w: v.tensor_scalar_mul(out=wqb[:, c, hlf * w:(hlf + 1) * w], in0=w32[:, 0:w], scalar1=gsc[:, c:c + 1]),
                       reads=[w32_buf, gsc_buf], writes=[wqb_buf], tag=f"sc_wqb{c}_{hlf}")
        for c in range(NKC):
            for hlf in range(2):
                w = HH * 128
                p.step("sync", lambda sy, c=c, hlf=hlf, w=w: sy.dma_start(out=w32[:, 0:w], in_=wkvb_d[c][:, hlf * w:(hlf + 1) * w]),
                       writes=[w32_buf], dma=True, tag=f"ld_wkvb{c}_{hlf}")
                p.step("vector", lambda v, c=c, hlf=hlf, w=w: v.tensor_scalar_mul(out=wkvb[:, c, hlf * w:(hlf + 1) * w], in0=w32[:, 0:w], scalar1=gsc[:, NQC + c:NQC + c + 1]),
                       reads=[w32_buf, gsc_buf], writes=[wkvb_buf], tag=f"sc_wkvb{c}_{hlf}")

        for gt in range(NTT):
            bi = gt % 2
            p.step("sync", lambda sy, gt=gt, bi=bi: sy.dma_start(out=xt[:, bi, :], in_=x_d[gt]),
                   writes=[x_bufs[bi]], dma=True, tag=f"ldx{gt}")
            p.step("sync", lambda sy, gt=gt: sy.dma_start(out=cs[:], in_=cs_d[gt]),
                   writes=[cs_buf], dma=True, tag=f"ldcs{gt}")
            emit_ln_tile(p, xt[:, bi, :], x_bufs[bi], xln[:, bi, :], xln_bufs[bi],
                         scr[:, bi, :], scr_bufs[bi], stats[:, bi, :], st_bufs[bi])
            p.step("sync", lambda sy, bi=bi: sy.dma_start(
                out=xf[:], in_=xln[:, bi, :].rearrange("t (c d) -> t c d", c=NDC), transpose=True),
                reads=[xln_bufs[bi]], writes=[xf_buf], dma=True, tag=f"trx{gt}")

            def t_qa(t):
                last = None
                for c in range(NDC):
                    t.matmul(out=ps[0][:], lhsT=xf[:, c, :], rhs=wqa[:, c, 0:512], start=(c == 0), stop=(c == NDC - 1))
                for c in range(NDC):
                    last = t.matmul(out=ps[1][:, 0:256], lhsT=xf[:, c, :], rhs=wqa[:, c, 512:768], start=(c == 0), stop=(c == NDC - 1))
                return last
            p.step("tensor", t_qa, reads=[xf_buf, wqa_buf], writes=[ps_bufs[0], ps_bufs[1]], tag=f"mm_qa{gt}")
            p.step("scalar", lambda s: s.activation(out=scr[:, 0, 0:512], in_=ps[0][:], func=AF.Square, accum_out=stats[:, 0, 4:5]),
                   reads=[ps_bufs[0]], writes=[scr_bufs[0], st_bufs[0]], tag=f"q_sq0{gt}")
            p.step("scalar", lambda s: s.activation(out=scr[:, 0, 0:256], in_=ps[1][:, 0:256], func=AF.Square, accum_out=stats[:, 0, 5:6]),
                   reads=[ps_bufs[1]], writes=[scr_bufs[0], st_bufs[0]], tag=f"q_sq1{gt}")
            p.step("vector", lambda v: v.tensor_tensor(out=stats[:, 0, 4:5], in0=stats[:, 0, 4:5], in1=stats[:, 0, 5:6], op=ALU.add),
                   reads=[st_bufs[0]], writes=[st_bufs[0]], tag=f"q_ss{gt}")
            p.step("vector", lambda v: v.tensor_scalar(out=stats[:, 0, 4:5], in0=stats[:, 0, 4:5], scalar1=1.0 / QR, scalar2=RMS_EPS, op0=ALU.mult, op1=ALU.add),
                   reads=[st_bufs[0]], writes=[st_bufs[0]], tag=f"q_var{gt}")
            p.step("scalar", lambda s: s.activation(out=stats[:, 0, 6:7], in_=stats[:, 0, 4:5], func=AF.Sqrt),
                   reads=[st_bufs[0]], writes=[st_bufs[0]], tag=f"q_sqrt{gt}")
            p.step("vector", lambda v: v.reciprocal(out=stats[:, 0, 6:7], in_=stats[:, 0, 6:7]),
                   reads=[st_bufs[0]], writes=[st_bufs[0]], tag=f"q_rec{gt}")
            p.step("vector", lambda v: v.tensor_scalar_mul(out=qcb[:, 0:512], in0=ps[0][:], scalar1=stats[:, 0, 6:7]),
                   reads=[ps_bufs[0], st_bufs[0]], writes=[qcb_buf], tag=f"q_n0{gt}")
            p.step("vector", lambda v: v.tensor_scalar_mul(out=qcb[:, 512:768], in0=ps[1][:, 0:256], scalar1=stats[:, 0, 6:7]),
                   reads=[ps_bufs[1], st_bufs[0]], writes=[qcb_buf], tag=f"q_n1{gt}")
            p.step("sync", lambda sy: sy.dma_start(out=qcf[:], in_=qcb[:].rearrange("t (c d) -> t c d", c=NQC), transpose=True),
                   reads=[qcb_buf], writes=[qcf_buf], dma=True, tag=f"trqc{gt}")

            def t_kva(t):
                last = None
                for c in range(NDC):
                    t.matmul(out=ps[2][:], lhsT=xf[:, c, :], rhs=wkva[:, c, 0:512], start=(c == 0), stop=(c == NDC - 1))
                for c in range(NDC):
                    last = t.matmul(out=ps[3][:, 0:ROPE_D], lhsT=xf[:, c, :], rhs=wkva[:, c, 512:576], start=(c == 0), stop=(c == NDC - 1))
                return last
            p.step("tensor", t_kva, reads=[xf_buf, wkva_buf], writes=[ps_bufs[2], ps_bufs[3]], tag=f"mm_kva{gt}")
            p.step("scalar", lambda s: s.activation(out=scr[:, 1, 0:512], in_=ps[2][:], func=AF.Square, accum_out=stats[:, 1, 4:5]),
                   reads=[ps_bufs[2]], writes=[scr_bufs[1], st_bufs[1]], tag=f"kv_sq{gt}")
            p.step("vector", lambda v: v.tensor_scalar(out=stats[:, 1, 4:5], in0=stats[:, 1, 4:5], scalar1=1.0 / KVR, scalar2=RMS_EPS, op0=ALU.mult, op1=ALU.add),
                   reads=[st_bufs[1]], writes=[st_bufs[1]], tag=f"kv_var{gt}")
            p.step("scalar", lambda s: s.activation(out=stats[:, 1, 6:7], in_=stats[:, 1, 4:5], func=AF.Sqrt),
                   reads=[st_bufs[1]], writes=[st_bufs[1]], tag=f"kv_sqrt{gt}")
            p.step("vector", lambda v: v.reciprocal(out=stats[:, 1, 6:7], in_=stats[:, 1, 6:7]),
                   reads=[st_bufs[1]], writes=[st_bufs[1]], tag=f"kv_rec{gt}")
            p.step("vector", lambda v: v.tensor_scalar_mul(out=kvb[:], in0=ps[2][:], scalar1=stats[:, 1, 6:7]),
                   reads=[ps_bufs[2], st_bufs[1]], writes=[kvb_buf], tag=f"kv_n{gt}")
            p.step("sync", lambda sy: sy.dma_start(out=kvf[:], in_=kvb[:].rearrange("t (c d) -> t c d", c=NKC), transpose=True),
                   reads=[kvb_buf], writes=[kvf_buf], dma=True, tag=f"trkv{gt}")
            p.step("vector", lambda v: v.tensor_copy(out=krb[:], in_=ps[3][:, 0:ROPE_D]),
                   reads=[ps_bufs[3]], writes=[krb_buf], tag=f"kr_cp{gt}")
            _emit_rope(p, krb[:].rearrange("t (h d) -> t h d", h=1), krb_buf, cs, cs_buf, rb[:, 0:1, :], rb_buf, 1, f"krope{gt}")
            p.step("vector", lambda v: v.tensor_copy(out=obf[:, HH * 256:HH * 256 + ROPE_D], in_=krb[:]),
                   reads=[krb_buf], writes=[obf_buf], tag=f"kr_bf{gt}")
            p.step("sync", lambda sy, gt=gt: sy.dma_start(out=kr_o[gt], in_=obf[:, HH * 256:HH * 256 + ROPE_D]),
                   reads=[obf_buf], writes=[], dma=True, tag=f"st_kr{gt}")

            def t_qb(t):
                last = None
                for j in range(6):
                    for c in range(NQC):
                        last = t.matmul(out=ps[j][:], lhsT=qcf[:, c, :], rhs=wqb[:, c, j * 512:(j + 1) * 512], start=(c == 0), stop=(c == NQC - 1))
                return last
            p.step("tensor", t_qb, reads=[qcf_buf, wqb_buf], writes=[ps_bufs[0], ps_bufs[1], ps_bufs[2], ps_bufs[3], ps_bufs[4], ps_bufs[5]], tag=f"mm_qb{gt}")
            for j in range(6):
                p.step("vector", lambda v, j=j: v.tensor_copy(out=qsb[:, j * 512:(j + 1) * 512], in_=ps[j][:]),
                       reads=[ps_bufs[j]], writes=[qsb_buf], tag=f"q_ev{j}_{gt}")
            _emit_rope(p, qsb[:].rearrange("t (h d) -> t h d", h=HH)[:, :, 128:192], qsb_buf, cs, cs_buf, rb, rb_buf, HH, f"qrope{gt}")
            p.step("vector", lambda v: v.tensor_copy(out=obf[:, 0:HH * QKD], in_=qsb[:]),
                   reads=[qsb_buf], writes=[obf_buf], tag=f"q_bf{gt}")
            p.step("sync", lambda sy, gt=gt: sy.dma_start(out=q_o[gt], in_=obf[:, 0:HH * QKD]),
                   reads=[obf_buf], writes=[], dma=True, tag=f"st_q{gt}")
            p.step("scalar", lambda s: s.activation(out=ksb[:, 0:HH * QKD], in_=qsb[:], func=AF.Square),
                   reads=[qsb_buf], writes=[ksb_buf], tag=f"qn_sq{gt}")
            p.step("vector", lambda v: v.reduce_sum(out=ksb[:, 4000:4000 + HH], in_=ksb[:, 0:HH * QKD].rearrange("t (h d) -> t h d", h=HH), axis=AX.X),
                   reads=[ksb_buf], writes=[ksb_buf], tag=f"qn_rs{gt}")
            p.step("scalar", lambda s: s.activation(out=ksb[:, 4000 + HH:4000 + 2 * HH], in_=ksb[:, 4000:4000 + HH], func=AF.Sqrt),
                   reads=[ksb_buf], writes=[ksb_buf], tag=f"qn_rt{gt}")
            p.step("sync", lambda sy, gt=gt: sy.dma_start(out=qn_o[gt], in_=ksb[:, 4000 + HH:4000 + 2 * HH]),
                   reads=[ksb_buf], writes=[], dma=True, tag=f"st_qn{gt}")

            def t_kb(t):
                last = None
                for j in range(8):
                    for c in range(NKC):
                        last = t.matmul(out=ps[j][:], lhsT=kvf[:, c, :], rhs=wkvb[:, c, j * 512:(j + 1) * 512], start=(c == 0), stop=(c == NKC - 1))
                return last
            p.step("tensor", t_kb, reads=[kvf_buf, wkvb_buf], writes=ps_bufs, tag=f"mm_kb{gt}")
            for j in range(8):
                p.step("vector", lambda v, j=j: v.tensor_copy(out=ksb[:, j * 512:(j + 1) * 512], in_=ps[j][:]),
                       reads=[ps_bufs[j]], writes=[ksb_buf], tag=f"k_ev{j}_{gt}")
            kview = ksb[:].rearrange("t (h z) -> t h z", h=HH)
            p.step("vector", lambda v: v.tensor_copy(out=obf[:, 0:HH * 128].rearrange("t (h z) -> t h z", h=HH), in_=kview[:, :, 0:128]),
                   reads=[ksb_buf], writes=[obf_buf], tag=f"kn_bf{gt}")
            p.step("sync", lambda sy, gt=gt: sy.dma_start(out=kn_o[gt], in_=obf[:, 0:HH * 128]),
                   reads=[obf_buf], writes=[], dma=True, tag=f"st_kn{gt}")
            p.step("vector", lambda v: v.tensor_copy(out=obf[:, HH * 128:HH * 256].rearrange("t (h z) -> t h z", h=HH), in_=kview[:, :, 128:256]),
                   reads=[ksb_buf], writes=[obf_buf], tag=f"v_bf{gt}")
            p.step("sync", lambda sy, gt=gt: sy.dma_start(out=v_o[gt], in_=obf[:, HH * 128:HH * 256]),
                   reads=[obf_buf], writes=[], dma=True, tag=f"st_v{gt}")

        with nc.Block() as block:
            p.emit(nc, block)
            p.finish(nc, block)
    return nc


NP_MLA = 4
NQT = T // 128
MLA_SCALE = float(QKD) ** -0.5


def build_mla_b():
    nc = bass.Bass("TRN2", target_bir_lowering=False)
    qf1_d = nc.dram_tensor("qf1", [NP_MLA, 128, T], BF16, kind="ExternalInput").ap()
    qf2_d = nc.dram_tensor("qf2", [NP_MLA, 64, T], BF16, kind="ExternalInput").ap()
    kf1_d = nc.dram_tensor("kf1", [NP_MLA, 128, T], BF16, kind="ExternalInput").ap()
    kf2_d = nc.dram_tensor("kf2", [NP_MLA, 64, T], BF16, kind="ExternalInput").ap()
    vt_d = nc.dram_tensor("vt", [NP_MLA, NQT, 128, 128], BF16, kind="ExternalInput").ap()
    qn_d = nc.dram_tensor("qn", [NP_MLA, 1, T], F32, kind="ExternalInput").ap()
    mask_d = nc.dram_tensor("mask", [128, 128], F32, kind="ExternalInput").ap()
    onesb_d = nc.dram_tensor("onesb", [128, 1], BF16, kind="ExternalInput").ap()
    onesf_d = nc.dram_tensor("onesf", [1, 128], F32, kind="ExternalInput").ap()
    o_d = nc.dram_tensor("o", [NP_MLA, 128, T], BF16, kind="ExternalOutput").ap()

    SD = 3

    from contextlib import ExitStack
    with ExitStack() as st_:
        e_ = st_.enter_context
        qf1 = e_(nc.sbuf_tensor("qf1_s", [128, 2, T], BF16))
        qf2 = e_(nc.sbuf_tensor("qf2_s", [65, 2, T], BF16))
        kf1 = e_(nc.sbuf_tensor("kf1_s", [128, 2, T], BF16))
        kf2 = e_(nc.sbuf_tensor("kf2_s", [65, 2, T], BF16))
        vt = e_(nc.sbuf_tensor("vt_s", [128, 2, NQT, 128], BF16))
        qnr = e_(nc.sbuf_tensor("qnr", [1, 2, 512], F32))
        nbrow = e_(nc.sbuf_tensor("nbrow", [1, 2, 512], BF16))
        maskt = e_(nc.sbuf_tensor("maskt", [128, 128], F32))
        onesb = e_(nc.sbuf_tensor("onesb_s", [128, 1], BF16))
        onesf = e_(nc.sbuf_tensor("onesf_s", [1, 128], F32))
        onesc = e_(nc.sbuf_tensor("onesc", [128, 1], F32))
        sqk = e_(nc.sbuf_tensor("sqk", [128, 2, 512], BF16))
        sqk2 = e_(nc.sbuf_tensor("sqk2", [64, 2, 512], BF16))
        kst = e_(nc.sbuf_tensor("kst", [1, 8], F32))
        Pt = e_(nc.sbuf_tensor("Pt", [128, SD, 512], BF16))
        Pacc = e_(nc.sbuf_tensor("Pacc", [128, 2, 512], F32))
        lrow = e_(nc.sbuf_tensor("lrow", [1, 2, 512], F32))
        linvb = e_(nc.sbuf_tensor("linvb", [128, 2, 512], F32))
        obf = e_(nc.sbuf_tensor("obf", [128, 2, 512], BF16))
        ps = [e_(nc.psum_tensor(f"pb{i}", [128, 512], F32)) for i in range(8)]

        p = Prog()
        qf1_bufs = [p.buf(f"qf1_{i}") for i in range(2)]
        qf2_bufs = [p.buf(f"qf2_{i}") for i in range(2)]
        kf1_bufs = [p.buf(f"kf1_{i}") for i in range(2)]
        kf2_bufs = [p.buf(f"kf2_{i}") for i in range(2)]
        vt_bufs = [p.buf(f"vt{i}") for i in range(2)]
        qnr_bufs = [p.buf(f"qnr{i}") for i in range(2)]
        nb_bufs = [p.buf(f"nb{i}") for i in range(2)]
        mask_buf = p.buf("mask")
        onesb_buf, onesf_buf, onesc_buf = p.buf("onesb"), p.buf("onesf"), p.buf("onesc")
        sqk_buf, sqk2_buf, kst_buf = p.buf("sqk"), p.buf("sqk2"), p.buf("kst")
        Pt_bufs = [p.buf(f"Pt{i}") for i in range(SD)]
        Pacc_bufs = [p.buf(f"Pa{i}") for i in range(2)]
        lrow_bufs = [p.buf(f"lr{i}") for i in range(2)]
        linv_bufs = [p.buf(f"li{i}") for i in range(2)]
        obf_bufs = [p.buf(f"ob{i}") for i in range(2)]
        ps_bufs = [p.buf(f"ps{i}") for i in range(8)]

        p.step("sync", lambda sy: sy.dma_start(out=maskt[:], in_=mask_d), writes=[mask_buf], dma=True, tag="ldmask")
        p.step("sync", lambda sy: sy.dma_start(out=onesb[:], in_=onesb_d), writes=[onesb_buf], dma=True, tag="ldob")
        p.step("sync", lambda sy: sy.dma_start(out=onesf[:], in_=onesf_d), writes=[onesf_buf], dma=True, tag="ldof")
        p.step("vector", lambda v: v.memset(onesc[:], 1.0), writes=[onesc_buf], tag="setoc")

        for pr in range(NP_MLA):
            pb = pr % 2
            p.step("sync", lambda sy, pr=pr, pb=pb: sy.dma_start(out=qf1[:, pb, :], in_=qf1_d[pr]),
                   writes=[qf1_bufs[pb]], dma=True, tag=f"ldq1_{pr}")
            p.step("sync", lambda sy, pr=pr, pb=pb: sy.dma_start(out=qf2[0:64, pb, :], in_=qf2_d[pr]),
                   writes=[qf2_bufs[pb]], dma=True, tag=f"ldq2_{pr}")
            p.step("sync", lambda sy, pr=pr, pb=pb: sy.dma_start(out=kf1[:, pb, :], in_=kf1_d[pr]),
                   writes=[kf1_bufs[pb]], dma=True, tag=f"ldk1_{pr}")
            p.step("sync", lambda sy, pr=pr, pb=pb: sy.dma_start(out=kf2[0:64, pb, :], in_=kf2_d[pr]),
                   writes=[kf2_bufs[pb]], dma=True, tag=f"ldk2_{pr}")
            p.step("gpsimd", lambda g, pr=pr, pb=pb: g.dma_start(out=vt[:, pb, :, :], in_=vt_d[pr].rearrange("k p z -> p k z")),
                   writes=[vt_bufs[pb]], dma=True, tag=f"ldv_{pr}")
            p.step("vector", lambda v, pb=pb: v.memset(kf2[64:65, pb, :], 1.0),
                   reads=[kf2_bufs[pb]], writes=[kf2_bufs[pb]], tag=f"k1r_{pr}")

            p.step("vector", lambda v: v.memset(kst[:, 0:1], 0.0), reads=[], writes=[kst_buf], tag=f"knz_{pr}")
            for j in range(16):
                jb = j % 2
                p.step("scalar", lambda s, pb=pb, j=j, jb=jb: s.activation(
                    out=sqk[:, jb, :], in_=kf1[:, pb, j * 512:(j + 1) * 512], func=AF.Square),
                    reads=[kf1_bufs[pb]], writes=[sqk_buf], tag=f"kn1_{pr}_{j}")
                p.step("scalar", lambda s, pb=pb, j=j, jb=jb: s.activation(
                    out=sqk2[:, jb, :], in_=kf2[0:64, pb, j * 512:(j + 1) * 512], func=AF.Square),
                    reads=[kf2_bufs[pb]], writes=[sqk2_buf], tag=f"kn2_{pr}_{j}")
                def t_kn(t, jb=jb):
                    t.matmul(out=ps[7][0:1, 0:512], lhsT=onesb[:], rhs=sqk[:, jb, :], start=True, stop=False)
                    return t.matmul(out=ps[7][0:1, 0:512], lhsT=onesb[0:64, :], rhs=sqk2[:, jb, :], start=False, stop=True)
                p.step("tensor", t_kn, reads=[sqk_buf, sqk2_buf, onesb_buf], writes=[ps_bufs[7]], tag=f"knm{pr}_{j}")
                p.step("vector", lambda v: v.reduce_max(out=kst[:, 2:3], in_=ps[7][0:1, 0:512], axis=AX.X),
                       reads=[ps_bufs[7]], writes=[kst_buf], tag=f"knr{pr}_{j}")
                p.step("vector", lambda v: v.tensor_tensor(out=kst[:, 0:1], in0=kst[:, 0:1], in1=kst[:, 2:3], op=ALU.max),
                       reads=[kst_buf], writes=[kst_buf], tag=f"knx{pr}_{j}")
            p.step("scalar", lambda s: s.activation(out=kst[:, 1:2], in_=kst[:, 0:1], func=AF.Sqrt),
                   reads=[kst_buf], writes=[kst_buf], tag=f"knsq{pr}")
            p.step("vector", lambda v: v.tensor_scalar_mul(out=kst[:, 1:2], in0=kst[:, 1:2], scalar1=-1.0),
                   reads=[kst_buf], writes=[kst_buf], tag=f"knsc{pr}")
            for qg in range(NQT // 4):
                gb = qg % 2
                bank_o = gb
                bank_l = 4 + gb
                nk = 4 * qg + 4
                q0, q1 = qg * 512, (qg + 1) * 512

                p.step("sync", lambda sy, pr=pr, gb=gb, q0=q0, q1=q1: sy.dma_start(
                    out=qnr[:, gb, :], in_=qn_d[pr][0:1, q0:q1]),
                    writes=[qnr_bufs[gb]], dma=True, tag=f"ldqn{pr}_{qg}")
                p.step("vector", lambda v, gb=gb: v.tensor_scalar_mul(out=nbrow[:, gb, :], in0=qnr[:, gb, :], scalar1=kst[0:1, 1:2]),
                       reads=[qnr_bufs[gb], kst_buf], writes=[nb_bufs[gb]], tag=f"nbr{pr}_{qg}")
                p.step("sync", lambda sy, pb=pb, gb=gb, q0=q0, q1=q1: sy.dma_start(
                    out=qf2[64:65, pb, q0:q1], in_=nbrow[0:1, gb, :]),
                    reads=[nb_bufs[gb]], writes=[qf2_bufs[pb]], dma=True, tag=f"nbcp{pr}_{qg}")

                def emit_qk(kt):
                    sb = (2, 3, 6)[kt % SD]
                    def t_qk(t, kt=kt, sb=sb, pb=pb, q0=q0, q1=q1):
                        t.matmul(out=ps[sb][:, 0:512], lhsT=kf1[:, pb, kt * 128:(kt + 1) * 128],
                                 rhs=qf1[:, pb, q0:q1], start=True, stop=False)
                        return t.matmul(out=ps[sb][:, 0:512], lhsT=kf2[:, pb, kt * 128:(kt + 1) * 128],
                                        rhs=qf2[:, pb, q0:q1], start=False, stop=True)
                    p.step("tensor", t_qk, reads=[kf1_bufs[pb], kf2_bufs[pb], qf1_bufs[pb], qf2_bufs[pb]],
                           writes=[ps_bufs[sb]], tag=f"qk{pr}_{qg}_{kt}")
                    return sb

                def emit_soft(kt, sb):
                    sd = kt % SD
                    if kt >= 4 * qg:
                        s0 = kt - 4 * qg
                        p.step("vector", lambda v, sb=sb, s0=s0: v.tensor_tensor(
                            out=ps[sb][:, s0 * 128:(s0 + 1) * 128], in0=ps[sb][:, s0 * 128:(s0 + 1) * 128],
                            in1=maskt[:], op=ALU.add),
                            reads=[mask_buf, ps_bufs[sb]], writes=[ps_bufs[sb]], tag=f"msk{pr}_{qg}_{kt}")
                    p.step("scalar", lambda sc, sb=sb, sd=sd: sc.activation(
                        out=Pt[:, sd, :], in_=ps[sb][:, 0:512], func=AF.Exp, scale=MLA_SCALE),
                        reads=[ps_bufs[sb]], writes=[Pt_bufs[sd]], tag=f"exp{pr}_{qg}_{kt}")
                    if kt > 4 * qg:
                        s0 = kt - 4 * qg
                        p.step("vector", lambda v, sd=sd, s0=s0: v.memset(Pt[:, sd, 0:s0 * 128], 0.0),
                               reads=[Pt_bufs[sd]], writes=[Pt_bufs[sd]], tag=f"pz{pr}_{qg}_{kt}")
                    if kt == 0:
                        p.step("vector", lambda v, sd=sd, gb=gb: v.tensor_copy(out=Pacc[:, gb, :], in_=Pt[:, sd, :]),
                               reads=[Pt_bufs[sd]], writes=[Pacc_bufs[gb]], tag=f"pa{pr}_{qg}_{kt}")
                    else:
                        p.step("vector", lambda v, sd=sd, gb=gb: v.tensor_tensor(
                            out=Pacc[:, gb, :], in0=Pacc[:, gb, :], in1=Pt[:, sd, :], op=ALU.add),
                            reads=[Pt_bufs[sd], Pacc_bufs[gb]], writes=[Pacc_bufs[gb]], tag=f"pa{pr}_{qg}_{kt}")

                def emit_pv(kt):
                    sd = kt % SD
                    p.step("tensor", lambda t, kt=kt, sd=sd, pb=pb, bank_o=bank_o, nk=nk: t.matmul(
                        out=ps[bank_o][:, 0:512], lhsT=vt[:, pb, kt, :], rhs=Pt[:, sd, :],
                        start=(kt == 0), stop=(kt == nk - 1)),
                        reads=[vt_bufs[pb], Pt_bufs[sd]], writes=[ps_bufs[bank_o]], tag=f"pv{pr}_{qg}_{kt}")

                sbs = {}
                for kt in range(min(2, nk)):
                    sbs[kt] = emit_qk(kt)
                for kt in range(nk):
                    emit_soft(kt, sbs[kt])
                    if kt + 2 < nk:
                        sbs[kt + 2] = emit_qk(kt + 2)
                    emit_pv(kt)
                p.step("tensor", lambda t, gb=gb, bank_l=bank_l: t.matmul(
                    out=ps[bank_l][0:1, 0:512], lhsT=onesc[:], rhs=Pacc[:, gb, :], start=True, stop=True),
                    reads=[onesc_buf, Pacc_bufs[gb]], writes=[ps_bufs[bank_l]], tag=f"lm{pr}_{qg}")
                p.step("vector", lambda v, gb=gb, bank_l=bank_l: v.reciprocal(out=lrow[:, gb, :], in_=ps[bank_l][0:1, 0:512]),
                       reads=[ps_bufs[bank_l]], writes=[lrow_bufs[gb]], tag=f"lr{pr}_{qg}")
                p.step("tensor", lambda t, gb=gb, bank_l=bank_l: t.matmul(
                    out=ps[bank_l][:, 0:512], lhsT=onesf[:], rhs=lrow[0:1, gb, :], start=True, stop=True),
                    reads=[onesf_buf, lrow_bufs[gb]], writes=[ps_bufs[bank_l]], tag=f"lb{pr}_{qg}")
                p.step("vector", lambda v, gb=gb, bank_l=bank_l: v.tensor_copy(out=linvb[:, gb, :], in_=ps[bank_l][:, 0:512]),
                       reads=[ps_bufs[bank_l]], writes=[linv_bufs[gb]], tag=f"lc{pr}_{qg}")
                p.step("vector", lambda v, gb=gb, bank_o=bank_o: v.tensor_tensor(
                    out=obf[:, gb, :], in0=ps[bank_o][:, 0:512], in1=linvb[:, gb, :], op=ALU.mult),
                    reads=[ps_bufs[bank_o], linv_bufs[gb]], writes=[obf_bufs[gb]], tag=f"fo{pr}_{qg}")
                p.step("sync", lambda sy, pr=pr, gb=gb, q0=q0, q1=q1: sy.dma_start(
                    out=o_d[pr][:, q0:q1], in_=obf[:, gb, :]),
                    reads=[obf_bufs[gb]], writes=[], dma=True, tag=f"sto{pr}_{qg}")

        with nc.Block() as block:
            p.emit(nc, block)
            p.finish(nc, block)
    return nc



def build_mla_c():
    nc = bass.Bass("TRN2", target_bir_lowering=False)
    x_d = nc.dram_tensor("x", [NTT, 128, D], F32, kind="ExternalInput").ap()
    of_d = nc.dram_tensor("of", [NDC, 128, TLOC], BF16, kind="ExternalInput").ap()
    wo_d = nc.dram_tensor("wo", [NDC, 128, D], F32, kind="ExternalInput").ap()
    y_d = nc.dram_tensor("y", [NTT, 128, D], F32, kind="ExternalOutput").ap()

    from contextlib import ExitStack
    with ExitStack() as st_:
        e_ = st_.enter_context
        of = e_(nc.sbuf_tensor("of_s", [128, NDC, TLOC], BF16))
        wo = e_(nc.sbuf_tensor("wo_s", [128, NDC, D], BF16))
        xres = e_(nc.sbuf_tensor("xres", [128, 2, 512], F32))
        ytile = e_(nc.sbuf_tensor("ytile", [128, 2, 512], F32))
        ps = [e_(nc.psum_tensor(f"pb{i}", [128, 512], F32)) for i in range(8)]

        p = Prog()
        of_buf = p.buf("of")
        wo_buf = p.buf("wo")
        xres_bufs = [p.buf(f"xr{i}") for i in range(2)]
        yt_bufs = [p.buf(f"yt{i}") for i in range(2)]
        ps_bufs = [p.buf(f"ps{i}") for i in range(8)]

        p.step("sync", lambda sy: sy.dma_start(out=of[:], in_=of_d.rearrange("c p t -> p c t")),
               writes=[of_buf], dma=True, tag="ld_of")
        p.step("gpsimd", lambda g: g.dma_start(out=wo[:], in_=wo_d.rearrange("c p d -> p c d")),
               writes=[wo_buf], dma=True, tag="ld_wo")

        for gt in range(NTT):
            for dc in range(4):
                bank = (gt * 4 + dc) % 8
                ri = (gt * 4 + dc) % 2
                p.step("sync", lambda sy, gt=gt, dc=dc, ri=ri: sy.dma_start(
                    out=xres[:, ri, :], in_=x_d[gt][:, dc * 512:(dc + 1) * 512]),
                    writes=[xres_bufs[ri]], dma=True, tag=f"ldres{gt}_{dc}")

                def t_wo(t, gt=gt, dc=dc, bank=bank):
                    last = None
                    for c in range(NDC):
                        last = t.matmul(out=ps[bank][:], lhsT=of[:, c, gt * 128:(gt + 1) * 128],
                                        rhs=wo[:, c, dc * 512:(dc + 1) * 512], start=(c == 0), stop=(c == NDC - 1))
                    return last
                p.step("tensor", t_wo, reads=[of_buf, wo_buf], writes=[ps_bufs[bank]], tag=f"mm{gt}_{dc}")
                p.step("vector", lambda v, bank=bank, ri=ri: v.tensor_tensor(
                    out=ytile[:, ri, :], in0=ps[bank][:], in1=xres[:, ri, :], op=ALU.add),
                    reads=[ps_bufs[bank], xres_bufs[ri]], writes=[yt_bufs[ri]], tag=f"ep{gt}_{dc}")
                p.step("sync", lambda sy, gt=gt, dc=dc, ri=ri: sy.dma_start(
                    out=y_d[gt][:, dc * 512:(dc + 1) * 512], in_=ytile[:, ri, :]),
                    reads=[yt_bufs[ri]], writes=[], dma=True, tag=f"sty{gt}_{dc}")

        with nc.Block() as block:
            p.emit(nc, block)
            p.finish(nc, block)
    return nc


def host_rope_tables():
    inv = 1.0 / (10000.0 ** (np.arange(0, ROPE_D, 2, dtype=np.float64) / ROPE_D))
    ang = np.arange(T, dtype=np.float64)[:, None] * inv[None, :]
    emb = np.concatenate([ang, ang], axis=-1)
    return np.cos(emb).astype(np.float32), np.sin(emb).astype(np.float32)


def host_mla_weights(Wqa, g_qa, Wqb, Wkva, g_kva, Wkvb, Wo):
    cos, sin = host_rope_tables()
    cs_all = []
    for c in range(NCORES):
        pos = (c % 4) * TLOC + np.arange(TLOC)
        cs = np.stack([cos[pos], sin[pos]], axis=1).reshape(NTT, 128, 2, ROPE_D)
        cs_all.append(f32(cs))
    wqa = f32(np.asarray(Wqa).reshape(NDC, 128, QR))
    wqb = f32(np.asarray(Wqb).reshape(NQC, 128, HH * QKD))
    wkva = f32(np.asarray(Wkva).reshape(NDC, 128, KVR + ROPE_D))
    wkvb = f32(np.asarray(Wkvb).reshape(NKC, 128, HH * 256))
    gqa = f32(np.asarray(g_qa).reshape(NQC, 128, 1))
    gkva = f32(np.asarray(g_kva).reshape(NKC, 128, 1))
    wo = f32(np.asarray(Wo).reshape(NDC, 128, D))
    mask = np.tril(np.ones((128, 128), np.float32), k=-1) * -1e30
    return dict(wqa=wqa, wqb=wqb, wkva=wkva, wkvb=wkvb, gqa=gqa, gkva=gkva, wo=wo,
                mask=f32(mask), cs_all=cs_all, onesb=to_bf16(np.ones((128, 1))),
                onesf=f32(np.ones((1, 128))))


def run_mla(x_full, W, nc_cache={}):
    if "mla_a" not in nc_cache:
        nc_cache["mla_a"] = build_mla_a()
        nc_cache["mla_b"] = build_mla_b()
        nc_cache["mla_c"] = build_mla_c()
    xr = np.ascontiguousarray(x_full.reshape(NCORES, NTT, 128, D))
    in_maps = []
    for c in range(NCORES):
        m = dict(x=f32(xr[c]), wqa=W["wqa"], wqb=W["wqb"], wkva=W["wkva"], wkvb=W["wkvb"],
                 gqa=W["gqa"], gkva=W["gkva"], cs=W["cs_all"][c])
        in_maps.append(m)
    res = run_bass_kernel_spmd(nc_cache["mla_a"], in_maps, core_ids=list(range(NCORES)))
    q = np.stack([r["q"] for r in res.results]).reshape(B, T, HH, QKD)
    kn = np.stack([r["kn"] for r in res.results]).reshape(B, T, HH, 128)
    kr = np.stack([r["kr"] for r in res.results]).reshape(B, T, ROPE_D)
    v = np.stack([r["v"] for r in res.results]).reshape(B, T, HH, 128)
    qnorm = np.stack([r["qn"] for r in res.results]).reshape(B, T, HH)

    pairs = [(b, h) for b in range(B) for h in range(HH)]
    in_maps_b = []
    for c in range(NCORES):
        myp = pairs[c * 4:(c + 1) * 4]
        qf1 = np.stack([np.ascontiguousarray(q[b, :, h, 0:128].T) for (b, h) in myp])
        qf2 = np.stack([np.ascontiguousarray(q[b, :, h, 128:192].T) for (b, h) in myp])
        kf1 = np.stack([np.ascontiguousarray(kn[b, :, h, :].T) for (b, h) in myp])
        kf2 = np.stack([np.ascontiguousarray(kr[b, :, :].T) for (b, h) in myp])
        vt = np.stack([np.ascontiguousarray(v[b, :, h, :].reshape(NQT, 128, 128)) for (b, h) in myp])
        qnp = np.stack([np.ascontiguousarray(qnorm[b, :, h].reshape(1, T)) for (b, h) in myp])
        in_maps_b.append(dict(qf1=qf1, qf2=qf2, kf1=kf1, kf2=kf2, vt=vt, qn=f32(qnp),
                              mask=W["mask"], onesb=W["onesb"], onesf=W["onesf"]))
    res_b = run_bass_kernel_spmd(nc_cache["mla_b"], in_maps_b, core_ids=list(range(NCORES)))
    o_fm = np.zeros((B, HH, 128, T), dtype=BF)
    for c in range(NCORES):
        for i, (b, h) in enumerate(pairs[c * 4:(c + 1) * 4]):
            o_fm[b, h] = res_b.results[c]["o"][i]

    in_maps_c = []
    for c in range(NCORES):
        off = np.ascontiguousarray(o_fm[c // 4][:, :, (c % 4) * TLOC:(c % 4 + 1) * TLOC])
        in_maps_c.append(dict(x=f32(xr[c]), of=off, wo=W["wo"]))
    res_c = run_bass_kernel_spmd(nc_cache["mla_c"], in_maps_c, core_ids=list(range(NCORES)))
    return np.stack([r["y"] for r in res_c.results]).reshape(B, T, D)


SWA_D = 128
SWA_H = 16
SWA_KV = 4
SWA_EPS = 1e-6
SWA_SCALE = SWA_D ** -0.5
SWA_CAP = 50.0


def _emit_qknorm(p, qv, q_buf, nh, sq, sq_buf, nst, nst_buf, scb, scb_buf, tag):
    p.step("scalar", lambda s: s.activation(out=sq, in_=qv.rearrange("t h z -> t (h z)"), func=AF.Square),
           reads=[q_buf], writes=[sq_buf], tag=f"{tag}_sq")
    p.step("vector", lambda v: v.reduce_sum(out=nst[:, 0:nh], in_=sq.rearrange("t (h z) -> t h z", h=nh), axis=AX.X),
           reads=[sq_buf], writes=[nst_buf], tag=f"{tag}_ss")
    p.step("scalar", lambda s: s.activation(out=nst[:, 8:8 + nh], in_=nst[:, 0:nh], func=AF.Sqrt),
           reads=[nst_buf], writes=[nst_buf], tag=f"{tag}_rt")
    p.step("vector", lambda v: v.tensor_scalar_add(out=nst[:, 8:8 + nh], in0=nst[:, 8:8 + nh], scalar1=SWA_EPS),
           reads=[nst_buf], writes=[nst_buf], tag=f"{tag}_ep")
    p.step("vector", lambda v: v.reciprocal(out=nst[:, 8:8 + nh], in_=nst[:, 8:8 + nh]),
           reads=[nst_buf], writes=[nst_buf], tag=f"{tag}_rc")
    p.step("vector", lambda v: v.tensor_tensor(
        out=qv, in0=qv, in1=nst[:, 8:8 + nh].rearrange("t (h o) -> t h o", o=1).broadcast_to([128, nh, SWA_D]), op=ALU.mult),
        reads=[nst_buf, q_buf], writes=[q_buf], tag=f"{tag}_nm")
    p.step("vector", lambda v: v.tensor_tensor(
        out=qv, in0=qv, in1=scb.rearrange("t (o z) -> t o z", o=1).broadcast_to([128, nh, SWA_D]), op=ALU.mult),
        reads=[scb_buf, q_buf], writes=[q_buf], tag=f"{tag}_sc")


def build_swa_a():
    nc = bass.Bass("TRN2", target_bir_lowering=False)
    x_d = nc.dram_tensor("x", [NTT, 128, D], F32, kind="ExternalInput").ap()
    wq_d = nc.dram_tensor("wq", [NDC, 128, D], F32, kind="ExternalInput").ap()
    wk_d = nc.dram_tensor("wk", [NDC, 128, 512], F32, kind="ExternalInput").ap()
    wv_d = nc.dram_tensor("wv", [NDC, 128, 512], F32, kind="ExternalInput").ap()
    qs_d = nc.dram_tensor("qs", [1, 128], F32, kind="ExternalInput").ap()
    ks_d = nc.dram_tensor("ks", [1, 128], F32, kind="ExternalInput").ap()
    ones_d = nc.dram_tensor("ones", [1, 128], F32, kind="ExternalInput").ap()
    cs_d = nc.dram_tensor("cs", [NTT, 128, 2, SWA_D], F32, kind="ExternalInput").ap()
    q_o = nc.dram_tensor("q", [NTT, 128, SWA_H * SWA_D], BF16, kind="ExternalOutput").ap()
    k_o = nc.dram_tensor("k", [NTT, 128, SWA_KV * SWA_D], BF16, kind="ExternalOutput").ap()
    v_o = nc.dram_tensor("v", [NTT, 128, SWA_KV * SWA_D], BF16, kind="ExternalOutput").ap()

    from contextlib import ExitStack
    with ExitStack() as st_:
        e_ = st_.enter_context
        xt = e_(nc.sbuf_tensor("xt", [128, 2, D], F32))
        xln = e_(nc.sbuf_tensor("xln", [128, 2, D], BF16))
        xf = e_(nc.sbuf_tensor("xf", [128, NDC, 128], BF16))
        scr = e_(nc.sbuf_tensor("scr", [128, 2, D], BF16))
        stats = e_(nc.sbuf_tensor("stats", [128, 2, 8], F32))
        wq = e_(nc.sbuf_tensor("wq_s", [128, NDC, D], BF16))
        wk = e_(nc.sbuf_tensor("wk_s", [128, NDC, 512], BF16))
        wv = e_(nc.sbuf_tensor("wv_s", [128, NDC, 512], BF16))
        qsb = e_(nc.sbuf_tensor("qsb", [128, SWA_H * SWA_D], F32))
        kvb = e_(nc.sbuf_tensor("kvb", [128, 2, SWA_KV * SWA_D], F32))
        sq = e_(nc.sbuf_tensor("sq", [128, SWA_H * SWA_D], BF16))
        nst = e_(nc.sbuf_tensor("nst", [128, 24], F32))
        cs = e_(nc.sbuf_tensor("cs_s", [128, 2, SWA_D], F32))
        rb = e_(nc.sbuf_tensor("rb", [128, SWA_H, SWA_D], F32))
        qscb = e_(nc.sbuf_tensor("qscb", [128, 128], F32))
        kscb = e_(nc.sbuf_tensor("kscb", [128, 128], F32))
        srow = e_(nc.sbuf_tensor("srow", [1, 128], F32))
        ones_s = e_(nc.sbuf_tensor("ones_s", [1, 128], F32))
        obf = e_(nc.sbuf_tensor("obf", [128, SWA_H * SWA_D], BF16))
        ps = [e_(nc.psum_tensor(f"pb{i}", [128, 512], F32)) for i in range(8)]

        p = Prog()
        x_bufs = [p.buf(f"xt{i}") for i in range(2)]
        xln_bufs = [p.buf(f"xln{i}") for i in range(2)]
        xf_buf = p.buf("xf")
        scr_bufs = [p.buf(f"scr{i}") for i in range(2)]
        st_bufs = [p.buf(f"stats{i}") for i in range(2)]
        wq_buf, wk_buf, wv_buf = p.buf("wq"), p.buf("wk"), p.buf("wv")
        qsb_buf = p.buf("qsb")
        kvb_bufs = [p.buf(f"kvb{i}") for i in range(2)]
        sq_buf = p.buf("sq")
        nst_buf = p.buf("nst")
        cs_buf = p.buf("cs")
        rb_buf = p.buf("rb")
        qscb_buf, kscb_buf = p.buf("qscb"), p.buf("kscb")
        srow_buf, ones_buf = p.buf("srow"), p.buf("ones")
        obf_buf = p.buf("obf")
        ps_bufs = [p.buf(f"ps{i}") for i in range(8)]

        p.step("gpsimd", lambda g: g.dma_start(out=wq[:], in_=wq_d.rearrange("c p f -> p c f")), writes=[wq_buf], dma=True, tag="ldwq")
        p.step("gpsimd", lambda g: g.dma_start(out=wk[:], in_=wk_d.rearrange("c p f -> p c f")), writes=[wk_buf], dma=True, tag="ldwk")
        p.step("gpsimd", lambda g: g.dma_start(out=wv[:], in_=wv_d.rearrange("c p f -> p c f")), writes=[wv_buf], dma=True, tag="ldwv")
        p.step("sync", lambda sy: sy.dma_start(out=ones_s[:], in_=ones_d), writes=[ones_buf], dma=True, tag="ldon")
        for nm, d_ap, sb, sbuf_ in (("qs", qs_d, qscb, qscb_buf), ("ks", ks_d, kscb, kscb_buf)):
            p.step("sync", lambda sy, d_ap=d_ap: sy.dma_start(out=srow[:], in_=d_ap), writes=[srow_buf], dma=True, tag=f"ld{nm}")
            p.step("tensor", lambda t: t.matmul(out=ps[6][:, 0:128], lhsT=ones_s[:], rhs=srow[:], start=True, stop=True),
                   reads=[ones_buf, srow_buf], writes=[ps_bufs[6]], tag=f"bc{nm}")
            p.step("vector", lambda v, sb=sb: v.tensor_copy(out=sb[:], in_=ps[6][:, 0:128]),
                   reads=[ps_bufs[6]], writes=[sbuf_], tag=f"cp{nm}")

        for gt in range(NTT):
            bi = gt % 2
            p.step("sync", lambda sy, gt=gt, bi=bi: sy.dma_start(out=xt[:, bi, :], in_=x_d[gt]),
                   writes=[x_bufs[bi]], dma=True, tag=f"ldx{gt}")
            p.step("sync", lambda sy, gt=gt: sy.dma_start(out=cs[:], in_=cs_d[gt]), writes=[cs_buf], dma=True, tag=f"ldcs{gt}")
            emit_ln_tile(p, xt[:, bi, :], x_bufs[bi], xln[:, bi, :], xln_bufs[bi],
                         scr[:, bi, :], scr_bufs[bi], stats[:, bi, :], st_bufs[bi])
            p.step("sync", lambda sy, bi=bi: sy.dma_start(
                out=xf[:], in_=xln[:, bi, :].rearrange("t (c d) -> t c d", c=NDC), transpose=True),
                reads=[xln_bufs[bi]], writes=[xf_buf], dma=True, tag=f"trx{gt}")

            def t_q(t):
                last = None
                for j in range(4):
                    for c in range(NDC):
                        last = t.matmul(out=ps[j][:], lhsT=xf[:, c, :], rhs=wq[:, c, j * 512:(j + 1) * 512],
                                        start=(c == 0), stop=(c == NDC - 1))
                return last
            p.step("tensor", t_q, reads=[xf_buf, wq_buf], writes=[ps_bufs[0], ps_bufs[1], ps_bufs[2], ps_bufs[3]], tag=f"mmq{gt}")

            def t_kv(t):
                last = None
                for c in range(NDC):
                    t.matmul(out=ps[4][:], lhsT=xf[:, c, :], rhs=wk[:, c, :], start=(c == 0), stop=(c == NDC - 1))
                for c in range(NDC):
                    last = t.matmul(out=ps[5][:], lhsT=xf[:, c, :], rhs=wv[:, c, :], start=(c == 0), stop=(c == NDC - 1))
                return last
            p.step("tensor", t_kv, reads=[xf_buf, wk_buf, wv_buf], writes=[ps_bufs[4], ps_bufs[5]], tag=f"mmkv{gt}")

            for j in range(4):
                p.step("vector", lambda v, j=j: v.tensor_copy(out=qsb[:, j * 512:(j + 1) * 512], in_=ps[j][:]),
                       reads=[ps_bufs[j]], writes=[qsb_buf], tag=f"qe{j}_{gt}")
            p.step("vector", lambda v: v.tensor_copy(out=kvb[:, 0, :], in_=ps[4][:]),
                   reads=[ps_bufs[4]], writes=[kvb_bufs[0]], tag=f"ke{gt}")
            p.step("vector", lambda v: v.tensor_copy(out=kvb[:, 1, :], in_=ps[5][:]),
                   reads=[ps_bufs[5]], writes=[kvb_bufs[1]], tag=f"ve{gt}")

            qv = qsb[:].rearrange("t (h z) -> t h z", h=SWA_H)
            _emit_rope(p, qv, qsb_buf, cs, cs_buf, rb, rb_buf, SWA_H, f"qr{gt}", hw=64)
            _emit_qknorm(p, qv, qsb_buf, SWA_H, sq[:], sq_buf, nst, nst_buf, qscb, qscb_buf, f"qn{gt}")
            kv_ = kvb[:, 0, :].rearrange("t (h z) -> t h z", h=SWA_KV)
            _emit_rope(p, kv_, kvb_bufs[0], cs, cs_buf, rb[:, 0:SWA_KV, :], rb_buf, SWA_KV, f"kr{gt}", hw=64)
            _emit_qknorm(p, kv_, kvb_bufs[0], SWA_KV, sq[:, 0:SWA_KV * SWA_D], sq_buf, nst, nst_buf, kscb, kscb_buf, f"kn{gt}")

            p.step("vector", lambda v: v.tensor_copy(out=obf[:], in_=qsb[:]), reads=[qsb_buf], writes=[obf_buf], tag=f"qb{gt}")
            p.step("sync", lambda sy, gt=gt: sy.dma_start(out=q_o[gt], in_=obf[:]), reads=[obf_buf], writes=[], dma=True, tag=f"stq{gt}")
            p.step("vector", lambda v: v.tensor_copy(out=obf[:, 0:512], in_=kvb[:, 0, :]), reads=[kvb_bufs[0]], writes=[obf_buf], tag=f"kb{gt}")
            p.step("sync", lambda sy, gt=gt: sy.dma_start(out=k_o[gt], in_=obf[:, 0:512]), reads=[obf_buf], writes=[], dma=True, tag=f"stk{gt}")
            p.step("vector", lambda v: v.tensor_copy(out=obf[:, 512:1024], in_=kvb[:, 1, :]), reads=[kvb_bufs[1]], writes=[obf_buf], tag=f"vb{gt}")
            p.step("sync", lambda sy, gt=gt: sy.dma_start(out=v_o[gt], in_=obf[:, 512:1024]), reads=[obf_buf], writes=[], dma=True, tag=f"stv{gt}")

        with nc.Block() as block:
            p.emit(nc, block)
            p.finish(nc, block)
    return nc



def build_swa_b():
    nc = bass.Bass("TRN2", target_bir_lowering=False)
    qf_d = nc.dram_tensor("qf", [4, 128, T], BF16, kind="ExternalInput").ap()
    kf_d = nc.dram_tensor("kf", [4, 128, T], BF16, kind="ExternalInput").ap()
    vt_d = nc.dram_tensor("vt", [4, NQT, 128, 128], BF16, kind="ExternalInput").ap()
    ml_d = nc.dram_tensor("maskL", [128, 128], F32, kind="ExternalInput").ap()
    mu_d = nc.dram_tensor("maskU", [128, 128], F32, kind="ExternalInput").ap()
    onesf_d = nc.dram_tensor("onesf", [1, 128], F32, kind="ExternalInput").ap()
    o_d = nc.dram_tensor("o", [4, 128, T], BF16, kind="ExternalOutput").ap()

    SD = 3
    SWA_BIAS = -12.0

    from contextlib import ExitStack
    with ExitStack() as st_:
        e_ = st_.enter_context
        qf = e_(nc.sbuf_tensor("qf_s", [128, 2, T], BF16))
        kf = e_(nc.sbuf_tensor("kf_s", [128, 2, T], BF16))
        vt = e_(nc.sbuf_tensor("vt_s", [128, 2, NQT, 128], BF16))
        mlt = e_(nc.sbuf_tensor("mlt", [128, 128], F32))
        mut = e_(nc.sbuf_tensor("mut", [128, 128], F32))
        onesf = e_(nc.sbuf_tensor("onesf_sw", [1, 128], F32))
        onesc = e_(nc.sbuf_tensor("onesc_sw", [128, 1], F32))
        bcol = e_(nc.sbuf_tensor("bcol", [128, 1], F32))
        cap = e_(nc.sbuf_tensor("cap", [128, SD, 512], F32))
        Pt = e_(nc.sbuf_tensor("Pt_sw", [128, SD, 512], BF16))
        Pacc = e_(nc.sbuf_tensor("Pacc_sw", [128, 2, 512], F32))
        lrow = e_(nc.sbuf_tensor("lrow_sw", [1, 2, 512], F32))
        linvb = e_(nc.sbuf_tensor("linvb_sw", [128, 2, 512], F32))
        obf = e_(nc.sbuf_tensor("obf_sw", [128, 2, 512], BF16))
        ps = [e_(nc.psum_tensor(f"pb{i}", [128, 512], F32)) for i in range(8)]

        p = Prog()
        qf_bufs = [p.buf(f"qf{i}") for i in range(2)]
        kf_bufs = [p.buf(f"kf{i}") for i in range(2)]
        vt_bufs = [p.buf(f"vt{i}") for i in range(2)]
        ml_buf, mu_buf, b_buf = p.buf("ml"), p.buf("mu"), p.buf("bcol")
        onesf_buf, onesc_buf = p.buf("onesf"), p.buf("onesc")
        cap_bufs = [p.buf(f"cap{i}") for i in range(SD)]
        Pt_bufs = [p.buf(f"Pt{i}") for i in range(SD)]
        Pacc_bufs = [p.buf(f"Pa{i}") for i in range(2)]
        lrow_bufs = [p.buf(f"lr{i}") for i in range(2)]
        linv_bufs = [p.buf(f"li{i}") for i in range(2)]
        obf_bufs = [p.buf(f"ob{i}") for i in range(2)]
        ps_bufs = [p.buf(f"ps{i}") for i in range(8)]

        p.step("sync", lambda sy: sy.dma_start(out=mlt[:], in_=ml_d), writes=[ml_buf], dma=True, tag="ldml")
        p.step("sync", lambda sy: sy.dma_start(out=mut[:], in_=mu_d), writes=[mu_buf], dma=True, tag="ldmu")
        p.step("sync", lambda sy: sy.dma_start(out=onesf[:], in_=onesf_d), writes=[onesf_buf], dma=True, tag="ldof")
        p.step("vector", lambda v: v.memset(bcol[:], SWA_BIAS), writes=[b_buf], tag="bset")
        p.step("vector", lambda v: v.memset(onesc[:], 1.0), writes=[onesc_buf], tag="ocset")

        for pr in range(4):
            pb = pr % 2
            p.step("sync", lambda sy, pr=pr, pb=pb: sy.dma_start(out=qf[:, pb, :], in_=qf_d[pr]),
                   writes=[qf_bufs[pb]], dma=True, tag=f"ldq{pr}")
            p.step("sync", lambda sy, pr=pr, pb=pb: sy.dma_start(out=kf[:, pb, :], in_=kf_d[pr]),
                   writes=[kf_bufs[pb]], dma=True, tag=f"ldk{pr}")
            p.step("gpsimd", lambda g, pr=pr, pb=pb: g.dma_start(out=vt[:, pb, :, :], in_=vt_d[pr].rearrange("k p z -> p k z")),
                   writes=[vt_bufs[pb]], dma=True, tag=f"ldv{pr}")

            for qg in range(NQT // 4):
                gb = qg % 2
                bank_o = gb
                bank_l = 4 + gb
                kt0 = max(0, 4 * qg - 8)
                kt1 = min(NQT - 1, 4 * qg + 11)
                kts = list(range(kt0, kt1 + 1))
                q0, q1 = qg * 512, (qg + 1) * 512

                def emit_qk(ki):
                    kt = kts[ki]
                    sb = (2, 3, 6)[ki % SD]
                    p.step("tensor", lambda t, kt=kt, sb=sb, pb=pb, q0=q0, q1=q1: t.matmul(
                        out=ps[sb][:, 0:512], lhsT=kf[:, pb, kt * 128:(kt + 1) * 128],
                        rhs=qf[:, pb, q0:q1], start=True, stop=True),
                        reads=[kf_bufs[pb], qf_bufs[pb]], writes=[ps_bufs[sb]], tag=f"qk{pr}_{qg}_{kt}")
                    return sb

                def emit_soft(ki, sb, qg=qg, gb=gb):
                    kt = kts[ki]
                    sd = ki % SD
                    p.step("scalar", lambda sc, sb=sb, sd=sd: sc.activation(
                        out=cap[:, sd, :], in_=ps[sb][:, 0:512], func=AF.Tanh, scale=SWA_SCALE / SWA_CAP),
                        reads=[ps_bufs[sb]], writes=[cap_bufs[sd]], tag=f"cp{pr}_{qg}_{kt}")
                    for s in range(4):
                        qt = 4 * qg + s
                        if qt - kt == 8:
                            p.step("vector", lambda v, sd=sd, s=s: v.tensor_tensor(
                                out=cap[:, sd, s * 128:(s + 1) * 128], in0=cap[:, sd, s * 128:(s + 1) * 128],
                                in1=mut[:], op=ALU.add),
                                reads=[mu_buf, cap_bufs[sd]], writes=[cap_bufs[sd]], tag=f"mkl{pr}_{qg}_{kt}")
                        if kt - qt == 8:
                            p.step("vector", lambda v, sd=sd, s=s: v.tensor_tensor(
                                out=cap[:, sd, s * 128:(s + 1) * 128], in0=cap[:, sd, s * 128:(s + 1) * 128],
                                in1=mlt[:], op=ALU.add),
                                reads=[ml_buf, cap_bufs[sd]], writes=[cap_bufs[sd]], tag=f"mku{pr}_{qg}_{kt}")
                    p.step("scalar", lambda sc, sd=sd: sc.activation(
                        out=Pt[:, sd, :], in_=cap[:, sd, :], func=AF.Exp, bias=bcol[:, 0:1], scale=SWA_CAP),
                        reads=[cap_bufs[sd], b_buf], writes=[Pt_bufs[sd]], tag=f"ex{pr}_{qg}_{kt}")
                    smin = kt - 4 * qg - 8
                    smax = kt - 4 * qg + 8
                    if smin > 0:
                        p.step("vector", lambda v, sd=sd, smin=smin: v.memset(Pt[:, sd, 0:smin * 128], 0.0),
                               reads=[Pt_bufs[sd]], writes=[Pt_bufs[sd]], tag=f"pz{pr}_{qg}_{kt}")
                    if smax < 3:
                        p.step("vector", lambda v, sd=sd, smax=smax: v.memset(Pt[:, sd, (smax + 1) * 128:512], 0.0),
                               reads=[Pt_bufs[sd]], writes=[Pt_bufs[sd]], tag=f"pz2{pr}_{qg}_{kt}")
                    if ki == 0:
                        p.step("vector", lambda v, sd=sd, gb=gb: v.tensor_copy(out=Pacc[:, gb, :], in_=Pt[:, sd, :]),
                               reads=[Pt_bufs[sd]], writes=[Pacc_bufs[gb]], tag=f"pa{pr}_{qg}_{kt}")
                    else:
                        p.step("vector", lambda v, sd=sd, gb=gb: v.tensor_tensor(
                            out=Pacc[:, gb, :], in0=Pacc[:, gb, :], in1=Pt[:, sd, :], op=ALU.add),
                            reads=[Pt_bufs[sd], Pacc_bufs[gb]], writes=[Pacc_bufs[gb]], tag=f"pa{pr}_{qg}_{kt}")

                def emit_pv(ki):
                    kt = kts[ki]
                    sd = ki % SD
                    p.step("tensor", lambda t, kt=kt, sd=sd, pb=pb, bank_o=bank_o, last=(ki == len(kts) - 1), first=(ki == 0): t.matmul(
                        out=ps[bank_o][:, 0:512], lhsT=vt[:, pb, kt, :], rhs=Pt[:, sd, :],
                        start=first, stop=last),
                        reads=[vt_bufs[pb], Pt_bufs[sd]], writes=[ps_bufs[bank_o]], tag=f"pv{pr}_{qg}_{kt}")

                sbs = {}
                for ki in range(min(2, len(kts))):
                    sbs[ki] = emit_qk(ki)
                for ki in range(len(kts)):
                    emit_soft(ki, sbs[ki])
                    if ki + 2 < len(kts):
                        sbs[ki + 2] = emit_qk(ki + 2)
                    emit_pv(ki)
                p.step("tensor", lambda t, gb=gb, bank_l=bank_l: t.matmul(
                    out=ps[bank_l][0:1, 0:512], lhsT=onesc[:], rhs=Pacc[:, gb, :], start=True, stop=True),
                    reads=[onesc_buf, Pacc_bufs[gb]], writes=[ps_bufs[bank_l]], tag=f"lm{pr}_{qg}")
                p.step("vector", lambda v, gb=gb, bank_l=bank_l: v.reciprocal(out=lrow[:, gb, :], in_=ps[bank_l][0:1, 0:512]),
                       reads=[ps_bufs[bank_l]], writes=[lrow_bufs[gb]], tag=f"lr{pr}_{qg}")
                p.step("tensor", lambda t, gb=gb, bank_l=bank_l: t.matmul(
                    out=ps[bank_l][:, 0:512], lhsT=onesf[:], rhs=lrow[0:1, gb, :], start=True, stop=True),
                    reads=[onesf_buf, lrow_bufs[gb]], writes=[ps_bufs[bank_l]], tag=f"lb{pr}_{qg}")
                p.step("vector", lambda v, gb=gb, bank_l=bank_l: v.tensor_copy(out=linvb[:, gb, :], in_=ps[bank_l][:, 0:512]),
                       reads=[ps_bufs[bank_l]], writes=[linv_bufs[gb]], tag=f"lc{pr}_{qg}")
                p.step("vector", lambda v, gb=gb, bank_o=bank_o: v.tensor_tensor(
                    out=obf[:, gb, :], in0=ps[bank_o][:, 0:512], in1=linvb[:, gb, :], op=ALU.mult),
                    reads=[ps_bufs[bank_o], linv_bufs[gb]], writes=[obf_bufs[gb]], tag=f"fo{pr}_{qg}")
                p.step("sync", lambda sy, pr=pr, gb=gb, q0=q0, q1=q1: sy.dma_start(
                    out=o_d[pr][:, q0:q1], in_=obf[:, gb, :]),
                    reads=[obf_bufs[gb]], writes=[], dma=True, tag=f"sto{pr}_{qg}")

        with nc.Block() as block:
            p.emit(nc, block)
            p.finish(nc, block)
    return nc


def host_swa_weights(wq, wk, wv, wo, query_scale, key_scale):
    cs_all = []
    inv = 1.0 / (10000.0 ** (np.arange(0, SWA_D // 2, dtype=np.float64) / (SWA_D // 2)))
    ang = np.arange(T, dtype=np.float64)[:, None] * inv[None, :]
    emb = np.concatenate([ang, ang], axis=-1)
    cos, sin = np.cos(emb).astype(np.float32), np.sin(emb).astype(np.float32)
    for c in range(NCORES):
        pos = (c % 4) * TLOC + np.arange(TLOC)
        cs = np.stack([cos[pos], sin[pos]], axis=1).reshape(NTT, 128, 2, SWA_D)
        cs_all.append(f32(cs))
    BIG = -3e4
    maskL = np.tril(np.ones((128, 128), np.float32), k=-1) * BIG
    maskU = np.triu(np.ones((128, 128), np.float32), k=1) * BIG
    return dict(wq=f32(np.asarray(wq).reshape(NDC, 128, D)),
                wk=f32(np.asarray(wk).reshape(NDC, 128, 512)),
                wv=f32(np.asarray(wv).reshape(NDC, 128, 512)),
                wo=f32(np.asarray(wo).reshape(NDC, 128, D)),
                qs=f32(query_scale.reshape(1, 128)), ks=f32(key_scale.reshape(1, 128)),
                ones=f32(np.ones((1, 128))), maskL=f32(maskL), maskU=f32(maskU), cs_all=cs_all)


def run_swa(x_full, W, nc_cache={}):
    if "swa_a" not in nc_cache:
        nc_cache["swa_a"] = build_swa_a()
        nc_cache["swa_b"] = build_swa_b()
        nc_cache["mla_c"] = nc_cache.get("mla_c") or build_mla_c()
    xr = np.ascontiguousarray(x_full.reshape(NCORES, NTT, 128, D))
    in_maps = [dict(x=f32(xr[c]), wq=W["wq"], wk=W["wk"], wv=W["wv"], qs=W["qs"], ks=W["ks"],
                    ones=W["ones"], cs=W["cs_all"][c]) for c in range(NCORES)]
    res = run_bass_kernel_spmd(nc_cache["swa_a"], in_maps, core_ids=list(range(NCORES)))
    q = np.stack([r["q"] for r in res.results]).reshape(B, T, SWA_H, SWA_D)
    k = np.stack([r["k"] for r in res.results]).reshape(B, T, SWA_KV, SWA_D)
    v = np.stack([r["v"] for r in res.results]).reshape(B, T, SWA_KV, SWA_D)

    pairs = [(b, h) for b in range(B) for h in range(SWA_H)]
    in_maps_b = []
    for c in range(NCORES):
        myp = pairs[c * 4:(c + 1) * 4]
        qfm = np.stack([np.ascontiguousarray(q[b, :, h, :].T) for (b, h) in myp])
        kfm = np.stack([np.ascontiguousarray(k[b, :, h // 4, :].T) for (b, h) in myp])
        vtm = np.stack([np.ascontiguousarray(v[b, :, h // 4, :].reshape(NQT, 128, 128)) for (b, h) in myp])
        in_maps_b.append(dict(qf=qfm, kf=kfm, vt=vtm, maskL=W["maskL"], maskU=W["maskU"],
                              onesf=W["ones"]))
    res_b = run_bass_kernel_spmd(nc_cache["swa_b"], in_maps_b, core_ids=list(range(NCORES)))
    o_fm = np.zeros((B, SWA_H, SWA_D, T), dtype=BF)
    for c in range(NCORES):
        for i, (b, h) in enumerate(pairs[c * 4:(c + 1) * 4]):
            o_fm[b, h] = res_b.results[c]["o"][i]

    in_maps_c = []
    for c in range(NCORES):
        off = np.ascontiguousarray(o_fm[c // 4].reshape(NDC, 128, T)[:, :, (c % 4) * TLOC:(c % 4 + 1) * TLOC])
        in_maps_c.append(dict(x=f32(xr[c]), of=off, wo=W["wo"]))
    res_c = run_bass_kernel_spmd(nc_cache["mla_c"], in_maps_c, core_ids=list(range(NCORES)))
    return np.stack([r["y"] for r in res_c.results]).reshape(B, T, D)


DIN = 4096
PROJ = 8480
NJP = 17


def build_mamba_a():
    nc = bass.Bass("TRN2", target_bir_lowering=False)
    x_d = nc.dram_tensor("x", [NTT, 128, D], F32, kind="ExternalInput").ap()
    win_d = nc.dram_tensor("win", [NJP, 128, NDC, 512], F32, kind="ExternalInput").ap()
    ns_d = nc.dram_tensor("ns", [1, D], F32, kind="ExternalInput").ap()
    ones_d = nc.dram_tensor("ones", [1, 128], F32, kind="ExternalInput").ap()
    gate_o = nc.dram_tensor("gate", [NTT, 128, DIN], BF16, kind="ExternalOutput").ap()
    ssm_o = nc.dram_tensor("ssm", [NTT, 128, DIN], BF16, kind="ExternalOutput").ap()
    dbc_o = nc.dram_tensor("dbc", [NTT, 128, 288], F32, kind="ExternalOutput").ap()

    from contextlib import ExitStack
    with ExitStack() as st_:
        e_ = st_.enter_context
        xt = e_(nc.sbuf_tensor("xt", [128, 2, D], F32))
        xln = e_(nc.sbuf_tensor("xln", [128, 2, D], F32))
        hrb = e_(nc.sbuf_tensor("hrb", [128, 2, D], BF16))
        xf = e_(nc.sbuf_tensor("xf", [128, NDC, TLOC], BF16))
        scr = e_(nc.sbuf_tensor("scr", [128, 2, D], BF16))
        stats = e_(nc.sbuf_tensor("stats", [128, 2, 8], F32))
        wj = e_(nc.sbuf_tensor("wj", [128, 2, NDC, 512], BF16))
        w32 = e_(nc.sbuf_tensor("w32", [128, NDC, 512], F32))
        nsb = e_(nc.sbuf_tensor("nsb", [128, D], F32))
        nsrow = e_(nc.sbuf_tensor("nsrow", [1, D], F32))
        ones_s = e_(nc.sbuf_tensor("ones_s", [1, 128], F32))
        ob = e_(nc.sbuf_tensor("ob", [128, 2, 512], BF16))
        ob32 = e_(nc.sbuf_tensor("ob32", [128, 2, 512], F32))
        ps = [e_(nc.psum_tensor(f"pb{i}", [128, 512], F32)) for i in range(8)]

        p = Prog()
        x_bufs = [p.buf(f"xt{i}") for i in range(2)]
        xln_bufs = [p.buf(f"xln{i}") for i in range(2)]
        hr_bufs = [p.buf(f"hr{i}") for i in range(2)]
        xf_buf = p.buf("xf")
        scr_bufs = [p.buf(f"scr{i}") for i in range(2)]
        st_bufs = [p.buf(f"st{i}") for i in range(2)]
        wj_bufs = [p.buf(f"wj{i}") for i in range(2)]
        w32_buf = p.buf("w32")
        nsb_buf, nsrow_buf, ones_buf = p.buf("nsb"), p.buf("nsrow"), p.buf("ones")
        ob_bufs = [p.buf(f"ob{i}") for i in range(2)]
        ob32_bufs = [p.buf(f"ob32{i}") for i in range(2)]
        ps_bufs = [p.buf(f"ps{i}") for i in range(8)]

        p.step("sync", lambda sy: sy.dma_start(out=ones_s[:], in_=ones_d), writes=[ones_buf], dma=True, tag="ldon")
        p.step("sync", lambda sy: sy.dma_start(out=nsrow[:], in_=ns_d), writes=[nsrow_buf], dma=True, tag="ldns")
        emit_bcast_row(p, ones_s[:], ones_buf, nsrow, nsrow_buf, nsb, nsb_buf, ps, ps_bufs, [0, 1, 2, 3], D)

        for gt in range(NTT):
            bi = gt % 2
            p.step("sync", lambda sy, gt=gt, bi=bi: sy.dma_start(out=xt[:, bi, :], in_=x_d[gt]),
                   writes=[x_bufs[bi]], dma=True, tag=f"ldx{gt}")
            emit_ln_tile(p, xt[:, bi, :], x_bufs[bi], xln[:, bi, :], xln_bufs[bi],
                         scr[:, bi, :], scr_bufs[bi], stats[:, bi, :], st_bufs[bi])
            p.step("scalar", lambda s, bi=bi: s.activation(out=scr[:, bi, :], in_=xln[:, bi, :], func=AF.Square, accum_out=stats[:, bi, 4:5]),
                   reads=[xln_bufs[bi]], writes=[scr_bufs[bi], st_bufs[bi]], tag=f"msq{gt}")
            p.step("vector", lambda v, bi=bi: v.tensor_scalar(out=stats[:, bi, 4:5], in0=stats[:, bi, 4:5], scalar1=1.0 / D, scalar2=RMS_EPS, op0=ALU.mult, op1=ALU.add),
                   reads=[st_bufs[bi]], writes=[st_bufs[bi]], tag=f"mv{gt}")
            p.step("scalar", lambda s, bi=bi: s.activation(out=stats[:, bi, 5:6], in_=stats[:, bi, 4:5], func=AF.Sqrt),
                   reads=[st_bufs[bi]], writes=[st_bufs[bi]], tag=f"msr{gt}")
            p.step("vector", lambda v, bi=bi: v.reciprocal(out=stats[:, bi, 5:6], in_=stats[:, bi, 5:6]),
                   reads=[st_bufs[bi]], writes=[st_bufs[bi]], tag=f"mrc{gt}")
            p.step("vector", lambda v, bi=bi: v.tensor_scalar_mul(out=xln[:, bi, :], in0=xln[:, bi, :], scalar1=stats[:, bi, 5:6]),
                   reads=[st_bufs[bi]], writes=[xln_bufs[bi]], tag=f"mh{gt}")
            p.step("vector", lambda v, bi=bi: v.tensor_tensor(out=hrb[:, bi, :], in0=xln[:, bi, :], in1=nsb[:], op=ALU.mult),
                   reads=[xln_bufs[bi], nsb_buf], writes=[hr_bufs[bi]], tag=f"mns{gt}")
            p.step("sync", lambda sy, gt=gt, bi=bi: sy.dma_start(
                out=xf[:, :, gt * 128:(gt + 1) * 128], in_=hrb[:, bi, :].rearrange("t (c d) -> t c d", c=NDC), transpose=True),
                reads=[hr_bufs[bi]], writes=[xf_buf], dma=True, tag=f"trx{gt}")

        for j in range(NJP):
            bj = j % 2
            p.step("gpsimd", lambda g, j=j, bj=bj: g.dma_start(out=w32[:], in_=win_d[j]),
                   reads=[], writes=[w32_buf], dma=True, tag=f"ldw{j}")
            p.step("vector", lambda v, bj=bj: v.tensor_copy(out=wj[:, bj, :, :], in_=w32[:]),
                   reads=[w32_buf], writes=[wj_bufs[bj]], tag=f"cw{j}")
            for gt in range(NTT):
                bank = (gt % 4) + (4 if j % 2 else 0)
                ri = gt % 2

                def t_pj(t, j=j, bj=bj, gt=gt, bank=bank):
                    last = None
                    for c in range(NDC):
                        last = t.matmul(out=ps[bank][:], lhsT=xf[:, c, gt * 128:(gt + 1) * 128],
                                        rhs=wj[:, bj, c, :], start=(c == 0), stop=(c == NDC - 1))
                    return last
                p.step("tensor", t_pj, reads=[xf_buf, wj_bufs[bj]], writes=[ps_bufs[bank]], tag=f"mm{j}_{gt}")
                if j < 8:
                    p.step("vector", lambda v, bank=bank, ri=ri: v.tensor_copy(out=ob[:, ri, :], in_=ps[bank][:]),
                           reads=[ps_bufs[bank]], writes=[ob_bufs[ri]], tag=f"ev{j}_{gt}")
                    p.step("sync", lambda sy, j=j, gt=gt, ri=ri: sy.dma_start(
                        out=gate_o[gt][:, j * 512:(j + 1) * 512], in_=ob[:, ri, :]),
                        reads=[ob_bufs[ri]], writes=[], dma=True, tag=f"sg{j}_{gt}")
                elif j < 16:
                    p.step("vector", lambda v, bank=bank, ri=ri: v.tensor_copy(out=ob[:, ri, :], in_=ps[bank][:]),
                           reads=[ps_bufs[bank]], writes=[ob_bufs[ri]], tag=f"ev{j}_{gt}")
                    p.step("sync", lambda sy, j=j, gt=gt, ri=ri: sy.dma_start(
                        out=ssm_o[gt][:, (j - 8) * 512:(j - 7) * 512], in_=ob[:, ri, :]),
                        reads=[ob_bufs[ri]], writes=[], dma=True, tag=f"ss{j}_{gt}")
                else:
                    p.step("vector", lambda v, bank=bank, ri=ri: v.tensor_copy(out=ob32[:, ri, 0:288], in_=ps[bank][:, 0:288]),
                           reads=[ps_bufs[bank]], writes=[ob32_bufs[ri]], tag=f"ev{j}_{gt}")
                    p.step("sync", lambda sy, gt=gt, ri=ri: sy.dma_start(
                        out=dbc_o[gt], in_=ob32[:, ri, 0:288]),
                        reads=[ob32_bufs[ri]], writes=[], dma=True, tag=f"sd{gt}")

        with nc.Block() as block:
            p.emit(nc, block)
            p.finish(nc, block)
    return nc



NCH = T // 128


def build_mamba_b():
    nc = bass.Bass("TRN2", target_bir_lowering=False)
    xh_d = nc.dram_tensor("xh", [8, 128, T], BF16, kind="ExternalInput").ap()
    cw_d = nc.dram_tensor("cw", [8, 128, 4], F32, kind="ExternalInput").ap()
    cb_d = nc.dram_tensor("cb", [8, 128, 1], F32, kind="ExternalInput").ap()
    dtr_d = nc.dram_tensor("dtr", [8, 128, NCH], F32, kind="ExternalInput").ap()
    alog_d = nc.dram_tensor("alog", [1, 8], F32, kind="ExternalInput").ap()
    dtb_d = nc.dram_tensor("dtb", [1, 8], F32, kind="ExternalInput").ap()
    dd_d = nc.dram_tensor("dd", [8, 1, 128], F32, kind="ExternalInput").ap()
    btm_d = nc.dram_tensor("btm", [2, NCH, 128, 128], BF16, kind="ExternalInput").ap()
    ctm_d = nc.dram_tensor("ctm", [2, NCH, 128, 128], BF16, kind="ExternalInput").ap()
    tri_d = nc.dram_tensor("tri", [128, 128], F32, kind="ExternalInput").ap()
    mask_d = nc.dram_tensor("maskc", [128, 128], F32, kind="ExternalInput").ap()
    sel_d = nc.dram_tensor("sel127", [128, 1], F32, kind="ExternalInput").ap()
    ones_d = nc.dram_tensor("ones", [1, 128], F32, kind="ExternalInput").ap()
    identf_d = nc.dram_tensor("identf", [128, 128], F32, kind="ExternalInput").ap()
    y_o = nc.dram_tensor("y", [8, NCH, 128, 128], BF16, kind="ExternalOutput").ap()

    from contextlib import ExitStack
    with ExitStack() as st_:
        e_ = st_.enter_context
        xh = e_(nc.sbuf_tensor("xh_s", [128, 2, 4 + T], BF16))
        btm = e_(nc.sbuf_tensor("btm_s", [128, 2, NCH, 128], BF16))
        ctm = e_(nc.sbuf_tensor("ctm_s", [128, 2, NCH, 128], BF16))
        acc = e_(nc.sbuf_tensor("acc_m", [128, T // 2], F32))
        sf = e_(nc.sbuf_tensor("sf_m", [128, T], BF16))
        sftm = e_(nc.sbuf_tensor("sftm", [128, 2, NCH, 128], BF16))
        cwv = e_(nc.sbuf_tensor("cw_s", [128, 2, 4], F32))
        cbv = e_(nc.sbuf_tensor("cb_s", [128, 2, 1], F32))
        dtr = e_(nc.sbuf_tensor("dtr_s", [128, 2, NCH], F32))
        dtv = e_(nc.sbuf_tensor("dtv", [128, 2, NCH], F32))
        tmpn = e_(nc.sbuf_tensor("tmpn", [128, NCH], F32))
        cal = e_(nc.sbuf_tensor("cal", [128, 2, NCH], F32))
        clb = e_(nc.sbuf_tensor("clb", [128, 2, NCH], F32))
        dte = e_(nc.sbuf_tensor("dte", [128, 2, NCH], F32))
        dfs = e_(nc.sbuf_tensor("dfs", [128, 2, NCH], F32))
        cdk = e_(nc.sbuf_tensor("cdk", [128, 2, NCH], F32))
        clr = e_(nc.sbuf_tensor("clr", [1, NCH], F32))
        ab = e_(nc.sbuf_tensor("ab", [128, 8], F32))
        dtbb = e_(nc.sbuf_tensor("dtbb", [128, 8], F32))
        ddb = e_(nc.sbuf_tensor("ddb", [128, 2, 128], F32))
        ddrow = e_(nc.sbuf_tensor("ddrow", [1, 8, 128], F32))
        arow = e_(nc.sbuf_tensor("arow", [1, 16], F32))
        tri = e_(nc.sbuf_tensor("tri_s", [128, 128], F32))
        maskc = e_(nc.sbuf_tensor("maskc_s", [128, 128], F32))
        sel = e_(nc.sbuf_tensor("sel_s", [128, 1], F32))
        ones_s = e_(nc.sbuf_tensor("ones_s", [1, 128], F32))
        identf = e_(nc.sbuf_tensor("identf_s", [128, 128], F32))
        crow = e_(nc.sbuf_tensor("crow", [1, 128], F32))
        expo = e_(nc.sbuf_tensor("expo", [128, 2, 128], F32))
        Mm = e_(nc.sbuf_tensor("Mm", [128, 2, 128], BF16))
        MT = e_(nc.sbuf_tensor("MT", [128, 2, 128], BF16))
        ctil = e_(nc.sbuf_tensor("ctil", [128, 2, 128], BF16))
        ctf = e_(nc.sbuf_tensor("ctf", [128, 2, 128], BF16))
        crf = e_(nc.sbuf_tensor("crf", [128, 2, 128], BF16))
        bff = e_(nc.sbuf_tensor("bff", [128, 2, 128], BF16))
        btil = e_(nc.sbuf_tensor("btil", [128, 2, 128], BF16))
        xdt = e_(nc.sbuf_tensor("xdt", [128, 2, 128], BF16))
        Ssb = e_(nc.sbuf_tensor("Ssb", [128, 128], F32))
        Sbf = e_(nc.sbuf_tensor("Sbf", [128, 128], BF16))
        dxb = e_(nc.sbuf_tensor("dxb", [128, 2, 128], F32))
        yb = e_(nc.sbuf_tensor("yb", [128, 2, 128], BF16))
        ps = [e_(nc.psum_tensor(f"pb{i}", [128, 512], F32)) for i in range(8)]

        p = Prog()
        names = (["xh0", "xh1", "btm0", "btm1", "ctm0", "ctm1", "acc", "sf", "sftm0", "sftm1",
                  "cw0", "cw1", "cb0", "cb1", "dtr0", "dtr1", "dt0", "dt1", "tmpn",
                  "cal0", "cal1", "clb0", "clb1", "dte0", "dte1", "dfs0", "dfs1", "cdk0", "cdk1",
                  "clr", "ab", "dtbb", "ddb0", "ddb1", "ddrow", "arow", "tri", "maskc", "sel",
                  "ones", "identf", "crow", "expo0", "expo1", "Mm0", "Mm1", "MT0", "MT1",
                  "ctil0", "ctil1", "ctf0", "ctf1", "crf0", "crf1", "bff0", "bff1",
                  "btil0", "btil1", "xdt0", "xdt1", "S", "Sbf", "dxb0", "dxb1", "yb0", "yb1"])
        bufs = {n: p.buf(n) for n in names}
        ps_bufs = [p.buf(f"ps{i}") for i in range(8)]

        for nm, sb, dd in (("tri", tri, tri_d), ("maskc", maskc, mask_d), ("sel", sel, sel_d),
                           ("ones", ones_s, ones_d), ("identf", identf, identf_d)):
            p.step("sync", lambda sy, sb=sb, dd=dd: sy.dma_start(out=sb[:], in_=dd), writes=[bufs[nm]], dma=True, tag=f"ld{nm}")
        p.step("sync", lambda sy: sy.dma_start(out=arow[:, 0:8], in_=alog_d), writes=[bufs["arow"]], dma=True, tag="ldal")
        p.step("sync", lambda sy: sy.dma_start(out=arow[:, 8:16], in_=dtb_d), writes=[bufs["arow"]], dma=True, tag="lddtb")
        p.step("sync", lambda sy: sy.dma_start(out=ddrow[:], in_=dd_d.rearrange("u o z -> o u z")), writes=[bufs["ddrow"]], dma=True, tag="lddd")
        p.step("scalar", lambda s: s.activation(out=arow[:, 0:8], in_=arow[:, 0:8], func=AF.Exp),
               reads=[bufs["arow"]], writes=[bufs["arow"]], tag="aexp")
        p.step("vector", lambda v: v.tensor_scalar_mul(out=arow[:, 0:8], in0=arow[:, 0:8], scalar1=-1.0),
               reads=[bufs["arow"]], writes=[bufs["arow"]], tag="aneg")
        p.step("tensor", lambda t: t.matmul(out=ps[6][:, 0:16], lhsT=ones_s[:], rhs=arow[:, 0:16], start=True, stop=True),
               reads=[bufs["ones"], bufs["arow"]], writes=[ps_bufs[6]], tag="abc")
        p.step("vector", lambda v: v.tensor_copy(out=ab[:], in_=ps[6][:, 0:8]), reads=[ps_bufs[6]], writes=[bufs["ab"]], tag="abcp")
        p.step("vector", lambda v: v.tensor_copy(out=dtbb[:], in_=ps[6][:, 8:16]), reads=[ps_bufs[6]], writes=[bufs["dtbb"]], tag="dtbcp")

        for u in range(8):
            ub = u % 2
            xh_b, btm_b, ctm_b = bufs[f"xh{ub}"], bufs[f"btm{ub}"], bufs[f"ctm{ub}"]
            p.step("vector", lambda v, ub=ub: v.memset(xh[:, ub, 0:4], 0.0), reads=[], writes=[xh_b], tag=f"xz{u}")
            p.step("sync", lambda sy, u=u, ub=ub: sy.dma_start(out=xh[:, ub, 4:], in_=xh_d[u]), writes=[xh_b], dma=True, tag=f"ldxh{u}")
            p.step("gpsimd", lambda g, u=u, ub=ub: g.dma_start(out=btm[:, ub, :, :], in_=btm_d[u // 4].rearrange("k p z -> p k z")),
                   writes=[btm_b], dma=True, tag=f"ldbt{u}")
            p.step("gpsimd", lambda g, u=u, ub=ub: g.dma_start(out=ctm[:, ub, :, :], in_=ctm_d[u // 4].rearrange("k p z -> p k z")),
                   writes=[ctm_b], dma=True, tag=f"ldct{u}")
            p.step("sync", lambda sy, u=u, ub=ub: sy.dma_start(out=cwv[:, ub, :], in_=cw_d[u]), writes=[bufs[f"cw{ub}"]], dma=True, tag=f"ldcw{u}")
            p.step("sync", lambda sy, u=u, ub=ub: sy.dma_start(out=cbv[:, ub, :], in_=cb_d[u]), writes=[bufs[f"cb{ub}"]], dma=True, tag=f"ldcb{u}")
            p.step("sync", lambda sy, u=u, ub=ub: sy.dma_start(out=dtr[:, ub, :], in_=dtr_d[u]), writes=[bufs[f"dtr{ub}"]], dma=True, tag=f"lddtr{u}")
            p.step("tensor", lambda t, u=u: t.matmul(out=ps[6][:, 0:128], lhsT=ones_s[:], rhs=ddrow[:, u, :], start=True, stop=True),
                   reads=[bufs["ones"], bufs["ddrow"]], writes=[ps_bufs[6]], tag=f"dbc{u}")
            p.step("vector", lambda v, ub=ub: v.tensor_copy(out=ddb[:, ub, :], in_=ps[6][:, 0:128]), reads=[ps_bufs[6]], writes=[bufs[f"ddb{ub}"]], tag=f"dbcp{u}")
            p.step("vector", lambda v: v.memset(Ssb[:], 0.0), reads=[], writes=[bufs["S"]], tag=f"sz{u}")

            for hf in range(2):
                o0 = hf * (T // 2)
                p.step("vector", lambda v, ub=ub, o0=o0: v.tensor_scalar_mul(
                    out=acc[:], in0=xh[:, ub, 4 + o0 - 3:4 + o0 - 3 + T // 2], scalar1=cwv[:, ub, 0:1]),
                    reads=[xh_b, bufs[f"cw{ub}"]], writes=[bufs["acc"]], tag=f"cv0_{u}_{hf}")
                for i in range(1, 4):
                    p.step("vector", lambda v, ub=ub, o0=o0, i=i: v.scalar_tensor_tensor(
                        out=acc[:], in0=xh[:, ub, 4 + o0 - 3 + i:4 + o0 - 3 + i + T // 2], scalar=cwv[:, ub, i:i + 1],
                        in1=acc[:], op0=ALU.mult, op1=ALU.add),
                        reads=[xh_b, bufs[f"cw{ub}"], bufs["acc"]], writes=[bufs["acc"]], tag=f"cv{i}_{u}_{hf}")
                p.step("scalar", lambda s, ub=ub, o0=o0: s.activation(out=sf[:, o0:o0 + T // 2], in_=acc[:], func=AF.Silu, bias=cbv[:, ub, 0:1]),
                       reads=[bufs["acc"], bufs[f"cb{ub}"]], writes=[bufs["sf"]], tag=f"cvs{u}_{hf}")
            p.step("sync", lambda sy, ub=ub: sy.dma_start(out=sftm[:, ub, :, :], in_=sf[:].rearrange("p (k t) -> p k t", k=NCH), transpose=True),
                   reads=[bufs["sf"]], writes=[bufs[f"sftm{ub}"]], dma=True, tag=f"cvt{u}")

            p.step("vector", lambda v, ub=ub, u=u: v.tensor_scalar(out=dtv[:, ub, :], in0=dtr[:, ub, :], scalar1=dtbb[:, u:u + 1], scalar2=None, op0=ALU.add),
                   reads=[bufs[f"dtr{ub}"], bufs["dtbb"]], writes=[bufs[f"dt{ub}"]], tag=f"dt0{u}")
            p.step("scalar", lambda s, ub=ub: s.activation(out=tmpn[:], in_=dtv[:, ub, :], func=AF.Abs),
                   reads=[bufs[f"dt{ub}"]], writes=[bufs["tmpn"]], tag=f"dt1{u}")
            p.step("scalar", lambda s: s.activation(out=tmpn[:], in_=tmpn[:], func=AF.Exp, scale=-1.0),
                   reads=[bufs["tmpn"]], writes=[bufs["tmpn"]], tag=f"dt2{u}")
            p.step("vector", lambda v: v.tensor_scalar_add(out=tmpn[:], in0=tmpn[:], scalar1=1.0),
                   reads=[bufs["tmpn"]], writes=[bufs["tmpn"]], tag=f"dt3{u}")
            p.step("scalar", lambda s: s.activation(out=tmpn[:], in_=tmpn[:], func=AF.Ln),
                   reads=[bufs["tmpn"]], writes=[bufs["tmpn"]], tag=f"dt4{u}")
            p.step("vector", lambda v, ub=ub: v.tensor_relu(out=dtv[:, ub, :], in_=dtv[:, ub, :]),
                   reads=[bufs[f"dt{ub}"]], writes=[bufs[f"dt{ub}"]], tag=f"dt5{u}")
            p.step("vector", lambda v, ub=ub: v.tensor_tensor(out=dtv[:, ub, :], in0=dtv[:, ub, :], in1=tmpn[:], op=ALU.add),
                   reads=[bufs[f"dt{ub}"], bufs["tmpn"]], writes=[bufs[f"dt{ub}"]], tag=f"dt6{u}")
            p.step("vector", lambda v, ub=ub, u=u: v.tensor_scalar_mul(out=tmpn[:], in0=dtv[:, ub, :], scalar1=ab[:, u:u + 1]),
                   reads=[bufs[f"dt{ub}"], bufs["ab"]], writes=[bufs["tmpn"]], tag=f"la{u}")
            p.step("tensor", lambda t: t.matmul(out=ps[6][:, 0:NCH], lhsT=tri[:], rhs=tmpn[:], start=True, stop=True),
                   reads=[bufs["tri"], bufs["tmpn"]], writes=[ps_bufs[6]], tag=f"cum{u}")
            p.step("vector", lambda v, ub=ub: v.tensor_copy(out=cal[:, ub, :], in_=ps[6][:, 0:NCH]),
                   reads=[ps_bufs[6]], writes=[bufs[f"cal{ub}"]], tag=f"cc{u}")
            p.step("tensor", lambda t, ub=ub: t.matmul(out=ps[6][0:1, 0:NCH], lhsT=sel[:], rhs=cal[:, ub, :], start=True, stop=True),
                   reads=[bufs["sel"], bufs[f"cal{ub}"]], writes=[ps_bufs[6]], tag=f"cl{u}")
            p.step("vector", lambda v: v.tensor_copy(out=clr[:], in_=ps[6][0:1, 0:NCH]),
                   reads=[ps_bufs[6]], writes=[bufs["clr"]], tag=f"clc{u}")
            p.step("tensor", lambda t: t.matmul(out=ps[6][:, 0:NCH], lhsT=ones_s[:], rhs=clr[:], start=True, stop=True),
                   reads=[bufs["ones"], bufs["clr"]], writes=[ps_bufs[6]], tag=f"clb{u}")
            p.step("vector", lambda v, ub=ub: v.tensor_copy(out=clb[:, ub, :], in_=ps[6][:, 0:NCH]),
                   reads=[ps_bufs[6]], writes=[bufs[f"clb{ub}"]], tag=f"clbc{u}")
            p.step("vector", lambda v, ub=ub: v.tensor_tensor(out=dte[:, ub, :], in0=clb[:, ub, :], in1=cal[:, ub, :], op=ALU.subtract),
                   reads=[bufs[f"clb{ub}"], bufs[f"cal{ub}"]], writes=[bufs[f"dte{ub}"]], tag=f"d0{u}")
            p.step("scalar", lambda s, ub=ub: s.activation(out=dte[:, ub, :], in_=dte[:, ub, :], func=AF.Exp),
                   reads=[bufs[f"dte{ub}"]], writes=[bufs[f"dte{ub}"]], tag=f"d1{u}")
            p.step("vector", lambda v, ub=ub: v.tensor_tensor(out=dte[:, ub, :], in0=dte[:, ub, :], in1=dtv[:, ub, :], op=ALU.mult),
                   reads=[bufs[f"dte{ub}"], bufs[f"dt{ub}"]], writes=[bufs[f"dte{ub}"]], tag=f"d2{u}")
            p.step("scalar", lambda s, ub=ub: s.activation(out=dfs[:, ub, :], in_=cal[:, ub, :], func=AF.Exp),
                   reads=[bufs[f"cal{ub}"]], writes=[bufs[f"dfs{ub}"]], tag=f"d3{u}")
            p.step("scalar", lambda s, ub=ub: s.activation(out=cdk[:, ub, :], in_=clb[:, ub, :], func=AF.Exp),
                   reads=[bufs[f"clb{ub}"]], writes=[bufs[f"cdk{ub}"]], tag=f"d4{u}")

            for k in range(NCH):
                kb = k % 2
                p.step("tensor", lambda t, ub=ub, k=k: t.transpose(out=ps[5][0:1, 0:128], in_=cal[:, ub, k:k + 1], identity=identf[:]),
                       reads=[bufs[f"cal{ub}"], bufs["identf"]], writes=[ps_bufs[5]], tag=f"ct{u}_{k}")
                p.step("vector", lambda v: v.tensor_copy(out=crow[:], in_=ps[5][0:1, 0:128]),
                       reads=[ps_bufs[5]], writes=[bufs["crow"]], tag=f"cr{u}_{k}")
                p.step("tensor", lambda t: t.matmul(out=ps[5][:, 0:128], lhsT=ones_s[:], rhs=crow[:], start=True, stop=True),
                       reads=[bufs["ones"], bufs["crow"]], writes=[ps_bufs[5]], tag=f"cb2{u}_{k}")
                p.step("vector", lambda v, ub=ub, k=k, kb=kb: v.tensor_scalar(out=expo[:, kb, :], in0=ps[5][:, 0:128], scalar1=cal[:, ub, k:k + 1], scalar2=-1.0, op0=ALU.subtract, op1=ALU.mult),
                       reads=[ps_bufs[5], bufs[f"cal{ub}"]], writes=[bufs[f"expo{kb}"]], tag=f"ex0{u}_{k}")
                p.step("vector", lambda v, kb=kb: v.tensor_tensor(out=expo[:, kb, :], in0=expo[:, kb, :], in1=maskc[:], op=ALU.add),
                       reads=[bufs[f"expo{kb}"], bufs["maskc"]], writes=[bufs[f"expo{kb}"]], tag=f"ex1{u}_{k}")
                p.step("scalar", lambda s, kb=kb: s.activation(out=expo[:, kb, :], in_=expo[:, kb, :], func=AF.Exp),
                       reads=[bufs[f"expo{kb}"]], writes=[bufs[f"expo{kb}"]], tag=f"ex2{u}_{k}")
                p.step("sync", lambda sy, ub=ub, k=k, kb=kb: sy.dma_start(out=crf[:, kb, :], in_=ctm[:, ub, k, :], transpose=True),
                       reads=[ctm_b], writes=[bufs[f"crf{kb}"]], dma=True, tag=f"ctf{u}_{k}")
                p.step("sync", lambda sy, ub=ub, k=k, kb=kb: sy.dma_start(out=bff[:, kb, :], in_=btm[:, ub, k, :], transpose=True),
                       reads=[btm_b], writes=[bufs[f"bff{kb}"]], dma=True, tag=f"btf{u}_{k}")
                p.step("tensor", lambda t, kb=kb: t.matmul(out=ps[6][:, 0:128], lhsT=crf[:, kb, :], rhs=bff[:, kb, :], start=True, stop=True),
                       reads=[bufs[f"crf{kb}"], bufs[f"bff{kb}"]], writes=[ps_bufs[6]], tag=f"cb{u}_{k}")
                p.step("vector", lambda v, kb=kb: v.tensor_tensor(out=Mm[:, kb, :], in0=expo[:, kb, :], in1=ps[6][:, 0:128], op=ALU.mult),
                       reads=[bufs[f"expo{kb}"], ps_bufs[6]], writes=[bufs[f"Mm{kb}"]], tag=f"m{u}_{k}")
                p.step("sync", lambda sy, kb=kb: sy.dma_start(out=MT[:, kb, :], in_=Mm[:, kb, :], transpose=True),
                       reads=[bufs[f"Mm{kb}"]], writes=[bufs[f"MT{kb}"]], dma=True, tag=f"mt{u}_{k}")
                p.step("vector", lambda v, ub=ub, k=k, kb=kb: v.tensor_scalar_mul(out=ctil[:, kb, :], in0=ctm[:, ub, k, :], scalar1=dfs[:, ub, k:k + 1]),
                       reads=[ctm_b, bufs[f"dfs{ub}"]], writes=[bufs[f"ctil{kb}"]], tag=f"cti{u}_{k}")
                p.step("sync", lambda sy, kb=kb: sy.dma_start(out=ctf[:, kb, :], in_=ctil[:, kb, :], transpose=True),
                       reads=[bufs[f"ctil{kb}"]], writes=[bufs[f"ctf{kb}"]], dma=True, tag=f"ctf2{u}_{k}")
                p.step("vector", lambda v, ub=ub, k=k, kb=kb: v.tensor_scalar_mul(out=xdt[:, kb, :], in0=sftm[:, ub, k, :], scalar1=dtv[:, ub, k:k + 1]),
                       reads=[bufs[f"sftm{ub}"], bufs[f"dt{ub}"]], writes=[bufs[f"xdt{kb}"]], tag=f"xd{u}_{k}")
                p.step("vector", lambda v: v.tensor_copy(out=Sbf[:], in_=Ssb[:]),
                       reads=[bufs["S"]], writes=[bufs["Sbf"]], tag=f"sb{u}_{k}")
                bank_y = kb

                def t_y(t, kb=kb, bank_y=bank_y):
                    t.matmul(out=ps[bank_y][:, 0:128], lhsT=MT[:, kb, :], rhs=xdt[:, kb, :], start=True, stop=False)
                    return t.matmul(out=ps[bank_y][:, 0:128], lhsT=ctf[:, kb, :], rhs=Sbf[:], start=False, stop=True)
                p.step("tensor", t_y, reads=[bufs[f"MT{kb}"], bufs[f"xdt{kb}"], bufs[f"ctf{kb}"], bufs["Sbf"]], writes=[ps_bufs[bank_y]], tag=f"y{u}_{k}")
                p.step("vector", lambda v, ub=ub, k=k, kb=kb: v.tensor_scalar_mul(out=btil[:, kb, :], in0=btm[:, ub, k, :], scalar1=dte[:, ub, k:k + 1]),
                       reads=[btm_b, bufs[f"dte{ub}"]], writes=[bufs[f"btil{kb}"]], tag=f"bt{u}_{k}")
                p.step("tensor", lambda t, kb=kb: t.matmul(out=ps[7][:, 0:128], lhsT=btil[:, kb, :], rhs=xdt[:, kb, :], start=True, stop=True),
                       reads=[bufs[f"btil{kb}"], bufs[f"xdt{kb}"]], writes=[ps_bufs[7]], tag=f"ds{u}_{k}")
                p.step("vector", lambda v, ub=ub, k=k: v.scalar_tensor_tensor(out=Ssb[:], in0=Ssb[:], scalar=cdk[:, ub, k:k + 1], in1=ps[7][:, 0:128], op0=ALU.mult, op1=ALU.add),
                       reads=[ps_bufs[7], bufs[f"cdk{ub}"], bufs["S"]], writes=[bufs["S"]], tag=f"su{u}_{k}")
                p.step("vector", lambda v, ub=ub, k=k, kb=kb: v.tensor_tensor(out=dxb[:, kb, :], in0=sftm[:, ub, k, :], in1=ddb[:, ub, :], op=ALU.mult),
                       reads=[bufs[f"sftm{ub}"], bufs[f"ddb{ub}"]], writes=[bufs[f"dxb{kb}"]], tag=f"dx{u}_{k}")
                p.step("vector", lambda v, kb=kb, bank_y=bank_y: v.tensor_tensor(out=yb[:, kb, :], in0=ps[bank_y][:, 0:128], in1=dxb[:, kb, :], op=ALU.add),
                       reads=[ps_bufs[bank_y], bufs[f"dxb{kb}"]], writes=[bufs[f"yb{kb}"]], tag=f"ye{u}_{k}")
                p.step("sync", lambda sy, u=u, k=k, kb=kb: sy.dma_start(out=y_o[u, k], in_=yb[:, kb, :]),
                       reads=[bufs[f"yb{kb}"]], writes=[], dma=True, tag=f"sy{u}_{k}")

        with nc.Block() as block:
            p.emit(nc, block)
            p.finish(nc, block)
    return nc



NHC = DIN // 128


def build_mamba_c():
    nc = bass.Bass("TRN2", target_bir_lowering=False)
    x_d = nc.dram_tensor("x", [NTT, 128, D], F32, kind="ExternalInput").ap()
    gate_d = nc.dram_tensor("gate", [NTT, 128, DIN], BF16, kind="ExternalInput").ap()
    yb_d = nc.dram_tensor("yb", [NTT, 128, DIN], BF16, kind="ExternalInput").ap()
    wout_d = nc.dram_tensor("wout", [NHC, 128, D], F32, kind="ExternalInput").ap()
    y_d = nc.dram_tensor("y", [NTT, 128, D], F32, kind="ExternalOutput").ap()

    TTB = 4
    NTB = NTT // TTB

    from contextlib import ExitStack
    with ExitStack() as st_:
        e_ = st_.enter_context
        xt = e_(nc.sbuf_tensor("xt", [128, 2, D], F32))
        xln = e_(nc.sbuf_tensor("xln", [128, 4, D], F32))
        scr = e_(nc.sbuf_tensor("scr", [128, 2, D], BF16))
        stats = e_(nc.sbuf_tensor("stats", [128, 2, 8], F32))
        gb = e_(nc.sbuf_tensor("gb", [128, DIN], BF16))
        ybt = e_(nc.sbuf_tensor("ybt", [128, DIN], BF16))
        sg = e_(nc.sbuf_tensor("sg", [128, DIN], BF16))
        zb = e_(nc.sbuf_tensor("zb", [128, DIN], BF16))
        zf = e_(nc.sbuf_tensor("zf", [128, NHC, TTB * 128], BF16))
        wo = e_(nc.sbuf_tensor("wo_s", [128, 2, NHC, 512], BF16))
        xres = e_(nc.sbuf_tensor("xres", [128, 2, 512], F32))
        ytile = e_(nc.sbuf_tensor("ytile", [128, 2, 512], F32))
        ps = [e_(nc.psum_tensor(f"pb{i}", [128, 512], F32)) for i in range(8)]

        p = Prog()
        x_bufs = [p.buf(f"xt{i}") for i in range(2)]
        xln_bufs = [p.buf(f"xln{i}") for i in range(4)]
        scr_bufs = [p.buf(f"scr{i}") for i in range(2)]
        st_bufs = [p.buf(f"st{i}") for i in range(2)]
        gb_buf, yb_buf, sg_buf, zb_buf, zf_buf = p.buf("gb"), p.buf("yb"), p.buf("sg"), p.buf("zb"), p.buf("zf")
        wo_bufs = [p.buf(f"wo{i}") for i in range(2)]
        xres_bufs = [p.buf(f"xr{i}") for i in range(2)]
        yt_bufs = [p.buf(f"yt{i}") for i in range(2)]
        ps_bufs = [p.buf(f"ps{i}") for i in range(8)]

        for tb in range(NTB):
            for tt in range(TTB):
                gt = tb * TTB + tt
                bi = gt % 2
                p.step("sync", lambda sy, gt=gt, bi=bi: sy.dma_start(out=xt[:, bi, :], in_=x_d[gt]),
                       writes=[x_bufs[bi]], dma=True, tag=f"ldx{gt}")
                emit_ln_tile(p, xt[:, bi, :], x_bufs[bi], xln[:, tt, :], xln_bufs[tt],
                             scr[:, bi, :], scr_bufs[bi], stats[:, bi, :], st_bufs[bi])
                p.step("sync", lambda sy, gt=gt: sy.dma_start(out=gb[:], in_=gate_d[gt]),
                       writes=[gb_buf], dma=True, tag=f"ldg{gt}")
                p.step("sync", lambda sy, gt=gt: sy.dma_start(out=ybt[:], in_=yb_d[gt]),
                       writes=[yb_buf], dma=True, tag=f"ldy{gt}")
                p.step("scalar", lambda s: s.activation(out=sg[:], in_=gb[:], func=AF.Silu),
                       reads=[gb_buf], writes=[sg_buf], tag=f"sl{gt}")
                p.step("vector", lambda v: v.tensor_tensor(out=zb[:], in0=ybt[:], in1=sg[:], op=ALU.mult),
                       reads=[yb_buf, sg_buf], writes=[zb_buf], tag=f"z{gt}")
                p.step("sync", lambda sy, tt=tt: sy.dma_start(
                    out=zf[:, :, tt * 128:(tt + 1) * 128], in_=zb[:].rearrange("t (c d) -> t c d", c=NHC), transpose=True),
                    reads=[zb_buf], writes=[zf_buf], dma=True, tag=f"trz{gt}")
            for dc in range(4):
                wbi = dc % 2
                p.step("gpsimd", lambda g, dc=dc, wbi=wbi: g.dma_start(
                    out=wo[:, wbi, :, :], in_=wout_d[:, :, dc * 512:(dc + 1) * 512].rearrange("c p f -> p c f")),
                    writes=[wo_bufs[wbi]], dma=True, tag=f"ldw{tb}_{dc}")
                for tt in range(TTB):
                    gt = tb * TTB + tt
                    bank = (tt * 4 + dc) % 8
                    ri = (tt + dc) % 2
                    p.step("sync", lambda sy, gt=gt, dc=dc, ri=ri: sy.dma_start(
                        out=xres[:, ri, :], in_=x_d[gt][:, dc * 512:(dc + 1) * 512]),
                        writes=[xres_bufs[ri]], dma=True, tag=f"lr{gt}_{dc}")

                    def t_o(t, tt=tt, dc=dc, wbi=wbi, bank=bank):
                        last = None
                        for c in range(NHC):
                            last = t.matmul(out=ps[bank][:], lhsT=zf[:, c, tt * 128:(tt + 1) * 128],
                                            rhs=wo[:, wbi, c, :], start=(c == 0), stop=(c == NHC - 1))
                        return last
                    p.step("tensor", t_o, reads=[zf_buf, wo_bufs[wbi]], writes=[ps_bufs[bank]], tag=f"mm{gt}_{dc}")
                    p.step("vector", lambda v, bank=bank, ri=ri, tt=tt, dc=dc: v.tensor_tensor(
                        out=ytile[:, ri, :], in0=ps[bank][:], in1=xln[:, tt, dc * 512:(dc + 1) * 512], op=ALU.add),
                        reads=[ps_bufs[bank], xln_bufs[tt]], writes=[yt_bufs[ri]], tag=f"e1{gt}_{dc}")
                    p.step("vector", lambda v, ri=ri: v.tensor_tensor(
                        out=ytile[:, ri, :], in0=ytile[:, ri, :], in1=xres[:, ri, :], op=ALU.add),
                        reads=[yt_bufs[ri], xres_bufs[ri]], writes=[yt_bufs[ri]], tag=f"e2{gt}_{dc}")
                    p.step("sync", lambda sy, gt=gt, dc=dc, ri=ri: sy.dma_start(
                        out=y_d[gt][:, dc * 512:(dc + 1) * 512], in_=ytile[:, ri, :]),
                        reads=[yt_bufs[ri]], writes=[], dma=True, tag=f"st{gt}_{dc}")

        with nc.Block() as block:
            p.emit(nc, block)
            p.finish(nc, block)
    return nc


def host_mamba_weights(norm_scale, in_proj_w, conv_weight, conv_bias, A_log, dt_bias, Dv, out_proj_w):
    A = np.asarray(in_proj_w)
    Ap = np.zeros((D, NJP * 512), np.float32)
    Ap[:, :PROJ] = A
    Ap = Ap.reshape(NDC, 128, NJP, 512)
    win = f32(Ap.transpose(2, 1, 0, 3))
    cw_ = np.asarray(conv_weight)
    cb_ = np.asarray(conv_bias)
    wout = f32(np.asarray(out_proj_w).reshape(NHC, 128, D))
    tri = f32(np.triu(np.ones((128, 128), np.float32)))
    maskc = f32(np.triu(np.ones((128, 128), np.float32), k=1) * -3e4)
    ones = f32(np.ones((1, 128)))
    sel127 = np.zeros((128, 1), np.float32)
    sel127[127, 0] = 1.0
    return dict(win=win, ns=f32(np.asarray(norm_scale).reshape(1, D)), cw=cw_, cb=cb_,
                alog=f32(np.asarray(A_log)), dtb=f32(np.asarray(dt_bias)), dd=f32(np.asarray(Dv)),
                wout=wout, tri=tri, maskc=maskc, ones=ones, sel127=f32(sel127), identf=f32(np.eye(128)))


def run_mamba(x_full, W, nc_cache={}):
    if "mam_a" not in nc_cache:
        nc_cache["mam_a"] = build_mamba_a()
        nc_cache["mam_b"] = build_mamba_b()
        nc_cache["mam_c"] = build_mamba_c()
    xr = np.ascontiguousarray(x_full.reshape(NCORES, NTT, 128, D))
    in_maps = [dict(x=f32(xr[c]), win=W["win"], ns=W["ns"], ones=W["ones"]) for c in range(NCORES)]
    res = run_bass_kernel_spmd(nc_cache["mam_a"], in_maps, core_ids=list(range(NCORES)))
    gate = np.stack([r["gate"] for r in res.results]).reshape(B, T, DIN)
    ssm = np.stack([r["ssm"] for r in res.results]).reshape(B, T, DIN)
    dbc = np.stack([r["dbc"] for r in res.results]).reshape(B, T, 288)

    in_maps_b = []
    for c in range(NCORES):
        units = [(bb, c * 4 + hh) for bb in range(2) for hh in range(4)]
        xh = np.stack([np.ascontiguousarray(ssm[bb, :, h * 128:(h + 1) * 128].T) for (bb, h) in units]).astype(BF)
        cwl = np.stack([np.ascontiguousarray(W["cw"][:, h * 128:(h + 1) * 128].T) for (bb, h) in units])
        cbl = np.stack([W["cb"][h * 128:(h + 1) * 128][:, None] for (bb, h) in units])
        dtr = np.stack([np.ascontiguousarray(dbc[bb, :, h].reshape(NCH, 128).T) for (bb, h) in units])
        alog = np.array([[W["alog"][h] for (bb, h) in units]], np.float32)
        dtb = np.array([[W["dtb"][h] for (bb, h) in units]], np.float32)
        dd = np.stack([W["dd"][h * 128:(h + 1) * 128][None, :] for (bb, h) in units])
        btm = np.stack([np.ascontiguousarray(dbc[bb, :, 32:160].reshape(NCH, 128, 128)) for bb in range(2)]).astype(BF)
        ctm = np.stack([np.ascontiguousarray(dbc[bb, :, 160:288].reshape(NCH, 128, 128)) for bb in range(2)]).astype(BF)
        in_maps_b.append(dict(xh=xh, cw=f32(cwl), cb=f32(cbl), dtr=f32(dtr), alog=alog, dtb=dtb,
                              dd=f32(dd), btm=btm, ctm=ctm, tri=W["tri"], maskc=W["maskc"],
                              sel127=W["sel127"], ones=W["ones"], identf=W["identf"]))
    res_b = run_bass_kernel_spmd(nc_cache["mam_b"], in_maps_b, core_ids=list(range(NCORES)))
    yss = np.zeros((B, T, 32, 128), dtype=BF)
    for c in range(NCORES):
        units = [(bb, c * 4 + hh) for bb in range(2) for hh in range(4)]
        for i, (bb, h) in enumerate(units):
            yss[bb, :, h, :] = res_b.results[c]["y"][i].reshape(T, 128)
    yflat = yss.reshape(B, T, DIN)

    in_maps_c = []
    gre = gate.reshape(NCORES, NTT, 128, DIN)
    yre = yflat.reshape(NCORES, NTT, 128, DIN)
    for c in range(NCORES):
        in_maps_c.append(dict(x=f32(xr[c]), gate=np.ascontiguousarray(gre[c]), yb=np.ascontiguousarray(yre[c]), wout=W["wout"]))
    res_c = run_bass_kernel_spmd(nc_cache["mam_c"], in_maps_c, core_ids=list(range(NCORES)))
    return np.stack([r["y"] for r in res_c.results]).reshape(B, T, D)


GM = 64
GR = 128


def build_gla_a():
    nc = bass.Bass("TRN2", target_bir_lowering=False)
    x_d = nc.dram_tensor("x", [NTT, 128, D], F32, kind="ExternalInput").ap()
    xh_d = nc.dram_tensor("xh", [128, D], F32, kind="ExternalInput").ap()
    inp_d = nc.dram_tensor("inp", [NDC, 128, 128], F32, kind="ExternalInput").ap()
    mgw_d = nc.dram_tensor("mgw", [NDC, 128, 128], F32, kind="ExternalInput").ap()
    w0_d = nc.dram_tensor("w0", [NDC, 128, GR], F32, kind="ExternalInput").ap()
    w1_d = nc.dram_tensor("w1", [GR, D], F32, kind="ExternalInput").ap()
    b1_d = nc.dram_tensor("b1", [1, D], F32, kind="ExternalInput").ap()
    mgb_d = nc.dram_tensor("mgb", [1, 128], F32, kind="ExternalInput").ap()
    cw_d = nc.dram_tensor("cwg", [NDC, 128, 4], F32, kind="ExternalInput").ap()
    cb_d = nc.dram_tensor("cbg", [NDC, 128, 1], F32, kind="ExternalInput").ap()
    ones_d = nc.dram_tensor("ones", [1, 128], F32, kind="ExternalInput").ap()
    rkh_o = nc.dram_tensor("rkh", [NTT, 128, 128], BF16, kind="ExternalOutput").ap()
    g_o = nc.dram_tensor("g", [NTT, 128, GM], F32, kind="ExternalOutput").ap()
    u_o = nc.dram_tensor("u", [NTT, 128, D], BF16, kind="ExternalOutput").ap()
    sh_o = nc.dram_tensor("sh", [NTT, 128, D], BF16, kind="ExternalOutput").ap()

    from contextlib import ExitStack
    with ExitStack() as st_:
        e_ = st_.enter_context
        xt = e_(nc.sbuf_tensor("xt", [128, 2, D], F32))
        xln = e_(nc.sbuf_tensor("xln", [128, 2, D], BF16))
        xlf = e_(nc.sbuf_tensor("xlf", [128, NDC, 128 + TLOC], BF16))
        scr = e_(nc.sbuf_tensor("scr", [128, 1, D], BF16))
        stats = e_(nc.sbuf_tensor("stats", [128, 2, 8], F32))
        shf = e_(nc.sbuf_tensor("shf", [128, NDC, TLOC], BF16))
        conv = e_(nc.sbuf_tensor("conv", [128, 128], F32))
        inp = e_(nc.sbuf_tensor("inp_s", [128, NDC, 128], BF16))
        mgw = e_(nc.sbuf_tensor("mgw_s", [128, NDC, 128], BF16))
        w0 = e_(nc.sbuf_tensor("w0_s", [128, NDC, GR], BF16))
        w1 = e_(nc.sbuf_tensor("w1_s", [128, D], BF16))
        b1b = e_(nc.sbuf_tensor("b1b", [128, D], BF16))
        mgbb = e_(nc.sbuf_tensor("mgbb", [128, 128], F32))
        cw = e_(nc.sbuf_tensor("cw_s", [128, NDC, 4], F32))
        cb = e_(nc.sbuf_tensor("cb_s", [128, NDC], F32))
        row = e_(nc.sbuf_tensor("row", [1, D], F32))
        ones_s = e_(nc.sbuf_tensor("ones_s", [1, 128], F32))
        ch0 = e_(nc.sbuf_tensor("ch0", [128, GR], BF16))
        ch0f = e_(nc.sbuf_tensor("ch0f", [128, GR], BF16))
        chs = e_(nc.sbuf_tensor("chs", [128, D], BF16))
        mgt = e_(nc.sbuf_tensor("mgt", [128, 128], F32))
        selt = e_(nc.sbuf_tensor("selt", [128, GM * 3], F32))
        rkt = e_(nc.sbuf_tensor("rkt", [128, 128], F32))
        nstt = e_(nc.sbuf_tensor("nstt", [128, 4], F32))
        obf = e_(nc.sbuf_tensor("obf", [128, D], BF16))
        ps = [e_(nc.psum_tensor(f"pb{i}", [128, 512], F32)) for i in range(8)]

        p = Prog()
        bufs = {n: p.buf(n) for n in ["xt0", "xt1", "xln0", "xln1", "xlf", "scr0", "scr1", "st0", "st1",
                                      "shf", "conv", "inp", "mgw", "w0", "w1", "b1b", "mgbb", "cw", "cb",
                                      "row", "ones", "ch0", "ch0f", "chs", "mgt", "selt", "rkt", "nstt", "obf"]}
        ps_bufs = [p.buf(f"ps{i}") for i in range(8)]

        p.step("sync", lambda sy: sy.dma_start(out=ones_s[:], in_=ones_d), writes=[bufs["ones"]], dma=True, tag="ldon")
        p.step("gpsimd", lambda g: g.dma_start(out=inp[:], in_=inp_d.rearrange("c p f -> p c f")), writes=[bufs["inp"]], dma=True, tag="ldin")
        p.step("gpsimd", lambda g: g.dma_start(out=mgw[:], in_=mgw_d.rearrange("c p f -> p c f")), writes=[bufs["mgw"]], dma=True, tag="ldmg")
        p.step("gpsimd", lambda g: g.dma_start(out=w0[:], in_=w0_d.rearrange("c p f -> p c f")), writes=[bufs["w0"]], dma=True, tag="ldw0")
        p.step("gpsimd", lambda g: g.dma_start(out=w1[:], in_=w1_d), writes=[bufs["w1"]], dma=True, tag="ldw1")
        p.step("sync", lambda sy: sy.dma_start(out=cw[:], in_=cw_d.rearrange("c p f -> p c f")), writes=[bufs["cw"]], dma=True, tag="ldcw")
        p.step("sync", lambda sy: sy.dma_start(out=cb[:], in_=cb_d.rearrange("c p o -> p (c o)"), allow_slow_non_contiguous=True), writes=[bufs["cb"]], dma=True, tag="ldcb")
        p.step("sync", lambda sy: sy.dma_start(out=row[:], in_=b1_d), writes=[bufs["row"]], dma=True, tag="ldb1")
        emit_bcast_row(p, ones_s[:], bufs["ones"], row, bufs["row"], b1b, bufs["b1b"], ps, ps_bufs, [0, 1, 2, 3], D)
        p.step("sync", lambda sy: sy.dma_start(out=row[:, 0:128], in_=mgb_d), writes=[bufs["row"]], dma=True, tag="ldmb")
        emit_bcast_row(p, ones_s[:], bufs["ones"], row[:, 0:128].rearrange("o f -> o f"), bufs["row"], mgbb, bufs["mgbb"], ps, ps_bufs, [4], 128)

        for ti in range(NTT + 1):
            bi = ti % 2
            src_ap = xh_d if ti == 0 else x_d[ti - 1]
            p.step("sync", lambda sy, src_ap=src_ap, bi=bi: sy.dma_start(out=xt[:, bi, :], in_=src_ap),
                   writes=[bufs[f"xt{bi}"]], dma=True, tag=f"ldx{ti}")
            emit_ln_tile(p, xt[:, bi, :], bufs[f"xt{bi}"], xln[:, bi, :], bufs[f"xln{bi}"],
                         scr[:, 0, :], bufs["scr0"], stats[:, bi, :], bufs[f"st{bi}"])
            p.step("sync", lambda sy, ti=ti, bi=bi: sy.dma_start(
                out=xlf[:, :, ti * 128:(ti + 1) * 128], in_=xln[:, bi, :].rearrange("t (c d) -> t c d", c=NDC), transpose=True),
                reads=[bufs[f"xln{bi}"]], writes=[bufs["xlf"]], dma=True, tag=f"trx{ti}")
            if ti >= 1:
                gt = ti - 1

                def t_ch0(t, ti=ti):
                    last = None
                    for c in range(NDC):
                        last = t.matmul(out=ps[7][:, 0:GR], lhsT=xlf[:, c, ti * 128:(ti + 1) * 128], rhs=w0[:, c, :], start=(c == 0), stop=(c == NDC - 1))
                    return last
                p.step("tensor", t_ch0, reads=[bufs["xlf"], bufs["w0"]], writes=[ps_bufs[7]], tag=f"mmc0{gt}")
                p.step("vector", lambda v: v.tensor_copy(out=ch0[:], in_=ps[7][:, 0:GR]),
                       reads=[ps_bufs[7]], writes=[bufs["ch0"]], tag=f"ch0{gt}")
                p.step("sync", lambda sy: sy.dma_start(out=ch0f[:], in_=ch0[:], transpose=True),
                       reads=[bufs["ch0"]], writes=[bufs["ch0f"]], dma=True, tag=f"trc{gt}")

                def t_ch1(t):
                    last = None
                    for j in range(4):
                        last = t.matmul(out=ps[j][:], lhsT=ch0f[:], rhs=w1[:, j * 512:(j + 1) * 512], start=True, stop=True)
                    return last
                p.step("tensor", t_ch1, reads=[bufs["ch0f"], bufs["w1"]], writes=[ps_bufs[0], ps_bufs[1], ps_bufs[2], ps_bufs[3]], tag=f"mmch{gt}")
                for j in range(4):
                    p.step("vector", lambda v, j=j: v.tensor_tensor(out=chs[:, j * 512:(j + 1) * 512], in0=ps[j][:], in1=b1b[:, j * 512:(j + 1) * 512], op=ALU.add),
                           reads=[ps_bufs[j], bufs["b1b"]], writes=[bufs["chs"]], tag=f"chb{j}_{gt}")
                p.step("scalar", lambda s: s.activation(out=chs[:], in_=chs[:], func=AF.Sigmoid),
                       reads=[bufs["chs"]], writes=[bufs["chs"]], tag=f"chg{gt}")
                p.step("vector", lambda v, bi=bi: v.tensor_tensor(out=obf[:], in0=chs[:], in1=xln[:, bi, :], op=ALU.mult),
                       reads=[bufs["chs"], bufs[f"xln{bi}"]], writes=[bufs["obf"]], tag=f"umul{gt}")
                p.step("sync", lambda sy, gt=gt: sy.dma_start(out=u_o[gt], in_=obf[:]),
                       reads=[bufs["obf"]], writes=[], dma=True, tag=f"stu{gt}")

        for c in range(NDC):
            for tt in range(NTT):
                base = 128 + tt * 128 - 3
                p.step("vector", lambda v, c=c, base=base: v.tensor_scalar_mul(
                    out=conv[:], in0=xlf[:, c, base:base + 128], scalar1=cw[:, c, 0:1]),
                    reads=[bufs["xlf"], bufs["cw"]], writes=[bufs["conv"]], tag=f"cv0_{c}_{tt}")
                for i in range(1, 4):
                    p.step("vector", lambda v, c=c, i=i, base=base: v.scalar_tensor_tensor(
                        out=conv[:], in0=xlf[:, c, base + i:base + i + 128], scalar=cw[:, c, i:i + 1],
                        in1=conv[:], op0=ALU.mult, op1=ALU.add),
                        reads=[bufs["xlf"], bufs["cw"], bufs["conv"]], writes=[bufs["conv"]], tag=f"cv{i}_{c}_{tt}")
                p.step("scalar", lambda s, c=c, tt=tt: s.activation(out=shf[:, c, tt * 128:(tt + 1) * 128], in_=conv[:], func=AF.Silu, bias=cb[:, c:c + 1]),
                       reads=[bufs["conv"], bufs["cb"]], writes=[bufs["shf"]], tag=f"cvs{c}_{tt}")

        for gt in range(NTT):
            for c in range(NDC):
                p.step("sync", lambda sy, gt=gt, c=c: sy.dma_start(
                    out=obf[:, c * 128:(c + 1) * 128], in_=shf[:, c, gt * 128:(gt + 1) * 128], transpose=True),
                    reads=[bufs["shf"]], writes=[bufs["obf"]], dma=True, tag=f"trs{gt}_{c}")
            p.step("sync", lambda sy, gt=gt: sy.dma_start(out=sh_o[gt], in_=obf[:]),
                   reads=[bufs["obf"]], writes=[], dma=True, tag=f"sts{gt}")

            def t_rk(t, gt=gt):
                last = None
                for c in range(NDC):
                    t.matmul(out=ps[5][:, 0:128], lhsT=shf[:, c, gt * 128:(gt + 1) * 128], rhs=inp[:, c, :], start=(c == 0), stop=(c == NDC - 1))
                for c in range(NDC):
                    last = t.matmul(out=ps[6][:, 0:128], lhsT=shf[:, c, gt * 128:(gt + 1) * 128], rhs=mgw[:, c, :], start=(c == 0), stop=(c == NDC - 1))
                return last
            p.step("tensor", t_rk, reads=[bufs["shf"], bufs["inp"], bufs["mgw"]],
                   writes=[ps_bufs[5], ps_bufs[6]], tag=f"mmrk{gt}")

            p.step("vector", lambda v: v.tensor_tensor(out=mgt[:], in0=ps[6][:, 0:128], in1=mgbb[:], op=ALU.add),
                   reads=[ps_bufs[6], bufs["mgbb"]], writes=[bufs["mgt"]], tag=f"mgb{gt}")
            p.step("scalar", lambda s: s.activation(out=selt[:, 0:GM], in_=mgt[:, 0:GM], func=AF.Abs),
                   reads=[bufs["mgt"]], writes=[bufs["selt"]], tag=f"sp0{gt}")
            p.step("scalar", lambda s: s.activation(out=selt[:, 0:GM], in_=selt[:, 0:GM], func=AF.Exp, scale=-1.0),
                   reads=[bufs["selt"]], writes=[bufs["selt"]], tag=f"sp1{gt}")
            p.step("vector", lambda v: v.tensor_scalar_add(out=selt[:, 0:GM], in0=selt[:, 0:GM], scalar1=1.0),
                   reads=[bufs["selt"]], writes=[bufs["selt"]], tag=f"sp2{gt}")
            p.step("scalar", lambda s: s.activation(out=selt[:, 0:GM], in_=selt[:, 0:GM], func=AF.Ln),
                   reads=[bufs["selt"]], writes=[bufs["selt"]], tag=f"sp3{gt}")
            p.step("vector", lambda v: v.tensor_relu(out=selt[:, GM:2 * GM], in_=mgt[:, 0:GM]),
                   reads=[bufs["mgt"]], writes=[bufs["selt"]], tag=f"sp4{gt}")
            p.step("vector", lambda v: v.tensor_tensor(out=selt[:, 0:GM], in0=selt[:, 0:GM], in1=selt[:, GM:2 * GM], op=ALU.add),
                   reads=[bufs["selt"]], writes=[bufs["selt"]], tag=f"sp5{gt}")
            p.step("scalar", lambda s: s.activation(out=selt[:, GM:2 * GM], in_=mgt[:, GM:128], func=AF.Sigmoid),
                   reads=[bufs["mgt"]], writes=[bufs["selt"]], tag=f"tau{gt}")
            p.step("vector", lambda v: v.tensor_tensor(out=selt[:, 2 * GM:3 * GM], in0=selt[:, 0:GM], in1=selt[:, GM:2 * GM], op=ALU.mult),
                   reads=[bufs["selt"]], writes=[bufs["selt"]], tag=f"g0{gt}")
            p.step("vector", lambda v: v.tensor_scalar_mul(out=selt[:, 2 * GM:3 * GM], in0=selt[:, 2 * GM:3 * GM], scalar1=-1.0),
                   reads=[bufs["selt"]], writes=[bufs["selt"]], tag=f"g1{gt}")
            p.step("sync", lambda sy, gt=gt: sy.dma_start(out=g_o[gt], in_=selt[:, 2 * GM:3 * GM]),
                   reads=[bufs["selt"]], writes=[], dma=True, tag=f"stg{gt}")
            p.step("scalar", lambda s: s.activation(out=rkt[:, 0:GM], in_=selt[:, 0:GM], func=AF.Ln),
                   reads=[bufs["selt"]], writes=[bufs["rkt"]], tag=f"it0{gt}")
            p.step("vector", lambda v: v.tensor_tensor(out=rkt[:, 0:GM], in0=rkt[:, 0:GM], in1=selt[:, GM:2 * GM], op=ALU.mult),
                   reads=[bufs["rkt"], bufs["selt"]], writes=[bufs["rkt"]], tag=f"it1{gt}")
            p.step("scalar", lambda s: s.activation(out=rkt[:, 0:GM], in_=rkt[:, 0:GM], func=AF.Exp),
                   reads=[bufs["rkt"]], writes=[bufs["rkt"]], tag=f"it2{gt}")
            p.step("scalar", lambda s: s.activation(out=rkt[:, GM:128], in_=ps[5][:, GM:128], func=AF.Square, accum_out=nstt[:, 0:1]),
                   reads=[ps_bufs[5]], writes=[bufs["rkt"], bufs["nstt"]], tag=f"kn0{gt}")
            p.step("scalar", lambda s: s.activation(out=nstt[:, 1:2], in_=nstt[:, 0:1], func=AF.Sqrt),
                   reads=[bufs["nstt"]], writes=[bufs["nstt"]], tag=f"kn1{gt}")
            p.step("vector", lambda v: v.tensor_scalar_max(out=nstt[:, 1:2], in0=nstt[:, 1:2], scalar1=1e-12),
                   reads=[bufs["nstt"]], writes=[bufs["nstt"]], tag=f"kn2{gt}")
            p.step("vector", lambda v: v.reciprocal(out=nstt[:, 1:2], in_=nstt[:, 1:2]),
                   reads=[bufs["nstt"]], writes=[bufs["nstt"]], tag=f"kn3{gt}")
            p.step("vector", lambda v: v.tensor_scalar_mul(out=rkt[:, GM:128], in0=ps[5][:, GM:128], scalar1=nstt[:, 1:2]),
                   reads=[ps_bufs[5], bufs["nstt"]], writes=[bufs["rkt"]], tag=f"kh0{gt}")
            p.step("vector", lambda v: v.tensor_tensor(out=obf[:, GM:128], in0=rkt[:, GM:128], in1=rkt[:, 0:GM], op=ALU.mult),
                   reads=[bufs["rkt"]], writes=[bufs["obf"]], tag=f"kh1{gt}")
            p.step("vector", lambda v: v.tensor_copy(out=obf[:, 0:GM], in_=ps[5][:, 0:GM]),
                   reads=[ps_bufs[5]], writes=[bufs["obf"]], tag=f"rc{gt}")
            p.step("sync", lambda sy, gt=gt: sy.dma_start(out=rkh_o[gt], in_=obf[:, 0:128]),
                   reads=[bufs["obf"]], writes=[], dma=True, tag=f"strk{gt}")

        with nc.Block() as block:
            p.emit(nc, block)
            p.finish(nc, block)
    return nc



GSC = 1.0 / 8.0


def build_gla_b():
    nc = bass.Bass("TRN2", target_bir_lowering=False)
    rkh_d = nc.dram_tensor("rkh", [128, 128, 128], BF16, kind="ExternalInput").ap()
    g_d = nc.dram_tensor("g", [128, 128, GM], F32, kind="ExternalInput").ap()
    u_d = nc.dram_tensor("u", [128, 128, 256], BF16, kind="ExternalInput").ap()
    sh_d = nc.dram_tensor("sh", [128, 128, 256], BF16, kind="ExternalInput").ap()
    x_d = nc.dram_tensor("x", [128, 128, 256], F32, kind="ExternalInput").ap()
    rw_d = nc.dram_tensor("rw", [1, 256], F32, kind="ExternalInput").ap()
    tri_d = nc.dram_tensor("tri", [128, 128], F32, kind="ExternalInput").ap()
    sel_d = nc.dram_tensor("sel127", [128, 1], F32, kind="ExternalInput").ap()
    msk_d = nc.dram_tensor("msks", [4, 32, 128], F32, kind="ExternalInput").ap()
    identb_d = nc.dram_tensor("identb", [128, 128], BF16, kind="ExternalInput").ap()
    identf_d = nc.dram_tensor("identf", [128, 128], F32, kind="ExternalInput").ap()
    ones_d = nc.dram_tensor("ones", [1, 128], F32, kind="ExternalInput").ap()
    y_o = nc.dram_tensor("y", [128, 128, 256], F32, kind="ExternalOutput").ap()

    from contextlib import ExitStack
    with ExitStack() as st_:
        e_ = st_.enter_context
        rkh = e_(nc.sbuf_tensor("rkh_s", [128, 2, 128], BF16))
        gts = e_(nc.sbuf_tensor("gts", [128, 2, GM], F32))
        ut = e_(nc.sbuf_tensor("ut", [128, 2, 256], BF16))
        sht = e_(nc.sbuf_tensor("sht", [128, 2, 256], BF16))
        xts = e_(nc.sbuf_tensor("xts", [128, 2, 256], F32))
        rwb = e_(nc.sbuf_tensor("rwb", [128, 256], F32))
        rwrow = e_(nc.sbuf_tensor("rwrow", [1, 256], F32))
        tri = e_(nc.sbuf_tensor("tri_s", [128, 128], F32))
        sel = e_(nc.sbuf_tensor("sel_s", [128, 1], F32))
        msks = e_(nc.sbuf_tensor("msks_s", [128, 128], F32))
        identb = e_(nc.sbuf_tensor("identb_s", [128, 128], BF16))
        identf = e_(nc.sbuf_tensor("identf_s", [128, 128], F32))
        ones_s = e_(nc.sbuf_tensor("ones_s", [1, 128], F32))
        csb = e_(nc.sbuf_tensor("csb", [128, GM], F32))
        clb = e_(nc.sbuf_tensor("clb", [128, GM], F32))
        ekt = e_(nc.sbuf_tensor("ekt", [128, GM], F32))
        kht = e_(nc.sbuf_tensor("kht", [128, GM], BF16))
        cf = e_(nc.sbuf_tensor("cf", [GM, 128], F32))
        rf = e_(nc.sbuf_tensor("rf", [GM, 128], BF16))
        kf = e_(nc.sbuf_tensor("kf_g", [GM, 128], BF16))
        rtf = e_(nc.sbuf_tensor("rtf", [GM, 128], BF16))
        ektf = e_(nc.sbuf_tensor("ektf", [GM, 128], F32))
        ktf = e_(nc.sbuf_tensor("ktf", [GM, 128], BF16))
        Am = e_(nc.sbuf_tensor("Am", [32, 128], BF16))
        AmT = e_(nc.sbuf_tensor("AmT", [128, 128], BF16))
        Ssb = e_(nc.sbuf_tensor("Ssb_g", [GM, 256], F32))
        Sbf = e_(nc.sbuf_tensor("Sbf_g", [GM, 256], BF16))
        yt = e_(nc.sbuf_tensor("yt", [128, 2, 256], F32))
        tmp = e_(nc.sbuf_tensor("tmp", [128, 256], F32))
        ps = [e_(nc.psum_tensor(f"pb{i}", [128, 512], F32)) for i in range(8)]

        p = Prog()
        bufs = {n: p.buf(n) for n in ["rkh0", "rkh1", "g0", "g1", "u0", "u1", "sh0", "sh1", "x0", "x1",
                                      "rwb", "rwrow", "tri", "sel", "msks", "identb", "identf", "ones",
                                      "csb", "clb", "ekt", "kht", "cf", "rf", "kf", "rtf", "ektf", "ktf",
                                      "Am", "AmT", "S", "Sbf", "yt0", "yt1", "tmp"]}
        ps_bufs = [p.buf(f"ps{i}") for i in range(8)]

        for nm, sb, dd in (("tri", tri, tri_d), ("sel", sel, sel_d), ("identb", identb, identb_d),
                           ("identf", identf, identf_d), ("ones", ones_s, ones_d)):
            p.step("sync", lambda sy, sb=sb, dd=dd: sy.dma_start(out=sb[:], in_=dd), writes=[bufs[nm]], dma=True, tag=f"ld{nm}")
        p.step("sync", lambda sy: sy.dma_start(out=msks[:], in_=msk_d.rearrange("i p f -> (i p) f")), writes=[bufs["msks"]], dma=True, tag="ldmsk")
        p.step("sync", lambda sy: sy.dma_start(out=rwrow[:], in_=rw_d), writes=[bufs["rwrow"]], dma=True, tag="ldrw")
        emit_bcast_row(p, ones_s[:], bufs["ones"], rwrow, bufs["rwrow"], rwb, bufs["rwb"], ps, ps_bufs, [0], 256)

        for k in range(128):
            kb = k % 2
            if k % 64 == 0:
                p.step("vector", lambda v: v.memset(Ssb[:], 0.0), reads=[], writes=[bufs["S"]], tag=f"sz{k}")
            p.step("sync", lambda sy, k=k, kb=kb: sy.dma_start(out=rkh[:, kb, :], in_=rkh_d[k]), writes=[bufs[f"rkh{kb}"]], dma=True, tag=f"ldrk{k}")
            p.step("sync", lambda sy, k=k, kb=kb: sy.dma_start(out=gts[:, kb, :], in_=g_d[k]), writes=[bufs[f"g{kb}"]], dma=True, tag=f"ldg{k}")
            p.step("sync", lambda sy, k=k, kb=kb: sy.dma_start(out=ut[:, kb, :], in_=u_d[k]), writes=[bufs[f"u{kb}"]], dma=True, tag=f"ldu{k}")
            p.step("sync", lambda sy, k=k, kb=kb: sy.dma_start(out=sht[:, kb, :], in_=sh_d[k]), writes=[bufs[f"sh{kb}"]], dma=True, tag=f"ldsh{k}")
            p.step("gpsimd", lambda g, k=k, kb=kb: g.dma_start(out=xts[:, kb, :], in_=x_d[k]), writes=[bufs[f"x{kb}"]], dma=True, tag=f"ldx{k}")

            def t_cum(t, kb=kb):
                t.matmul(out=ps[4][:, 0:GM], lhsT=tri[:], rhs=gts[:, kb, :], start=True, stop=True)
                return t.matmul(out=ps[5][0:1, 0:GM], lhsT=sel[:], rhs=ps_dummy_rhs(gts, kb), start=True, stop=True)
            p.step("tensor", lambda t, kb=kb: t.matmul(out=ps[4][:, 0:GM], lhsT=tri[:], rhs=gts[:, kb, :], start=True, stop=True),
                   reads=[bufs["tri"], bufs[f"g{kb}"]], writes=[ps_bufs[4]], tag=f"cum{k}")
            p.step("vector", lambda v: v.tensor_copy(out=csb[:], in_=ps[4][:, 0:GM]),
                   reads=[ps_bufs[4]], writes=[bufs["csb"]], tag=f"cc{k}")
            p.step("tensor", lambda t: t.matmul(out=ps[5][0:1, 0:GM], lhsT=sel[:], rhs=csb[:], start=True, stop=True),
                   reads=[bufs["sel"], bufs["csb"]], writes=[ps_bufs[5]], tag=f"cl{k}")
            p.step("vector", lambda v: v.tensor_copy(out=clb[0:1, :], in_=ps[5][0:1, 0:GM]),
                   reads=[ps_bufs[5]], writes=[bufs["clb"]], tag=f"clc{k}")
            p.step("tensor", lambda t: t.matmul(out=ps[5][:, 0:GM], lhsT=ones_s[:], rhs=clb[0:1, :], start=True, stop=True),
                   reads=[bufs["ones"], bufs["clb"]], writes=[ps_bufs[5]], tag=f"clb{k}")
            p.step("vector", lambda v: v.tensor_tensor(out=ekt[:], in0=ps[5][:, 0:GM], in1=csb[:], op=ALU.subtract),
                   reads=[ps_bufs[5], bufs["csb"]], writes=[bufs["ekt"]], tag=f"ek0{k}")
            p.step("scalar", lambda s: s.activation(out=ekt[:], in_=ekt[:], func=AF.Exp),
                   reads=[bufs["ekt"]], writes=[bufs["ekt"]], tag=f"ek1{k}")
            p.step("vector", lambda v, kb=kb: v.tensor_tensor(out=kht[:], in0=rkh[:, kb, GM:128], in1=ekt[:], op=ALU.mult),
                   reads=[bufs[f"rkh{kb}"], bufs["ekt"]], writes=[bufs["kht"]], tag=f"ek2{k}")
            p.step("tensor", lambda t: t.transpose(out=ps[5][0:GM, 0:128], in_=csb[:], identity=identf[:]),
                   reads=[bufs["csb"], bufs["identf"]], writes=[ps_bufs[5]], tag=f"ctr{k}")
            p.step("vector", lambda v: v.tensor_copy(out=cf[:], in_=ps[5][0:GM, 0:128]),
                   reads=[ps_bufs[5]], writes=[bufs["cf"]], tag=f"cfc{k}")
            p.step("tensor", lambda t, kb=kb: t.transpose(out=ps[6][:].bitcast(BF16)[0:GM, 0:128], in_=rkh[:, kb, 0:GM], identity=identb[:]),
                   reads=[bufs[f"rkh{kb}"], bufs["identb"]], writes=[ps_bufs[6]], tag=f"rtr{k}")
            p.step("vector", lambda v: v.tensor_copy(out=rf[:], in_=ps[6][:].bitcast(BF16)[0:GM, 0:128]),
                   reads=[ps_bufs[6]], writes=[bufs["rf"]], tag=f"rfc{k}")
            p.step("tensor", lambda t, kb=kb: t.transpose(out=ps[6][:].bitcast(BF16)[0:GM, 0:128], in_=rkh[:, kb, GM:128], identity=identb[:]),
                   reads=[bufs[f"rkh{kb}"], bufs["identb"]], writes=[ps_bufs[6]], tag=f"ktr{k}")
            p.step("vector", lambda v: v.tensor_copy(out=kf[:], in_=ps[6][:].bitcast(BF16)[0:GM, 0:128]),
                   reads=[ps_bufs[6]], writes=[bufs["kf"]], tag=f"kfc{k}")
            p.step("scalar", lambda s: s.activation(out=ektf[:], in_=cf[:], func=AF.Exp),
                   reads=[bufs["cf"]], writes=[bufs["ektf"]], tag=f"erf{k}")
            p.step("vector", lambda v: v.tensor_tensor(out=rtf[:], in0=rf[:], in1=ektf[:], op=ALU.mult),
                   reads=[bufs["rf"], bufs["ektf"]], writes=[bufs["rtf"]], tag=f"rt{k}")
            p.step("vector", lambda v: v.tensor_copy(out=Sbf[:], in_=Ssb[:]),
                   reads=[bufs["S"]], writes=[bufs["Sbf"]], tag=f"sb{k}")
            bank_o = kb
            for I in range(4):
                p.step("vector", lambda v, I=I: v.tensor_scalar(out=ektf[:], in0=cf[:], scalar1=cf[:, 32 * I:32 * I + 1], scalar2=-1.0, op0=ALU.subtract, op1=ALU.mult),
                       reads=[bufs["cf"]], writes=[bufs["ektf"]], tag=f"ki0{k}_{I}")
                p.step("vector", lambda v: v.tensor_scalar_min(out=ektf[:], in0=ektf[:], scalar1=75.0),
                       reads=[bufs["ektf"]], writes=[bufs["ektf"]], tag=f"kic{k}_{I}")
                p.step("scalar", lambda s: s.activation(out=ektf[:], in_=ektf[:], func=AF.Exp),
                       reads=[bufs["ektf"]], writes=[bufs["ektf"]], tag=f"ki1{k}_{I}")
                p.step("vector", lambda v: v.tensor_tensor(out=ktf[:], in0=kf[:], in1=ektf[:], op=ALU.mult),
                       reads=[bufs["kf"], bufs["ektf"]], writes=[bufs["ktf"]], tag=f"ki2{k}_{I}")
                p.step("tensor", lambda t, I=I: t.matmul(out=ps[2][0:32, 0:128], lhsT=rtf[:, 32 * I:32 * I + 32], rhs=ktf[:], start=True, stop=True),
                       reads=[bufs["rtf"], bufs["ktf"]], writes=[ps_bufs[2]], tag=f"am{k}_{I}")
                p.step("vector", lambda v, I=I: v.tensor_tensor(out=Am[:], in0=ps[2][0:32, 0:128], in1=msks[32 * I:32 * I + 32, :], op=ALU.mult),
                       reads=[ps_bufs[2], bufs["msks"]], writes=[bufs["Am"]], tag=f"amm{k}_{I}")
                p.step("tensor", lambda t: t.transpose(out=ps[3][:].bitcast(BF16)[:, 0:32], in_=Am[:], identity=identb[0:32, 0:32]),
                       reads=[bufs["Am"], bufs["identb"]], writes=[ps_bufs[3]], tag=f"amt{k}_{I}")
                p.step("vector", lambda v, I=I: v.tensor_copy(out=AmT[:, 32 * I:32 * I + 32], in_=ps[3][:].bitcast(BF16)[:, 0:32]),
                       reads=[ps_bufs[3]], writes=[bufs["AmT"]], tag=f"atc{k}_{I}")
            def t_oi(t, kb=kb, bank_o=bank_o):
                t.matmul(out=ps[bank_o][:, 0:256], lhsT=AmT[:], rhs=ut[:, kb, :], start=True, stop=False)
                return t.matmul(out=ps[bank_o][:, 0:256], lhsT=rtf[:], rhs=Sbf[:], start=False, stop=True)
            p.step("tensor", t_oi, reads=[bufs["AmT"], bufs[f"u{kb}"], bufs["rtf"], bufs["Sbf"]], writes=[ps_bufs[bank_o]], tag=f"oi{k}")
            def t_ds(t, kb=kb):
                return t.matmul(out=ps[6][0:GM, 0:256], lhsT=kht[:], rhs=ut[:, kb, :], start=True, stop=True)
            p.step("tensor", t_ds, reads=[bufs["kht"], bufs[f"u{kb}"]], writes=[ps_bufs[6]], tag=f"ds{k}")
            p.step("scalar", lambda s: s.activation(out=ektf[:, 0:1], in_=cf[:, 127:128], func=AF.Exp),
                   reads=[bufs["cf"]], writes=[bufs["ektf"]], tag=f"dk{k}")
            p.step("vector", lambda v: v.scalar_tensor_tensor(out=Ssb[:], in0=Ssb[:], scalar=ektf[0:GM, 0:1], in1=ps[6][0:GM, 0:256], op0=ALU.mult, op1=ALU.add),
                   reads=[ps_bufs[6], bufs["ektf"], bufs["S"]], writes=[bufs["S"]], tag=f"su{k}")
            p.step("vector", lambda v, kb=kb, bank_o=bank_o: v.scalar_tensor_tensor(
                out=yt[:, kb, :], in0=ps[bank_o][:, 0:256], scalar=GSC, in1=xts[:, kb, :], op0=ALU.mult, op1=ALU.add),
                reads=[ps_bufs[bank_o], bufs[f"x{kb}"]], writes=[bufs[f"yt{kb}"]], tag=f"ye0{k}")
            p.step("vector", lambda v, kb=kb: v.tensor_tensor(out=tmp[:], in0=sht[:, kb, :], in1=rwb[:], op=ALU.mult),
                   reads=[bufs[f"sh{kb}"], bufs["rwb"]], writes=[bufs["tmp"]], tag=f"ye1{k}")
            p.step("vector", lambda v, kb=kb: v.tensor_tensor(out=yt[:, kb, :], in0=yt[:, kb, :], in1=tmp[:], op=ALU.add),
                   reads=[bufs["tmp"], bufs[f"yt{kb}"]], writes=[bufs[f"yt{kb}"]], tag=f"ye2{k}")
            p.step("sync", lambda sy, k=k, kb=kb: sy.dma_start(out=y_o[k], in_=yt[:, kb, :]),
                   reads=[bufs[f"yt{kb}"]], writes=[], dma=True, tag=f"sy{k}")

        with nc.Block() as block:
            p.emit(nc, block)
            p.finish(nc, block)
    return nc

def host_gla_weights(conv_w, conv_b, in_proj_w, ch_gate_w0, ch_gate_w1, ch_gate_b1, mem_gate_w, mem_gate_b, residual_weight):
    inp = f32(np.asarray(in_proj_w).T.reshape(NDC, 128, 128))
    mgw = f32(np.asarray(mem_gate_w).T.reshape(NDC, 128, 128))
    w0 = f32(np.asarray(ch_gate_w0).T.reshape(NDC, 128, GR))
    w1 = f32(np.asarray(ch_gate_w1).T)
    b1 = f32(np.asarray(ch_gate_b1).reshape(1, D))
    mgb = f32(np.asarray(mem_gate_b).reshape(1, 128))
    cw = f32(np.asarray(conv_w).reshape(NDC, 128, 4))
    cb = f32(np.asarray(conv_b).reshape(NDC, 128, 1))
    tri = f32(np.triu(np.ones((128, 128), np.float32)))
    sel127 = np.zeros((128, 1), np.float32)
    sel127[127, 0] = 1.0
    msks = np.zeros((4, 32, 128), np.float32)
    for I in range(4):
        for i in range(32):
            msks[I, i, :32 * I + i + 1] = 1.0
    return dict(inp=inp, mgw=mgw, w0=w0, w1=w1, b1=b1, mgb=mgb, cwg=cw, cbg=cb,
                ones=f32(np.ones((1, 128))), tri=tri, sel127=f32(sel127), msks=f32(msks),
                identb=to_bf16(np.eye(128)), identf=f32(np.eye(128)),
                rw=f32(np.asarray(residual_weight).reshape(1, D)))


def run_gla(x_full, W, nc_cache={}):
    if "gla_a" not in nc_cache:
        nc_cache["gla_a"] = build_gla_a()
        nc_cache["gla_b"] = build_gla_b()
    xr = np.ascontiguousarray(x_full.reshape(NCORES, NTT, 128, D))
    in_maps = []
    for c in range(NCORES):
        if c % 4 == 0:
            xh = np.zeros((128, D), np.float32)
        else:
            xh = xr[c - 1][-1]
        in_maps.append(dict(x=f32(xr[c]), xh=f32(xh), inp=W["inp"], mgw=W["mgw"], w0=W["w0"],
                            w1=W["w1"], b1=W["b1"], mgb=W["mgb"], cwg=W["cwg"], cbg=W["cbg"], ones=W["ones"]))
    res = run_bass_kernel_spmd(nc_cache["gla_a"], in_maps, core_ids=list(range(NCORES)))
    rkh = np.stack([r["rkh"] for r in res.results]).reshape(B * T // 128, 128, 128)
    g = np.stack([r["g"] for r in res.results]).reshape(B * T // 128, 128, GM)
    u = np.stack([r["u"] for r in res.results]).reshape(B * T, D)
    sh = np.stack([r["sh"] for r in res.results]).reshape(B * T, D)

    in_maps_b = []
    xflat = x_full.reshape(B * T, D)
    for c in range(NCORES):
        sl = slice(c * 256, (c + 1) * 256)
        in_maps_b.append(dict(rkh=np.ascontiguousarray(rkh), g=f32(g),
                              u=np.ascontiguousarray(u[:, sl].reshape(128, 128, 256)),
                              sh=np.ascontiguousarray(sh[:, sl].reshape(128, 128, 256)),
                              x=f32(xflat[:, sl].reshape(128, 128, 256)),
                              rw=f32(W["rw"][:, sl]), tri=W["tri"], sel127=W["sel127"], msks=W["msks"],
                              identb=W["identb"], identf=W["identf"], ones=W["ones"]))
    res_b = run_bass_kernel_spmd(nc_cache["gla_b"], in_maps_b, core_ids=list(range(NCORES)))
    y = np.concatenate([r["y"].reshape(B * T, 256) for r in res_b.results], axis=1)
    return y.reshape(B, T, D)


_KCACHE = {}


def kernel(L0_attn_a_hidden_states, L0_attn_a_conv_w, L0_attn_a_conv_b, L0_attn_a_in_proj_w,
           L0_attn_a_ch_gate_w0, L0_attn_a_ch_gate_w1, L0_attn_a_ch_gate_b1, L0_attn_a_mem_gate_w,
           L0_attn_a_mem_gate_b, L0_attn_a_residual_weight, L0_attn_b_norm_scale, L0_attn_b_in_proj_w,
           L0_attn_b_conv_weight, L0_attn_b_conv_bias, L0_attn_b_A_log, L0_attn_b_dt_bias, L0_attn_b_D,
           L0_attn_b_out_proj_w, L0_ffn_wq_gate, L0_ffn_s_gate, L0_ffn_wq_up, L0_ffn_s_up,
           L0_ffn_wq_down, L0_ffn_s_down, L1_attn_a_Wqa, L1_attn_a_g_qa, L1_attn_a_Wqb, L1_attn_a_Wkva,
           L1_attn_a_g_kva, L1_attn_a_Wkvb, L1_attn_a_Wo, L1_attn_b_wq, L1_attn_b_wk, L1_attn_b_wv,
           L1_attn_b_wo, L1_attn_b_query_scale, L1_attn_b_key_scale, L1_ffn_gate_w, L1_ffn_w_gate,
           L1_ffn_w_up, L1_ffn_w_down, L1_ffn_sw_gate, L1_ffn_sw_up, L1_ffn_sw_down):
    W_gla = host_gla_weights(L0_attn_a_conv_w, L0_attn_a_conv_b, L0_attn_a_in_proj_w,
                             L0_attn_a_ch_gate_w0, L0_attn_a_ch_gate_w1, L0_attn_a_ch_gate_b1,
                             L0_attn_a_mem_gate_w, L0_attn_a_mem_gate_b, L0_attn_a_residual_weight)
    W_ffn = host_ffn_weights(L0_ffn_wq_gate, L0_ffn_s_gate, L0_ffn_wq_up, L0_ffn_s_up,
                             L0_ffn_wq_down, L0_ffn_s_down)
    W_mam = host_mamba_weights(L0_attn_b_norm_scale, L0_attn_b_in_proj_w, L0_attn_b_conv_weight,
                               L0_attn_b_conv_bias, L0_attn_b_A_log, L0_attn_b_dt_bias, L0_attn_b_D,
                               L0_attn_b_out_proj_w)
    W_mla = host_mla_weights(L1_attn_a_Wqa, L1_attn_a_g_qa, L1_attn_a_Wqb, L1_attn_a_Wkva,
                             L1_attn_a_g_kva, L1_attn_a_Wkvb, L1_attn_a_Wo)
    W_moe = host_moe_weights(L1_ffn_gate_w, L1_ffn_w_gate, L1_ffn_w_up, L1_ffn_w_down,
                             L1_ffn_sw_gate, L1_ffn_sw_up, L1_ffn_sw_down)
    W_swa = host_swa_weights(L1_attn_b_wq, L1_attn_b_wk, L1_attn_b_wv, L1_attn_b_wo,
                             L1_attn_b_query_scale, L1_attn_b_key_scale)

    x = f32(L0_attn_a_hidden_states)
    x = run_gla(x, W_gla, _KCACHE)
    x = run_ffn_quant(x, W_ffn, _KCACHE)
    x = run_mamba(x, W_mam, _KCACHE)
    x = run_ffn_quant(x, W_ffn, _KCACHE)
    x = run_mla(x, W_mla, _KCACHE)
    x = run_moe(x, W_moe, _KCACHE)
    x = run_swa(x, W_swa, _KCACHE)
    x = run_moe(x, W_moe, _KCACHE)
    return f32(x.reshape(B, T, D))
